# version 01
# baseline (speedup 1.0000x reference)
import numpy as np
import ml_dtypes
from contextlib import ExitStack
import concourse.bass as bass
import concourse.mybir as mybir
from concourse.bass_utils import run_bass_kernel_spmd

F32 = mybir.dt.float32
BF16 = mybir.dt.bfloat16
AF = mybir.ActivationFunctionType
ALU = mybir.AluOpType
AXX = mybir.AxisListType.X
BF = ml_dtypes.bfloat16

D = 1024; L = 4096; SL = 512; NSL = L // SL
CH = 256; NCH = L // CH
DIN = 2048; CONVDIM = 2304; DINPROJ = 4384
NH = 32; HD = 64; DS = 128
NZT = DIN // 128
NBCT = CONVDIM // 128
NMT = 35
EPS_LN = 1e-5
EPS_OUT = 1.1920929e-07
EPS_GATE = 1e-5
SCALE_ATT = 1.0 / 32.0

ENGS = ("sync", "gpsimd", "tensor", "vector", "scalar")


class Prog:

    def __init__(self, nc, ctx):
        self.nc = nc
        self.ctx = ctx
        self.items = {e: [] for e in ENGS}
        self.sems = {}
        self.cnt = {}
        self.asyncs = set()
        self.prod = {}

    def sem(self, name, producer, is_async=False):
        self.sems[name] = self.ctx.enter_context(self.nc.semaphore(name))
        self.cnt[name] = 0
        self.prod[name] = producer
        if is_async:
            self.asyncs.add(name)
        return name

    def op(self, eng, fn, inc=None, inck=1, waits=()):
        w = []
        for item in waits:
            if item is None:
                continue
            s, v = item
            if v is None:
                continue
            if isinstance(v, tuple):
                s, v = v
            assert s in self.cnt, s
            assert v <= self.cnt[s], (s, v, self.cnt[s])
            if v > 0:
                w.append((s, v))
        n = None
        if inc is not None:
            self.cnt[inc] += inck
            n = self.cnt[inc]
        self.items[eng].append((fn, inc, inck, tuple(w)))
        return n

    def emit(self):
        nwait = 0
        with self.nc.Block() as block:
            for engname in ENGS:
                items = self.items[engname]
                if not items:
                    continue

                def mk(items, engname):
                    def body(eng):
                        nonlocal nwait
                        seen = {}
                        own = {}
                        for fn, inc, inck, waits in items:
                            for s, v in waits:
                                if s not in self.asyncs and self.prod[s] == engname:
                                    assert v <= own.get(s, 0), (s, v, own.get(s, 0), engname)
                                    continue
                                if v <= seen.get(s, 0):
                                    continue
                                eng.wait_ge(self.sems[s], v)
                                seen[s] = v
                                nwait += 1
                            ins = fn(eng)
                            if inc is not None:
                                ins.then_inc(self.sems[inc], inck)
                                own[inc] = own.get(inc, 0) + inck
                    return body

                getattr(block, engname)(mk(items, engname))
        return nwait

    def fence(self):
        snap = [(s, self.cnt[s]) for s in self.sems if self.cnt[s] > 0]
        for e in ENGS:
            self.op(e, lambda en: en.nop(), waits=snap)

    def n_ops(self):
        return {e: len(v) for e, v in self.items.items()}


def _consts():
    c = {}
    c["ones1f"] = np.ones((1, 128), np.float32)
    c["ones1b"] = np.ones((1, 128), dtype=BF)
    c["ones128b"] = np.ones((128, 1), dtype=BF)
    c["I32b"] = np.eye(32, dtype=BF)
    c["I128b"] = np.eye(128, dtype=BF)
    c["eyeA"] = np.eye(256, dtype=BF)[0:128].copy()
    c["eyeB"] = np.eye(256, dtype=BF)[128:256].copy()
    c["REPb"] = (np.arange(2048)[None, :] // 64 == np.arange(32)[:, None]).astype(BF)
    c["REPf"] = (np.arange(2048)[None, :] // 64 == np.arange(32)[:, None]).astype(np.float32)
    c["IDX16"] = np.tile(np.arange(16, dtype=np.float32)[:, None], (1, 128)).astype(BF)
    c["leafof"] = ((np.arange(8)[None, :] * 128 + np.arange(128)[:, None]) // 64).astype(np.float32)
    c["ones32f"] = np.ones((32, 256), np.float32)
    c["ones128f"] = np.ones((128, 1), np.float32)
    A = np.zeros((15, 16), np.float32)
    neg = np.zeros((1, 16), np.float32)
    for l in range(16):
        b = [(l >> 3) & 1, (l >> 2) & 1, (l >> 1) & 1, l & 1]
        nodes = [0, 1 + b[0], 3 + 2 * b[0] + b[1], 7 + 4 * b[0] + 2 * b[1] + b[2]]
        for k in range(4):
            A[nodes[k], l] += 1.0 if b[k] else -1.0
            if not b[k]:
                neg[0, l] += 1.0
    c["Abit"] = A.astype(BF)
    c["negrow"] = neg.astype(BF)
    c["ones512b"] = np.ones((1, 512), dtype=BF)
    return c


class Env:
    pass


def build(nphases=5, dbg=False):
    nc = bass.Bass("TRN2", target_bir_lowering=False)
    E = Env()
    E.nc = nc

    E.xt = nc.dram_tensor("xt", [D, L], F32, kind="ExternalInput")
    E.w_in = nc.dram_tensor("w_in", [D, DINPROJ], F32, kind="ExternalInput")
    E.conv_wl = nc.dram_tensor("conv_wl", [128, 72], F32, kind="ExternalInput")
    E.conv_bl = nc.dram_tensor("conv_bl", [128, 18], F32, kind="ExternalInput")
    E.dt_bias = nc.dram_tensor("dt_bias", [32, 1], F32, kind="ExternalInput")
    E.a_log = nc.dram_tensor("a_log", [32, 1], F32, kind="ExternalInput")
    E.d_skip = nc.dram_tensor("d_skip", [32, 1], F32, kind="ExternalInput")
    E.normw = nc.dram_tensor("normw", [128, 16], F32, kind="ExternalInput")
    E.w_out = nc.dram_tensor("w_out", [DIN, D], F32, kind="ExternalInput")
    E.w_fc = nc.dram_tensor("w_fc", [D, 4 * D], F32, kind="ExternalInput")
    E.w_proj = nc.dram_tensor("w_proj", [4 * D, D], F32, kind="ExternalInput")
    E.fff = {}
    for pp in "qkv":
        E.fff[pp] = dict(
            nwT=nc.dram_tensor(f"{pp}_nwT", [D, 15], F32, kind="ExternalInput"),
            nb=nc.dram_tensor(f"{pp}_nb", [15, 1], F32, kind="ExternalInput"),
            w1=nc.dram_tensor(f"{pp}_w1", [D, D], F32, kind="ExternalInput"),
            b1c=nc.dram_tensor(f"{pp}_b1c", [128, 8], F32, kind="ExternalInput"),
            w2=nc.dram_tensor(f"{pp}_w2", [D, D], F32, kind="ExternalInput"),
            b2=nc.dram_tensor(f"{pp}_b2", [16, D], F32, kind="ExternalInput"),
        )
    E.out_t = nc.dram_tensor("o", [D, L], F32, kind="ExternalOutput")

    spk = "ExternalOutput" if dbg else "Internal"
    E.sp_xn = nc.dram_tensor("sp_xn", [D, L], F32, kind=spk)
    E.sp_silz = nc.dram_tensor("sp_silz", [DIN, L], BF16, kind=spk)
    E.sp_xc = nc.dram_tensor("sp_xc", [CONVDIM, L], BF16, kind=spk)
    E.sp_dtraw = nc.dram_tensor("sp_dtraw", [32, L], F32, kind=spk)
    E.sp_h = nc.dram_tensor("sp_h", [D, L], F32, kind=spk)
    E.sp_x1 = nc.dram_tensor("sp_x1", [D, L], F32, kind=spk)
    E.sp_q = nc.dram_tensor("sp_q", [D, L], BF16, kind=spk)
    E.sp_k = nc.dram_tensor("sp_k", [D, L], BF16, kind=spk)
    E.sp_v = nc.dram_tensor("sp_v", [L, D], BF16, kind=spk)
    if dbg:
        E.t_dsm = nc.dram_tensor("t_dsm", [32, 5 * 256], F32, kind="ExternalOutput")
        E.t_u = nc.dram_tensor("t_u", [128, 4096], F32, kind="ExternalOutput")
        E.t_ud = nc.dram_tensor("t_ud", [128, 4096], F32, kind="ExternalOutput")
        E.t_cbt = nc.dram_tensor("t_cbt", [128, 512], F32, kind="ExternalOutput")
        E.t_mexa = nc.dram_tensor("t_mexa", [128, 4096], F32, kind="ExternalOutput")
        E.t_G = nc.dram_tensor("t_G", [128, 2048], F32, kind="ExternalOutput")
        E.t_g = nc.dram_tensor("t_g", [128, 4096], F32, kind="ExternalOutput")
        E.t_prev = nc.dram_tensor("t_prev", [128, 2048], F32, kind="ExternalOutput")
        E.t_msb = nc.dram_tensor("t_msb", [128, 512], F32, kind="ExternalOutput")
        E.t_gf = nc.dram_tensor("t_gf", [128, 4096], F32, kind="ExternalOutput")
        E.t_xn2 = nc.dram_tensor("t_xn2", [128, 8 * 512], F32, kind="ExternalOutput")
        E.t_rt = nc.dram_tensor("t_rt", [128, 480], F32, kind="ExternalOutput")
        E.t_sel = nc.dram_tensor("t_sel", [128, 64], F32, kind="ExternalOutput")
        E.t_selT = nc.dram_tensor("t_selT", [16, 512], F32, kind="ExternalOutput")
        E.t_leaf = nc.dram_tensor("t_leaf", [128, 512], F32, kind="ExternalOutput")
        E.t_hid = nc.dram_tensor("t_hid", [128, 4096], F32, kind="ExternalOutput")
        E.t_fout = nc.dram_tensor("t_fout", [128, 4096], F32, kind="ExternalOutput")
        E.t_A5 = nc.dram_tensor("t_A5", [128, 4096], F32, kind="ExternalOutput")
        E.t_u5 = nc.dram_tensor("t_u5", [128, 4096], F32, kind="ExternalOutput")
        E.t_d5 = nc.dram_tensor("t_d5", [128, 4], F32, kind="ExternalOutput")
        E.t_r5 = nc.dram_tensor("t_r5", [1, 2048], F32, kind="ExternalOutput")
        E.t_pt5 = nc.dram_tensor("t_pt5", [128, 8192], F32, kind="ExternalOutput")
    E.dbg = dbg
    E.dbg_guard = 0

    cst = {k: nc.inline_tensor(v, f"c_{k}") for k, v in _consts().items()}

    ctx = ExitStack()
    p = Prog(nc, ctx)
    E.p = p
    dS = p.sem("dS", "sync", True)
    dG = p.sem("dG", "gpsimd", True)
    dC = p.sem("dC", "scalar", True)
    sT = p.sem("sT", "tensor")
    sV = p.sem("sV", "vector")
    sS = p.sem("sS", "scalar")
    sG = p.sem("sG", "gpsimd")
    E.dS, E.dG, E.dC, E.sT, E.sV, E.sS, E.sG = dS, dG, dC, sT, sV, sS, sG

    sb = lambda n, s, d: ctx.enter_context(nc.sbuf_tensor(n, s, d))
    E.sb = sb

    E.ones1f = sb("ones1f", [1, 128], F32)
    E.ones1b = sb("ones1b", [1, 128], BF16)
    E.ones128b = sb("ones128b", [128, 1], BF16)
    E.I32b = sb("I32b", [32, 32], BF16)
    E.I128b = sb("I128b", [128, 128], BF16)
    E.IDX16 = sb("IDX16", [16, 128], BF16)
    E.leafof = sb("leafof", [128, 8], F32)
    E.Abit = sb("Abit", [15, 16], BF16)
    E.negrow = sb("negrow", [1, 16], BF16)
    E.ones512b = sb("ones512b", [1, 512], BF16)
    E.smalls = sb("smalls", [32, 8], F32)
    E.ones128f = sb("ones128f", [128, 1], F32)
    E.cst = cst
    for t, cc in [(E.ones1f, "ones1f"), (E.ones1b, "ones1b"), (E.ones128b, "ones128b"),
                  (E.I32b, "I32b"), (E.I128b, "I128b"),
                  (E.IDX16, "IDX16"),
                  (E.leafof, "leafof"),
                  (E.Abit, "Abit"), (E.negrow, "negrow"), (E.ones512b, "ones512b"),
                  (E.ones128f, "ones128f")]:
        p.op("sync", lambda e, t=t, cc=cc: e.dma_start(out=t[:, :], in_=cst[cc][:, :]), inc=dS, inck=16)
    p.op("sync", lambda e: e.dma_start(out=E.smalls[:, 0:1], in_=E.dt_bias[:, :]), inc=dS, inck=16)
    p.op("sync", lambda e: e.dma_start(out=E.smalls[:, 3:4], in_=E.a_log[:, :]), inc=dS, inck=16)
    p.op("sync", lambda e: e.dma_start(out=E.smalls[:, 2:3], in_=E.d_skip[:, :]), inc=dS, inck=16)
    E.c_loaded = p.cnt[dS]
    E.epsb = sb("epsb", [128, 4], F32)
    for i, v in enumerate([EPS_LN, EPS_OUT, EPS_GATE, 1.0]):
        p.op("vector", lambda e, i=i, v=v: e.memset(E.epsb[:, i:i+1], v), inc=sV)
    E.epsb_ready = p.cnt[sV]
    n = p.op("scalar", lambda e: e.activation(E.smalls[0:32, 4:5], E.smalls[0:32, 3:4], AF.Exp),
             inc=sS, waits=[(dS, E.c_loaded)])
    p.op("vector", lambda e: e.tensor_scalar(E.smalls[0:32, 1:2], E.smalls[0:32, 4:5], -1.0, None, ALU.mult),
         inc=sV, waits=[(sS, n)])

    _phase1(E, nphases)
    if nphases >= 2:
        _phase2(E, nphases)
    if nphases >= 3:
        _phase3(E, nphases)
    if nphases >= 4:
        _phase4(E, nphases)
    if nphases >= 5:
        _phase5(E)

    nw = p.emit()
    ctx.close()
    return nc, p.n_ops(), nw


def _phase1(E, nphases):
    nc, p = E.nc, E.p
    dS, dG, sT, sV, sS = E.dS, E.dG, E.sT, E.sV, E.sS
    px = ExitStack()
    sbl = lambda n, s, d: px.enter_context(nc.sbuf_tensor(n, s, d))
    psl = lambda n, s: px.enter_context(nc.psum_tensor(n, s, F32))
    w_in_sb = sbl("w_in_sb", [128, 8 * DINPROJ], BF16)
    convw = sbl("convw", [128, 72], F32)
    convb = sbl("convb", [128, 18], F32)
    diag = sbl("diag", [128, 72 * 128], BF16)
    xbuf = sbl("xbuf", [128, 2 * 8 * SL], F32)
    xbf = sbl("xbf", [128, 8 * SL], BF16)
    rows1 = sbl("rows1", [1, 5 * SL], F32)
    mapsb = sbl("mapsb", [128, 3 * SL], F32)
    tch = sbl("tch", [128, SL], F32)
    xnb = sbl("xnb", [128, 2 * SL], F32)
    ain_sb = sbl("ain_sb", [128, 8 * SL], BF16)
    xsq = ain_sb
    xbc_sb = sbl("xbc_sb", [128, NBCT * 516], BF16)
    szb = sbl("szb", [128, 2 * SL], BF16)
    xcb = sbl("xcb", [128, 2 * SL], BF16)
    dtraw_sb = sbl("dtraw_sb", [32, SL], F32)
    E.p1_slice_dG = [0] * NSL
    pzx = psl("pzx", [128, 2048])
    pconv = psl("pconv", [128, 1024])
    pstatA = psl("pstatA", [1, 512])
    pstatB = psl("pstatB", [1, 512])
    if True:
        mu_r = rows1[0:1, 0:SL]
        var_r = rows1[0:1, SL:2*SL]; r1_r = rows1[0:1, 2*SL:3*SL]
        r2_r = rows1[0:1, 3*SL:4*SL]
        sc1_r = rows1[0:1, 4*SL:5*SL]
        ms_r = var_r; q_r = var_r

        for kc in range(8):
            p.op("gpsimd", lambda e, kc=kc: e.dma_start(
                out=w_in_sb[:, kc * DINPROJ:(kc + 1) * DINPROJ],
                in_=E.w_in[kc * 128:(kc + 1) * 128, :]), inc=dG, inck=16)
        p.op("gpsimd", lambda e: e.dma_start(out=convw[:, :], in_=E.conv_wl[:, :]), inc=dG, inck=16)
        p.op("gpsimd", lambda e: e.dma_start(out=convb[:, :], in_=E.conv_bl[:, :]), inc=dG, inck=16)
        w1_loaded = p.cnt[dG]

        for c in range(NBCT):
            for k in range(4):
                p.op("vector", lambda e, c=c, k=k: e.tensor_scalar(
                    diag[:, (c * 4 + k) * 128:(c * 4 + k + 1) * 128],
                    E.I128b[:, :], convw[:, c * 4 + k:c * 4 + k + 1], None, ALU.mult),
                    inc=sV, waits=[(dG, w1_loaded), (dS, E.c_loaded)])
        p.op("vector", lambda e: e.memset(xbc_sb[:, :], 0.0), inc=sV)
        diag_done = p.cnt[sV]

        xbuf_free = [0, 0]
        pzx_free = [None, None, None, None]
        pconv_free = [None, None]
        stats_free = 0
        rows_free_t = 0
        xbf_free_t = 0
        ain_free_t = 0
        xn_stored = [0, 0]
        silz_stored = [0, 0]
        dtraw_stored = 0
        xc_stored = [0, 0]
        mapsb_free = 0
        xc_prev_conv = 0

        def load_x(s):
            b = s % 2
            return p.op("sync", lambda e, s=s, b=b: e.dma_start(
                out=xbuf[:, b * 8 * SL:(b + 1) * 8 * SL].rearrange("p (c t) -> p c t", t=SL),
                in_=E.xt[:, s * SL:(s + 1) * SL].rearrange("(c p) t -> p c t", p=128)),
                inc=dS, inck=16, waits=[(sV, xbuf_free[b])])

        x_ld = [None] * (NSL + 1)
        x_ld[0] = load_x(0)

        for s in range(NSL):
            b = s % 2
            xv = lambda ch, b=b: xbuf[:, (b * 8 + ch) * SL:(b * 8 + ch + 1) * SL]
            if s + 1 < NSL:
                x_ld[s + 1] = load_x(s + 1)

            for ch in range(8):
                p.op("vector", lambda e, ch=ch, xv=xv: e.tensor_copy(
                    xbf[:, ch * SL:(ch + 1) * SL], xv(ch)), inc=sV,
                    waits=[(dS, x_ld[s]), (sT, xbf_free_t)])
                p.op("scalar", lambda e, ch=ch, xv=xv: e.activation(
                    xsq[:, ch * SL:(ch + 1) * SL], xv(ch), AF.Square), inc=sS,
                    waits=[(dS, x_ld[s]), (sT, xbf_free_t), (sT, ain_free_t)])
            xbf_done = p.cnt[sV]; xsq_done = p.cnt[sS]

            for ch in range(8):
                p.op("tensor", lambda e, ch=ch: e.matmul(
                    pstatA[0:1, :], E.ones128b[:, 0:1], xbf[:, ch * SL:(ch + 1) * SL],
                    start=(ch == 0), stop=(ch == 7)),
                    inc=(sT if ch == 7 else None),
                    waits=[(sV, xbf_done), (sV, stats_free)] if ch == 0 else [])
            for ch in range(8):
                p.op("tensor", lambda e, ch=ch: e.matmul(
                    pstatB[0:1, :], E.ones128b[:, 0:1], xsq[:, ch * SL:(ch + 1) * SL],
                    start=(ch == 0), stop=(ch == 7)),
                    inc=(sT if ch == 7 else None),
                    waits=[(sS, xsq_done)] if ch == 0 else [])
            stats_mm = p.cnt[sT]
            xbf_free_t = stats_mm

            p.op("vector", lambda e: e.tensor_scalar(mu_r, pstatA[0:1, :], 1.0 / 1024, None, ALU.mult),
                 inc=sV, waits=[(sT, stats_mm), (sT, rows_free_t)])
            p.op("vector", lambda e: e.tensor_scalar(ms_r, pstatB[0:1, :], 1.0 / 1024, None, ALU.mult),
                 inc=sV)
            stats_evaced = p.cnt[sV]
            p.op("scalar", lambda e: e.activation(sc1_r, mu_r, AF.Square), inc=sS,
                 waits=[(sV, stats_evaced), (sT, rows_free_t)])
            p.op("vector", lambda e: e.tensor_sub(var_r, ms_r, sc1_r), inc=sV,
                 waits=[(sS, p.cnt[sS])])
            p.op("scalar", lambda e: e.activation(sc1_r, var_r, AF.Ln, bias=E.epsb[0:1, 0:1]), inc=sS,
                 waits=[(sV, p.cnt[sV]), (sV, E.epsb_ready)])
            p.op("scalar", lambda e: e.activation(r1_r, sc1_r, AF.Exp, scale=-0.5), inc=sS)
            p.op("scalar", lambda e: e.activation(sc1_r, r1_r, AF.Square), inc=sS)
            p.op("vector", lambda e: e.tensor_mul(q_r, var_r, sc1_r), inc=sV,
                 waits=[(sS, p.cnt[sS])])
            p.op("scalar", lambda e: e.activation(sc1_r, q_r, AF.Ln, bias=E.epsb[0:1, 1:2]), inc=sS,
                 waits=[(sV, p.cnt[sV])])
            p.op("scalar", lambda e: e.activation(sc1_r, sc1_r, AF.Exp, scale=-0.5), inc=sS)
            p.op("vector", lambda e: e.tensor_mul(r2_r, r1_r, sc1_r), inc=sV,
                 waits=[(sS, p.cnt[sS])])
            rows_done_v = p.cnt[sV]
            rows_done_s = p.cnt[sS]

            prev_evac = None
            for i, (row, mi) in enumerate([(mu_r, 0), (r1_r, 1), (r2_r, 2)]):
                w = [(sV, rows_done_v), (sS, rows_done_s)]
                if i == 0:
                    w += [pzx_free[3], (sV, mapsb_free)]
                else:
                    w.append((sV, prev_evac))
                n = p.op("tensor", lambda e, row=row: e.matmul(
                    pzx[:, 1536:2048], E.ones1f[0:1, :], row, start=True, stop=True),
                    inc=sT, waits=w)
                prev_evac = p.op("vector", lambda e, mi=mi: e.tensor_copy(
                    mapsb[:, mi * SL:(mi + 1) * SL], pzx[:, 1536:2048]), inc=sV,
                    waits=[(sT, n)])
            maps_done = p.cnt[sV]
            rows_free_t = p.cnt[sT]
            pzx_free[3] = (sV, maps_done)

            for ch in range(8):
                hb = ch % 2
                wv = [(dG, xn_stored[hb])]
                if ch == 0:
                    wv += [(sV, maps_done), (sT, ain_free_t)]
                p.op("vector", lambda e, ch=ch, xv=xv: e.tensor_sub(tch[:, :], xv(ch), mapsb[:, 0:SL]),
                     inc=sV, waits=wv)
                n = p.op("vector", lambda e, ch=ch, hb=hb: e.tensor_mul(
                    xnb[:, hb * SL:(hb + 1) * SL], tch[:, :], mapsb[:, SL:2*SL]), inc=sV)
                p.op("vector", lambda e, ch=ch: e.tensor_mul(
                    ain_sb[:, ch * SL:(ch + 1) * SL], tch[:, :], mapsb[:, 2*SL:3*SL]), inc=sV)
                xn_stored[hb] = p.op("gpsimd", lambda e, s=s, ch=ch, hb=hb: e.dma_start(
                    out=E.sp_xn[ch * 128:(ch + 1) * 128, s * SL:(s + 1) * SL],
                    in_=xnb[:, hb * SL:(hb + 1) * SL]),
                    inc=dG, inck=16, waits=[(sV, n)])
            apply_done = p.cnt[sV]
            xbuf_free[b] = apply_done
            mapsb_free = apply_done

            for mt in range(NMT):
                r = mt % 4
                M = 128 if mt < NMT - 1 else 32
                w = []
                if mt == 0:
                    w += [(sV, apply_done), (dG, w1_loaded)]
                if pzx_free[r] is not None:
                    w.append(pzx_free[r])
                for kc in range(8):
                    p.op("tensor", lambda e, kc=kc, mt=mt, r=r, M=M: e.matmul(
                        pzx[0:M, r * 512:r * 512 + 512],
                        w_in_sb[:, kc * DINPROJ + mt * 128: kc * DINPROJ + mt * 128 + M],
                        ain_sb[:, kc * SL:(kc + 1) * SL],
                        start=(kc == 0), stop=(kc == 7)),
                        inc=(sT if kc == 7 else None),
                        waits=w if kc == 0 else [])
                g = p.cnt[sT]
                if mt < NZT:
                    hb = mt % 2
                    wz = [(sT, g), (dG, silz_stored[hb])]
                    n = p.op("scalar", lambda e, hb=hb, r=r: e.activation(
                        szb[:, hb * SL:(hb + 1) * SL], pzx[:, r * 512:r * 512 + 512], AF.Silu),
                        inc=sS, waits=wz)
                    pzx_free[r] = (sS, n)
                    silz_stored[hb] = p.op("gpsimd", lambda e, s=s, mt=mt, hb=hb: e.dma_start(
                        out=E.sp_silz[mt * 128:(mt + 1) * 128, s * SL:(s + 1) * SL],
                        in_=szb[:, hb * SL:(hb + 1) * SL]),
                        inc=dG, inck=16, waits=[(sS, n)])
                elif mt < NZT + NBCT:
                    c = mt - NZT
                    wz = [(sT, g)]
                    if c == 0:
                        wz += [(sT, xc_prev_conv)]
                    n = p.op("vector", lambda e, c=c, r=r: e.tensor_copy(
                        xbc_sb[:, c * 516 + 3:c * 516 + 515], pzx[:, r * 512:r * 512 + 512]),
                        inc=sV, waits=wz)
                    pzx_free[r] = (sV, n)
                else:
                    n = p.op("vector", lambda e, r=r: e.tensor_copy(
                        dtraw_sb[0:32, :], pzx[0:32, r * 512:r * 512 + 512]),
                        inc=sV, waits=[(sT, g), (dG, dtraw_stored)])
                    pzx_free[r] = (sV, n)
            zx_done_v = p.cnt[sV]; zx_done_s = p.cnt[sS]
            ain_free_t = p.cnt[sT]
            dtraw_stored = p.op("gpsimd", lambda e, s=s: e.dma_start(
                out=E.sp_dtraw[:, s * SL:(s + 1) * SL], in_=dtraw_sb[0:32, :]),
                inc=dG, inck=16, waits=[(sV, zx_done_v)])

            for c in range(NBCT):
                r = c % 2
                w = [(sV, zx_done_v), (sV, diag_done)]
                if pconv_free[r] is not None:
                    w.append(pconv_free[r])
                for k in range(4):
                    p.op("tensor", lambda e, c=c, k=k, r=r: e.matmul(
                        pconv[:, r * 512:r * 512 + 512],
                        diag[:, (c * 4 + k) * 128:(c * 4 + k + 1) * 128],
                        xbc_sb[:, c * 516 + k:c * 516 + k + 512],
                        start=(k == 0), stop=(k == 3)),
                        inc=(sT if k == 3 else None),
                        waits=w if k == 0 else [])
                g = p.cnt[sT]
                hb = c % 2
                wz = [(sT, g), (dG, xc_stored[hb])]
                n = p.op("scalar", lambda e, hb=hb, c=c, r=r: e.activation(
                    xcb[:, hb * SL:(hb + 1) * SL], pconv[:, r * 512:r * 512 + 512],
                    AF.Silu, bias=convb[:, c:c + 1]),
                    inc=sS, waits=wz)
                pconv_free[r] = (sS, n)
                xc_stored[hb] = p.op("gpsimd", lambda e, s=s, c=c, hb=hb: e.dma_start(
                    out=E.sp_xc[c * 128:(c + 1) * 128, s * SL:(s + 1) * SL],
                    in_=xcb[:, hb * SL:(hb + 1) * SL]),
                    inc=dG, inck=16, waits=[(sS, n)])
                p.op("vector", lambda e, c=c: e.tensor_copy(
                    xbc_sb[:, c * 516:c * 516 + 3], xbc_sb[:, c * 516 + 512:c * 516 + 515]),
                    inc=sV, waits=[(sT, g)])
            xc_prev_conv = p.cnt[sT]
            stats_free = stats_evaced
            E.p1_slice_dG[s] = p.cnt[dG]

        E.p1_spills = p.cnt[dG]
        if nphases == 1:
            p.op("sync", lambda e: e.nop(), waits=[(dG, E.p1_spills)])
    px.close()


def _phase2(E, nphases):
    nc, p = E.nc, E.p
    p.fence()
    dS, dG, sT, sV, sS, sG = E.dS, E.dG, E.sT, E.sV, E.sS, E.sG
    px = ExitStack()
    sbl = lambda n, s, d: px.enter_context(nc.sbuf_tensor(n, s, d))
    w_out_sb = sbl("w_out_sb", [128, 16 * 1024], BF16)
    wstage = sbl("wstage", [128, 2 * 1024], F32)
    normw_sb = sbl("normw_sb", [128, 16], F32)
    xctok = sbl("xctok", [128, 2 * 2176], BF16)
    bcfeat = sbl("bcfeat", [128, 512], BF16)
    siluzt = sbl("siluzt", [128, 2 * 2048], BF16)
    xn_c = sbl("xn_c", [128, 8 * 256], F32)
    dtraw_c = sbl("dtraw_c", [32, 256], F32)
    dsm = sbl("dsm", [32, 5 * 256], F32)
    dsb = sbl("dsb", [32, 4 * 256], BF16)
    cdcol = sbl("cdcol", [32, 1], F32)
    mEXA = sbl("mEXA", [128, 2 * 2048], BF16)
    mU = sbl("mU", [128, 2048], BF16)
    mSKIP = sbl("mSKIP", [128, 2048], BF16)
    mDECAY = sbl("mDECAY", [128, 2048], F32)
    u_tok = sbl("u_tok", [128, 2 * 2048], BF16)
    udec = sbl("udec", [128, 2 * 2048], BF16)
    CBT_sb = sbl("CBT_sb", [128, 512], BF16)
    rowb = sbl("rowb", [1, 256], BF16)
    mapb2 = sbl("mapb2", [128, 256], BF16)
    LT = sbl("LT", [128, 512], BF16)
    Msb = sbl("Msb", [128, 512], BF16)
    Abuf = sbl("Abuf", [128, 2048], F32)
    Bbuf = sbl("Bbuf", [128, 2048], F32)
    gtok = sbl("gtok", [128, 2 * 2048], BF16)
    gfeat = sbl("gfeat", [128, 16 * 256], BF16)
    hbuf = sbl("hbuf", [128, 2 * 256], F32)
    prev = sbl("prev", [128, 2048], F32)
    prevbf = sbl("prevbf", [128, 2048], BF16)
    rcol = sbl("rcol", [128, 4], F32)
    eyeA = sbl("eyeA2", [128, 256], BF16)
    eyeB = sbl("eyeB2", [128, 256], BF16)
    REPb = sbl("REPb2", [32, 2048], BF16)
    REPf = sbl("REPf2", [32, 2048], F32)
    ones32f = sbl("ones32f2", [32, 256], F32)
    for t, cc in [(eyeA, "eyeA"), (eyeB, "eyeB"), (REPb, "REPb"), (REPf, "REPf"),
                  (ones32f, "ones32f")]:
        p.op("sync", lambda e, t=t, cc=cc: e.dma_start(out=t[:, :], in_=E.cst[cc][:, :]),
             inc=dS, inck=16)
    E.c2_loaded = p.cnt[dS]
    pA = px.enter_context(nc.psum_tensor("pA", [128, 2048], F32))
    pB = px.enter_context(nc.psum_tensor("pB", [128, 2048], F32))

    s_abs = dsm[0:32, 0:256]; s_dt = dsm[0:32, 256:512]
    s_a = dsm[0:32, 512:768]; s_acum = dsm[0:32, 768:1024]
    s_dec = dsm[0:32, 1024:1280]
    b_dt = dsb[0:32, 0:256]; b_dtdec = dsb[0:32, 256:512]
    b_expa = dsb[0:32, 512:768]; b_expacum = dsb[0:32, 768:1024]

    p.op("gpsimd", lambda e: e.dma_start(out=normw_sb[:, :], in_=E.normw[:, :]), inc=dG, inck=16)
    nwl = p.cnt[dG]
    ws_free = [0, 0]
    for kc in range(16):
        hb = kc % 2
        n = p.op("gpsimd", lambda e, kc=kc, hb=hb: e.dma_start(
            out=wstage[:, hb * 1024:(hb + 1) * 1024],
            in_=E.w_out[kc * 128:(kc + 1) * 128, :]),
            inc=dG, inck=16, waits=[(sV, ws_free[hb])])
        ws_free[hb] = p.op("vector", lambda e, kc=kc, hb=hb: e.tensor_scalar(
            w_out_sb[:, kc * 1024:(kc + 1) * 1024], wstage[:, hb * 1024:(hb + 1) * 1024],
            normw_sb[:, kc:kc + 1], None, ALU.mult),
            inc=sV, waits=[(dG, n), (dG, nwl)])
    wout_ready = p.cnt[sV]

    for i in range(4):
        w = [(dS, E.c_loaded), (dS, E.c2_loaded)] if i == 0 else []
        n = p.op("tensor", lambda e, i=i: e.matmul(
            pB[0:1, i * 512:(i + 1) * 512], E.smalls[0:32, 2:3], REPf[:, i * 512:(i + 1) * 512],
            start=True, stop=True), inc=sT, waits=w)
    nv = p.op("vector", lambda e: e.tensor_copy(Abuf[0:1, :], pB[0:1, :]), inc=sV, waits=[(sT, n)])
    for i in range(4):
        w = [(sV, nv)] if i == 0 else []
        n = p.op("tensor", lambda e, i=i: e.matmul(
            pA[:, i * 512:(i + 1) * 512], E.ones1f[0:1, :], Abuf[0:1, i * 512:(i + 1) * 512],
            start=True, stop=True), inc=sT, waits=w)
    nv2 = p.op("vector", lambda e: e.tensor_copy(mSKIP[:, :], pA[:, :]), inc=sV, waits=[(sT, n)])
    skip_ready = nv2
    pA_free = (sV, nv2)
    pB_free = (sV, nv)

    def rep_map(lhs_slice, rep, dtype_dst, dst_ap, wait_mm, wait_evac, ps):
        for i in range(4):
            w = list(wait_mm) if i == 0 else []
            n = p.op("tensor", lambda e, i=i, lhs_slice=lhs_slice, rep=rep, ps=ps: e.matmul(
                ps[:, i * 512:(i + 1) * 512], lhs_slice, rep[:, i * 512:(i + 1) * 512],
                start=True, stop=True), inc=sT, waits=w)
        return p.op("vector", lambda e, dst_ap=dst_ap, ps=ps: e.tensor_copy(dst_ap, ps[:, :]),
                    inc=sV, waits=[(sT, n)] + list(wait_evac))

    prev_ready_v = 0
    hb_free = [0, 0]
    gfeat_free_t = 0
    gtok_free_d = 0
    xn_free_v = 0
    xctok_free = [(0, 0), (0, 0)]
    siluz_free = [0, 0]
    bc_free_t = 0
    dtraw_free_s = 0
    mexa_free = [0, 0]
    utok_free_t = 0
    udec_free_t = 0
    mdecay_free_g = 0
    abuf_free = 0
    bbuf_free = 0
    prevbf_free_t = 0
    wout_evacs = []

    for c in range(NCH):
        sl = c // 2
        p1rdy = E.p1_slice_dG[sl]
        ld_xct = []
        for tt in range(2):
            ld_xct.append(p.op("sync", lambda e, c=c, tt=tt: e.dma_start_transpose(
                xctok[:, tt * 2176:(tt + 1) * 2176],
                E.sp_xc[0:2176, c * 256 + tt * 128: c * 256 + (tt + 1) * 128]),
                inc=dS, inck=16,
                waits=[(dG, p1rdy), (sT, xctok_free[tt][0]), (sG, xctok_free[tt][1])]))
        ld_bc = p.op("sync", lambda e, c=c: e.dma_start(
            out=bcfeat[:, 0:512].rearrange("p (g t) -> p g t", t=256),
            in_=E.sp_xc[2048:2304, c * 256:(c + 1) * 256].rearrange("(g p) t -> p g t", p=128)),
            inc=dS, inck=16, waits=[(dG, p1rdy), (sT, bc_free_t)])
        ld_sz = []
        for tt in range(2):
            ld_sz.append(p.op("sync", lambda e, c=c, tt=tt: e.dma_start_transpose(
                siluzt[:, tt * 2048:(tt + 1) * 2048],
                E.sp_silz[0:2048, c * 256 + tt * 128: c * 256 + (tt + 1) * 128]),
                inc=dS, inck=16, waits=[(dG, p1rdy), (sV, siluz_free[tt])]))
        ld_xn = p.op("sync", lambda e, c=c: e.dma_start(
            out=xn_c.rearrange("p (g t) -> p g t", t=256),
            in_=E.sp_xn[:, c * 256:(c + 1) * 256].rearrange("(g p) t -> p g t", p=128)),
            inc=dS, inck=16, waits=[(dG, p1rdy), (sV, xn_free_v)])
        ld_dt = p.op("sync", lambda e, c=c: e.dma_start(
            out=dtraw_c[:, :], in_=E.sp_dtraw[0:32, c * 256:(c + 1) * 256]),
            inc=dS, inck=16, waits=[(dG, p1rdy), (sS, dtraw_free_s)])

        dbgw = [(dG, getattr(E, "dbg_guard", 0))] if (E.dbg and c == 1) else []
        p.op("scalar", lambda e: e.activation(s_abs, dtraw_c[:, :], AF.Abs, bias=E.smalls[0:32, 0:1]),
             inc=sS, waits=[(dS, ld_dt)] + dbgw)
        p.op("scalar", lambda e: e.activation(s_abs, s_abs, AF.Exp, scale=-1.0), inc=sS)
        p.op("scalar", lambda e: e.activation(s_abs, s_abs, AF.Ln, bias=E.epsb[0:32, 3:4]), inc=sS)
        p.op("scalar", lambda e: e.activation(s_dt, dtraw_c[:, :], AF.Relu, bias=E.smalls[0:32, 0:1]),
             inc=sS)
        dtraw_free_s = p.cnt[sS]
        ns = p.cnt[sS]
        p.op("vector", lambda e: e.tensor_add(s_dt, s_dt, s_abs), inc=sV, waits=[(sS, ns)])
        p.op("vector", lambda e: e.tensor_copy(b_dt, s_dt), inc=sV)
        p.op("vector", lambda e: e.tensor_scalar(s_a, s_dt, E.smalls[0:32, 1:2], None, ALU.mult),
             inc=sV)
        p.op("vector", lambda e: e.reduce_sum(E.smalls[0:32, 5:6], s_a, axis=AXX), inc=sV)
        p.op("vector", lambda e: e.tensor_tensor_scan(s_acum, ones32f[:, :], s_a, 0.0,
                                                      ALU.mult, ALU.add), inc=sV)
        nv = p.cnt[sV]
        p.op("scalar", lambda e: e.activation(b_expa, s_a, AF.Exp), inc=sS, waits=[(sV, nv)])
        p.op("scalar", lambda e: e.activation(b_expacum, s_acum, AF.Exp), inc=sS)
        p.op("scalar", lambda e: e.activation(cdcol[0:32, :], E.smalls[0:32, 5:6], AF.Exp), inc=sS)
        p.op("vector", lambda e: e.tensor_scalar(s_dec, s_acum, E.smalls[0:32, 5:6], -1.0,
                                                 ALU.subtract, ALU.mult), inc=sV)
        nv = p.cnt[sV]
        p.op("scalar", lambda e: e.activation(s_dec, s_dec, AF.Exp), inc=sS,
             waits=[(sV, nv)])
        ns = p.cnt[sS]
        p.op("vector", lambda e: e.tensor_tensor(b_dtdec, s_dt, s_dec, ALU.mult), inc=sV,
             waits=[(sS, ns)])
        dt_small_v = p.cnt[sV]; dt_small_s = p.cnt[sS]

        prevbf_done = 0
        if c > 0:
            prevbf_done = p.op("vector", lambda e: e.tensor_copy(prevbf[:, :], prev[:, :]),
                               inc=sV, waits=[(sV, prev_ready_v)])

        mexa_ev = []
        for tt in range(2):
            nn = rep_map(dsb[0:32, 768 + tt * 128:768 + (tt + 1) * 128], REPb, BF16,
                         mEXA[:, tt * 2048:(tt + 1) * 2048],
                         [(sS, dt_small_s), pA_free], [(sV, mexa_free[tt])], pA)
            pA_free = (sV, nn)
            mexa_ev.append(nn)
        upool = 0
        for j, (off, dst, freet) in enumerate(((0, u_tok, utok_free_t), (256, udec, udec_free_t))):
            for tt in range(2):
                nn = rep_map(dsb[0:32, off + tt * 128:off + (tt + 1) * 128], REPb, BF16,
                             mU[:, :], [(sV, dt_small_v), pA_free], [(sG, upool)], pA)
                pA_free = (sV, nn)
                upool = p.op("gpsimd", lambda e, dst=dst, tt=tt: e.tensor_tensor(
                    dst[:, tt * 2048:(tt + 1) * 2048], xctok[:, tt * 2176:tt * 2176 + 2048],
                    mU[:, :], ALU.mult), inc=sG,
                    waits=[(sV, nn), (dS, ld_xct[tt]), (sT, freet)])
        upool_done = p.cnt[sG]
        for i in range(4):
            w = [(sS, dt_small_s), pB_free] if i == 0 else []
            n = p.op("tensor", lambda e, i=i: e.matmul(
                pB[0:1, i * 512:(i + 1) * 512], cdcol[0:32, :], REPf[:, i * 512:(i + 1) * 512],
                start=True, stop=True), inc=sT, waits=w)
        dbgw = [(dG, E.dbg_guard)] if (E.dbg and c == 1) else []
        drow_done = p.op("vector", lambda e: e.tensor_copy(Abuf[0:1, :], pB[0:1, :]), inc=sV,
                         waits=[(sT, n), (sG, mdecay_free_g), (sV, abuf_free)] + dbgw)
        for i in range(4):
            w = [(sV, drow_done), pA_free] if i == 0 else []
            n = p.op("tensor", lambda e, i=i: e.matmul(
                pA[:, i * 512:(i + 1) * 512], E.ones1f[0:1, :], Abuf[0:1, i * 512:(i + 1) * 512],
                start=True, stop=True), inc=sT, waits=w)
        mdecay_done = p.op("vector", lambda e: e.tensor_copy(mDECAY[:, :], pA[:, :]), inc=sV,
                           waits=[(sT, n)])
        pA_free = (sV, mdecay_done)

        for st in range(2):
            w = [(dS, ld_bc), pA_free] if st == 0 else []
            n = p.op("tensor", lambda e, st=st: e.matmul(
                pA[0:128, st * 256:(st + 1) * 256],
                bcfeat[:, st * 128:(st + 1) * 128],
                bcfeat[:, 256:512], start=True, stop=True), inc=sT, waits=w)
        cbt_done = p.op("vector", lambda e: e.tensor_copy(CBT_sb[:, :], pA[:, 0:512]), inc=sV,
                        waits=[(sT, n)])
        pA_free = (sV, cbt_done)

        if c > 0:
            y0 = []
            for tt in range(2):
                w0 = [(sV, prevbf_done), (dS, ld_bc), pB_free]
                for h in range(NH):
                    n3 = p.op("tensor", lambda e, tt=tt, h=h: e.matmul(
                        pB[:, h * 64:(h + 1) * 64],
                        bcfeat[:, 256 + tt * 128: 384 + tt * 128],
                        prevbf[:, h * 64:(h + 1) * 64], start=True, stop=True), inc=sT,
                        waits=w0 if h == 0 else [])
                    w0 = []
                dstb = Abuf if tt == 0 else Bbuf
                nn = p.op("vector", lambda e, tt=tt, dstb=dstb: e.scalar_tensor_tensor(
                    dstb[:, :], pB[:, :], 1.0, mEXA[:, tt * 2048:(tt + 1) * 2048],
                    ALU.mult, ALU.mult), inc=sV,
                    waits=[(sT, n3), (sV, mexa_ev[tt]), (sV, bbuf_free)])
                pB_free = (sV, nn)
                y0.append(nn)
            y0a, y0b = y0
            prevbf_free_t = p.cnt[sT]
        else:
            tcur = p.cnt[sT]
            y0a = p.op("vector", lambda e: e.memset(Abuf[:, :], 0.0), inc=sV,
                       waits=[(sT, tcur), (sV, abuf_free)])
            y0b = p.op("vector", lambda e: e.memset(Bbuf[:, :], 0.0), inc=sV,
                       waits=[(sV, bbuf_free)])
        mexa_free = [y0a, y0b]

        psE = pB[0:1, 1024:1280]
        psBC = pB[:, 1536:1792]
        psT1 = pB[:, 0:1024]
        psE_free = pB_free
        psBC_free = None
        e1_last = 0
        lt_free_g = 0
        rowb_free_t = 0
        t1_evacs = []
        for h in range(NH):
            n1 = p.op("tensor", lambda e, h=h: e.matmul(
                psE, E.I32b[:, h:h + 1], b_expa, start=True, stop=True), inc=sT,
                waits=[(sS, dt_small_s), psE_free])
            e1 = p.op("vector", lambda e: e.tensor_copy(rowb[0:1, :], psE), inc=sV,
                      waits=[(sT, n1), (sT, rowb_free_t)])
            psE_free = (sV, e1)
            n2 = p.op("tensor", lambda e: e.matmul(
                psBC, E.ones1b[0:1, :], rowb[0:1, :], start=True, stop=True), inc=sT,
                waits=[(sV, e1)] + ([psBC_free] if psBC_free else []))
            rowb_free_t = n2
            e2 = p.op("vector", lambda e: e.tensor_copy(mapb2[:, :], psBC), inc=sV,
                      waits=[(sT, n2)])
            psBC_free = (sV, e2)
            p.op("vector", lambda e: e.tensor_tensor_scan(
                LT[:, 0:256], mapb2[:, :], eyeA[:, :], 0.0, ALU.mult, ALU.add), inc=sV,
                waits=[(sG, lt_free_g), (dS, E.c2_loaded)])
            e3 = p.op("vector", lambda e: e.tensor_tensor_scan(
                LT[:, 256:512], mapb2[:, :], eyeB[:, :], 0.0, ALU.mult, ALU.add), inc=sV)
            gm = p.op("gpsimd", lambda e: e.tensor_tensor(Msb[:, 0:256], CBT_sb[:, 0:256],
                                                          LT[:, 0:256], ALU.mult), inc=sG,
                      waits=[(sV, e3), (sV, cbt_done), (sT, e1_last)])
            gm = p.op("gpsimd", lambda e: e.tensor_tensor(Msb[:, 256:512], CBT_sb[:, 256:512],
                                                          LT[:, 256:512], ALU.mult), inc=sG)
            lt_free_g = gm
            w = [(sG, gm), (sG, upool_done)]
            if h == 0:
                w += [pA_free]
            p.op("tensor", lambda e, h=h: e.matmul(
                pA[:, h * 64:(h + 1) * 64], Msb[:, 0:128], u_tok[:, h * 64:(h + 1) * 64],
                start=True, stop=True), inc=sT, waits=w)
            hh = h % 16
            w = [(sV, t1_evacs[0])] if h == 16 else []
            p.op("tensor", lambda e, h=h, hh=hh: e.matmul(
                psT1[:, hh * 64:(hh + 1) * 64], Msb[:, 128:256],
                u_tok[:, h * 64:(h + 1) * 64], start=True, stop=False), inc=sT, waits=w)
            e1_last = p.op("tensor", lambda e, h=h, hh=hh: e.matmul(
                psT1[:, hh * 64:(hh + 1) * 64], Msb[:, 384:512],
                u_tok[:, 2048 + h * 64: 2048 + (h + 1) * 64], start=False, stop=True), inc=sT)
            if h in (15, 31):
                half = 0 if h == 15 else 1
                nn = p.op("vector", lambda e, half=half: e.tensor_add(
                    Bbuf[:, half * 1024:(half + 1) * 1024], psT1[:, :],
                    Bbuf[:, half * 1024:(half + 1) * 1024]), inc=sV,
                    waits=[(sT, e1_last), (sV, y0b)])
                t1_evacs.append(nn)
        y1a = p.op("vector", lambda e: e.tensor_add(Abuf[:, :], pA[:, :], Abuf[:, :]), inc=sV,
                   waits=[(sT, e1_last), (sV, y0a)])
        pA_free = (sV, y1a)
        pB_free = (sV, t1_evacs[-1])
        utok_free_t = p.cnt[sT]

        g_done = []
        skip_g = []
        for tt in range(2):
            buf = Abuf if tt == 0 else Bbuf
            yev = y1a if tt == 0 else t1_evacs[-1]
            ng = p.op("gpsimd", lambda e, tt=tt: e.tensor_tensor(
                gtok[:, tt * 2048:(tt + 1) * 2048], xctok[:, tt * 2176:tt * 2176 + 2048],
                mSKIP[:, :], ALU.mult), inc=sG,
                waits=[(sV, skip_ready), (dS, ld_xct[tt]), (dS, gtok_free_d), (sS, 0)])
            skip_g.append(ng)
            p.op("vector", lambda e, buf=buf, tt=tt: e.tensor_add(
                buf[:, :], buf[:, :], gtok[:, tt * 2048:(tt + 1) * 2048]), inc=sV,
                waits=[(sG, ng), (sV, yev)])
            nn = p.op("vector", lambda e, buf=buf, tt=tt: e.tensor_tensor(
                buf[:, :], buf[:, :], siluzt[:, tt * 2048:(tt + 1) * 2048], ALU.mult), inc=sV,
                waits=[(dS, ld_sz[tt])])
            siluz_free[tt] = nn
            if E.dbg and c == 0 and tt == 0:
                ndbg = p.op("gpsimd", lambda e: e.dma_start(out=E.t_G[:, :], in_=Abuf[:, :]),
                            inc=dG, inck=16, waits=[(sV, nn)])
            ns = p.op("scalar", lambda e, buf=buf, tt=tt: e.activation(
                gtok[:, tt * 2048:(tt + 1) * 2048], buf[:, :], AF.Square,
                accum_out=rcol[:, tt:tt + 1]), inc=sS, waits=[(sV, nn), (sG, ng)])
            ns = p.op("scalar", lambda e, tt=tt: e.activation(
                rcol[:, 2 + tt:3 + tt], rcol[:, tt:tt + 1], AF.Ln, scale=1.0 / 2048,
                bias=E.epsb[:, 2:3]), inc=sS)
            ns = p.op("scalar", lambda e, tt=tt: e.activation(
                rcol[:, 2 + tt:3 + tt], rcol[:, 2 + tt:3 + tt], AF.Exp, scale=-0.5), inc=sS)
            ng2 = p.op("vector", lambda e, buf=buf, tt=tt: e.tensor_scalar(
                gtok[:, tt * 2048:(tt + 1) * 2048], buf[:, :], rcol[:, 2 + tt:3 + tt],
                None, ALU.mult), inc=sV, waits=[(sS, ns)])
            g_done.append(ng2)
        abuf_free = g_done[0]
        bbuf_free = g_done[1]

        if E.dbg and c == 0:
            p.op("gpsimd", lambda e: e.dma_start(out=E.t_dsm[:, :], in_=dsm[0:32, :]),
                 inc=dG, inck=16, waits=[(sV, dt_small_v), (sS, dt_small_s)])
            p.op("gpsimd", lambda e: e.dma_start(out=E.t_u[:, :], in_=u_tok[:, :]),
                 inc=dG, inck=16, waits=[(sG, upool_done)])
            p.op("gpsimd", lambda e: e.dma_start(out=E.t_ud[:, :], in_=udec[:, :]),
                 inc=dG, inck=16, waits=[(sG, upool_done)])
            p.op("gpsimd", lambda e: e.dma_start(out=E.t_cbt[:, :], in_=CBT_sb[:, :]),
                 inc=dG, inck=16, waits=[(sV, cbt_done)])
            p.op("gpsimd", lambda e: e.dma_start(out=E.t_mexa[:, :], in_=mEXA[:, :]),
                 inc=dG, inck=16, waits=[(sV, mexa_ev[1])])
            p.op("gpsimd", lambda e: e.dma_start(out=E.t_msb[:, :], in_=Msb[:, :]),
                 inc=dG, inck=16, waits=[(sG, lt_free_g)])
            p.op("gpsimd", lambda e: e.dma_start(out=E.t_g[:, :], in_=gtok[:, :]),
                 inc=dG, inck=16, waits=[(sV, g_done[1])])
            E.dbg_guard = p.cnt[dG]

        tr = 0
        for tt in range(2):
            for hk in range(16):
                tr = p.op("sync", lambda e, tt=tt, hk=hk: e.dma_start_transpose(
                    gfeat[:, hk * 256 + tt * 128: hk * 256 + (tt + 1) * 128],
                    gtok[:, tt * 2048 + hk * 128: tt * 2048 + (hk + 1) * 128]),
                    inc=dS, inck=16,
                    waits=[(sV, g_done[tt]), (sT, gfeat_free_t)])
        gtok_free_d = tr

        st_regs = []
        for st in range(2):
            psreg = pA if st == 0 else pB
            w = [(sG, upool_done), (dS, ld_xct[st]), (pA_free if st == 0 else pB_free)]
            for h in range(NH):
                n2 = p.op("tensor", lambda e, st=st, h=h, psreg=psreg: e.matmul(
                    psreg[:, h * 64:(h + 1) * 64], xctok[:, st * 2176 + 2048: st * 2176 + 2176],
                    udec[:, st * 2048 + h * 64: st * 2048 + (h + 1) * 64],
                    start=True, stop=True), inc=sT, waits=w if h == 0 else [])
                w = []
            st_regs.append(n2)
        udec_free_t = p.cnt[sT]
        bc_free_t = p.cnt[sT]
        e2t = p.cnt[sT]
        xctok_free = [(e2t, skip_g[0]), (e2t, skip_g[1])]

        if c == 0:
            p.op("vector", lambda e: e.tensor_copy(prev[:, :], pA[:, :]), inc=sV,
                 waits=[(sT, st_regs[0])])
            prev_ready_v = p.op("vector", lambda e: e.tensor_add(prev[:, :], pB[:, :], prev[:, :]),
                                inc=sV, waits=[(sT, st_regs[1])])
        else:
            ng = p.op("gpsimd", lambda e: e.tensor_tensor(prev[:, :], prev[:, :], mDECAY[:, :],
                                                          ALU.mult), inc=sG,
                      waits=[(sV, mdecay_done), (sT, prevbf_free_t)])
            mdecay_free_g = ng
            p.op("vector", lambda e: e.tensor_add(prev[:, :], pA[:, :], prev[:, :]), inc=sV,
                 waits=[(sT, st_regs[0]), (sG, ng)])
            prev_ready_v = p.op("vector", lambda e: e.tensor_add(prev[:, :], pB[:, :], prev[:, :]),
                                inc=sV, waits=[(sT, st_regs[1])])
        pA_free = (sV, prev_ready_v)
        pB_free = (sV, prev_ready_v)

        wout_evacs = []
        for mt in range(8):
            r = mt % 4
            if mt == 0:
                w = [pA_free, (dS, tr), (dG, nwl), (sV, wout_ready)]
            elif mt >= 4:
                w = [wout_evacs[mt - 4]]
            else:
                w = []
            for kc in range(16):
                p.op("tensor", lambda e, mt=mt, kc=kc, r=r: e.matmul(
                    pA[:, r * 512: r * 512 + 256],
                    w_out_sb[:, kc * 1024 + mt * 128: kc * 1024 + (mt + 1) * 128],
                    gfeat[:, kc * 256:(kc + 1) * 256],
                    start=(kc == 0), stop=(kc == 15)),
                    inc=(sT if kc == 15 else None),
                    waits=w if kc == 0 else [])
            g = p.cnt[sT]
            hb = mt % 2
            nn = p.op("vector", lambda e, mt=mt, r=r, hb=hb: e.tensor_add(
                hbuf[:, hb * 256:(hb + 1) * 256], pA[:, r * 512:r * 512 + 256],
                xn_c[:, mt * 256:(mt + 1) * 256]), inc=sV,
                waits=[(sT, g), (dS, ld_xn), (dG, hb_free[hb])])
            wout_evacs.append((sV, nn))
            hb_free[hb] = p.op("gpsimd", lambda e, c=c, mt=mt, hb=hb: e.dma_start(
                out=E.sp_h[mt * 128:(mt + 1) * 128, c * 256:(c + 1) * 256],
                in_=hbuf[:, hb * 256:(hb + 1) * 256]), inc=dG, inck=16,
                waits=[(sV, nn)])
        gfeat_free_t = p.cnt[sT]
        xn_free_v = wout_evacs[-1][1]
        pA_free = wout_evacs[-1]
        if E.dbg and c == 0:
            p.op("gpsimd", lambda e: e.dma_start(out=E.t_prev[:, :], in_=prev[:, :]),
                 inc=dG, inck=16, waits=[(sV, prev_ready_v)])
            p.op("gpsimd", lambda e: e.dma_start(out=E.t_gf[:, :], in_=gfeat[:, :]),
                 inc=dG, inck=16, waits=[(dS, tr)])
        E.p2_chunk_dG = getattr(E, "p2_chunk_dG", [0] * NCH)
        E.p2_chunk_dG[c] = p.cnt[dG]

    E.p2_done_dG = p.cnt[dG]
    if nphases == 2:
        p.op("sync", lambda e: e.nop(), waits=[(dG, E.p2_done_dG)])
    px.close()


def _phase3(E, nphases):
    nc, p = E.nc, E.p
    p.fence()
    dS, dG, sT, sV, sS, sG = E.dS, E.dG, E.sT, E.sV, E.sS, E.sG
    SL3 = 256
    px = ExitStack()
    sbl = lambda n, s, d: px.enter_context(nc.sbuf_tensor(n, s, d))
    wfc_sb = sbl("wfc_sb", [128, 8 * 4096], BF16)
    wproj_sb = sbl("wproj_sb", [128, 32 * 1024], BF16)
    h_sl = sbl("h_sl", [128, 8 * SL3], F32)
    hbf = sbl("hbf", [128, 8 * SL3], BF16)
    r_sb = sbl("r_sb", [128, 32 * SL3], BF16)
    hsq = r_sb
    ao_sb = sbl("ao_sb", [128, 8 * SL3], F32)
    x_sl = sbl("x_sl", [128, 8 * SL3], F32)
    x1_sb = sbl("x1_sb", [128, 8 * SL3], F32)
    rows3 = sbl("rows3", [1, 6 * SL3], F32)
    maps3 = sbl("maps3", [128, 3 * SL3], F32)
    pP3 = px.enter_context(nc.psum_tensor("pP3", [128, 2048], F32))
    pQ3 = px.enter_context(nc.psum_tensor("pQ3", [128, 1024], F32))
    pstA = px.enter_context(nc.psum_tensor("pstA3", [1, 256], F32))
    pstB = px.enter_context(nc.psum_tensor("pstB3", [1, 256], F32))

    mu_r = rows3[0:1, 0:SL3]
    var_r = rows3[0:1, SL3:2 * SL3]
    r1_r = rows3[0:1, 2 * SL3:3 * SL3]
    mur_r = rows3[0:1, 3 * SL3:4 * SL3]
    sc_r = rows3[0:1, 4 * SL3:5 * SL3]
    rstd2_r = rows3[0:1, 5 * SL3:6 * SL3]

    for kc in range(8):
        p.op("gpsimd", lambda e, kc=kc: e.dma_start(
            out=wfc_sb[:, kc * 4096:(kc + 1) * 4096],
            in_=E.w_fc[kc * 128:(kc + 1) * 128, :]), inc=dG, inck=16)
    for kc in range(32):
        p.op("gpsimd", lambda e, kc=kc: e.dma_start(
            out=wproj_sb[:, kc * 1024:(kc + 1) * 1024],
            in_=E.w_proj[kc * 128:(kc + 1) * 128, :]), inc=dG, inck=16)
    w_loaded = p.cnt[dG]

    hbuf_free = [0, 0]
    hbf_free_t = 0
    stats_free = 0
    rows_free_t = 0
    maps_free = 0
    r_free_t = 0
    pP_free = [None] * 4
    pQ_free = None
    ao_free = 0
    x1_stored = 0
    aocast_free_t = 0
    hsl_free_s = 0
    ao_sq_s = 0

    for s in range(16):
        ld_h = p.op("sync", lambda e, s=s: e.dma_start(
            out=h_sl.rearrange("p (g t) -> p g t", t=SL3),
            in_=E.sp_h[:, s * SL3:(s + 1) * SL3].rearrange("(g p) t -> p g t", p=128)),
            inc=dS, inck=16, waits=[(dG, E.p2_done_dG), (sV, ao_free), (sS, hsl_free_s)])
        ld_x = p.op("sync", lambda e, s=s: e.dma_start(
            out=x_sl.rearrange("p (g t) -> p g t", t=SL3),
            in_=E.xt[:, s * SL3:(s + 1) * SL3].rearrange("(g p) t -> p g t", p=128)),
            inc=dS, inck=16, waits=[(sV, x1_stored if False else 0), (sV, maps_free)])

        for ch in range(8):
            p.op("vector", lambda e, ch=ch: e.tensor_copy(
                hbf[:, ch * SL3:(ch + 1) * SL3], h_sl[:, ch * SL3:(ch + 1) * SL3]), inc=sV,
                waits=[(dS, ld_h), (sT, hbf_free_t)])
            p.op("scalar", lambda e, ch=ch: e.activation(
                hsq[:, ch * SL3:(ch + 1) * SL3], h_sl[:, ch * SL3:(ch + 1) * SL3], AF.Square),
                inc=sS, waits=[(dS, ld_h), (sT, r_free_t)])
        cast_v = p.cnt[sV]; cast_s = p.cnt[sS]
        for ch in range(8):
            p.op("tensor", lambda e, ch=ch: e.matmul(
                pstA[0:1, :], E.ones128b[:, 0:1], hsq[:, ch * SL3:(ch + 1) * SL3],
                start=(ch == 0), stop=(ch == 7)), inc=(sT if ch == 7 else None),
                waits=[(sS, cast_s), (sV, stats_free)] if ch == 0 else [])
        g1 = p.cnt[sT]
        nv = p.op("vector", lambda e: e.tensor_copy(sc_r, pstA[0:1, :]), inc=sV,
                  waits=[(sT, g1), (sT, rows_free_t)])
        ns = p.op("scalar", lambda e: e.activation(rstd2_r, sc_r, AF.Ln, scale=1.0 / 1024,
                                                   bias=E.epsb[0:1, 1:2]), inc=sS,
                  waits=[(sV, nv)])
        ns = p.op("scalar", lambda e: e.activation(rstd2_r, rstd2_r, AF.Exp, scale=-1.0), inc=sS)
        stats_evA = nv
        w = [(sS, ns)]
        if pQ_free is not None:
            w.append(pQ_free)
        n = p.op("tensor", lambda e: e.matmul(pQ3[:, 0:256], E.ones1f[0:1, :], rstd2_r,
                                              start=True, stop=True), inc=sT, waits=w)
        m_rstd2 = p.op("vector", lambda e: e.tensor_copy(maps3[:, 2 * SL3:3 * SL3], pQ3[:, 0:256]),
                       inc=sV, waits=[(sT, n)])
        pQ_free = (sV, m_rstd2)

        for mt in range(32):
            r = mt % 4
            w = [(sV, cast_v)] if mt == 0 else []
            if pP_free[r] is not None:
                w.append(pP_free[r])
            for kc in range(8):
                p.op("tensor", lambda e, mt=mt, kc=kc, r=r: e.matmul(
                    pP3[:, r * 512:r * 512 + 256],
                    wfc_sb[:, kc * 4096 + mt * 128: kc * 4096 + (mt + 1) * 128],
                    hbf[:, kc * SL3:(kc + 1) * SL3],
                    start=(kc == 0), stop=(kc == 7)), inc=(sT if kc == 7 else None),
                    waits=(w + [(dG, w_loaded)]) if kc == 0 else [])
            g = p.cnt[sT]
            n = p.op("scalar", lambda e, mt=mt, r=r: e.activation(
                r_sb[:, mt * SL3:(mt + 1) * SL3], pP3[:, r * 512:r * 512 + 256], AF.Relu),
                inc=sS, waits=[(sT, g)])
            pP_free[r] = (sS, n)
        hbf_free_t = p.cnt[sT]
        relu_s = p.cnt[sS]
        for q8 in range(8):
            p.op("vector", lambda e, q8=q8: e.tensor_tensor(
                r_sb[:, q8 * 1024:(q8 + 1) * 1024], r_sb[:, q8 * 1024:(q8 + 1) * 1024],
                r_sb[:, q8 * 1024:(q8 + 1) * 1024], ALU.mult), inc=sV,
                waits=[(sS, relu_s)] if q8 == 0 else [])
        sq_v = p.cnt[sV]

        ao_evs = []
        for mt in range(8):
            r = mt % 4
            w = [(sV, sq_v)] if mt == 0 else []
            if pP_free[r] is not None:
                w.append(pP_free[r])
            for kc in range(32):
                p.op("tensor", lambda e, mt=mt, kc=kc, r=r: e.matmul(
                    pP3[:, r * 512:r * 512 + 256],
                    wproj_sb[:, kc * 1024 + mt * 128: kc * 1024 + (mt + 1) * 128],
                    r_sb[:, kc * SL3:(kc + 1) * SL3],
                    start=(kc == 0), stop=(kc == 31)), inc=(sT if kc == 31 else None),
                    waits=w if kc == 0 else [])
            g = p.cnt[sT]
            nn = p.op("vector", lambda e, mt=mt, r=r: e.tensor_tensor(
                ao_sb[:, mt * SL3:(mt + 1) * SL3], pP3[:, r * 512:r * 512 + 256],
                maps3[:, 2 * SL3:3 * SL3], ALU.mult), inc=sV,
                waits=[(sT, g), (sV, m_rstd2), (sS, ao_sq_s)] if mt == 0 else [(sT, g)])
            nn = p.op("vector", lambda e, mt=mt: e.tensor_add(
                ao_sb[:, mt * SL3:(mt + 1) * SL3], ao_sb[:, mt * SL3:(mt + 1) * SL3],
                h_sl[:, mt * SL3:(mt + 1) * SL3]), inc=sV)
            ao_evs.append((sV, nn))
            pP_free[r] = (sV, nn)
        r_free_t = p.cnt[sT]
        ao_v = p.cnt[sV]
        ao_free = ao_v

        for ch in range(8):
            p.op("vector", lambda e, ch=ch: e.tensor_copy(
                r_sb[:, ch * SL3:(ch + 1) * SL3], ao_sb[:, ch * SL3:(ch + 1) * SL3]), inc=sV,
                waits=[(sV, ao_v), (sT, r_free_t)] if ch == 0 else [])
            p.op("scalar", lambda e, ch=ch: e.activation(
                r_sb[:, 2048 + ch * SL3:2048 + (ch + 1) * SL3],
                ao_sb[:, ch * SL3:(ch + 1) * SL3], AF.Square), inc=sS,
                waits=[(sV, ao_v), (sT, r_free_t)] if ch == 0 else [])
        c2v = p.cnt[sV]; c2s = p.cnt[sS]
        for ch in range(8):
            p.op("tensor", lambda e, ch=ch: e.matmul(
                pstA[0:1, :], E.ones128b[:, 0:1], r_sb[:, ch * SL3:(ch + 1) * SL3],
                start=(ch == 0), stop=(ch == 7)), inc=(sT if ch == 7 else None),
                waits=[(sV, c2v), (sV, stats_evA)] if ch == 0 else [])
        for ch in range(8):
            p.op("tensor", lambda e, ch=ch: e.matmul(
                pstB[0:1, :], E.ones128b[:, 0:1], r_sb[:, 2048 + ch * SL3:2048 + (ch + 1) * SL3],
                start=(ch == 0), stop=(ch == 7)), inc=(sT if ch == 7 else None),
                waits=[(sS, c2s)] if ch == 0 else [])
        g2 = p.cnt[sT]
        aocast_free_t = g2
        r_free_t = g2
        ao_sq_s = c2s
        hsl_free_s = c2s
        p.op("vector", lambda e: e.tensor_scalar(mu_r, pstA[0:1, :], 1.0 / 1024, None, ALU.mult),
             inc=sV, waits=[(sT, g2)])
        p.op("vector", lambda e: e.tensor_scalar(var_r, pstB[0:1, :], 1.0 / 1024, None, ALU.mult),
             inc=sV)
        stats_free = p.cnt[sV]
        ns = p.op("scalar", lambda e: e.activation(sc_r, mu_r, AF.Square), inc=sS,
                  waits=[(sV, stats_free)])
        nv = p.op("vector", lambda e: e.tensor_sub(var_r, var_r, sc_r), inc=sV,
                  waits=[(sS, ns)])
        ns = p.op("scalar", lambda e: e.activation(sc_r, var_r, AF.Ln, bias=E.epsb[0:1, 0:1]),
                  inc=sS, waits=[(sV, nv)])
        ns = p.op("scalar", lambda e: e.activation(r1_r, sc_r, AF.Exp, scale=-0.5), inc=sS)
        nv = p.op("vector", lambda e: e.tensor_mul(mur_r, mu_r, r1_r), inc=sV, waits=[(sS, ns)])
        rows_v = p.cnt[sV]
        prev_ev = None
        for i, row in enumerate([r1_r, mur_r]):
            w = [(sV, rows_v), pQ_free] if i == 0 else [(sV, prev_ev)]
            n = p.op("tensor", lambda e, row=row: e.matmul(
                pQ3[:, 0:256], E.ones1f[0:1, :], row, start=True, stop=True), inc=sT, waits=w)
            prev_ev = p.op("vector", lambda e, i=i: e.tensor_copy(
                maps3[:, i * SL3:(i + 1) * SL3], pQ3[:, 0:256]), inc=sV, waits=[(sT, n)])
        maps_v = p.cnt[sV]
        rows_free_t = p.cnt[sT]
        pQ_free = (sV, maps_v)
        maps_free_new = []
        for ch in range(8):
            p.op("vector", lambda e, ch=ch: e.tensor_tensor(
                x1_sb[:, ch * SL3:(ch + 1) * SL3], ao_sb[:, ch * SL3:(ch + 1) * SL3],
                maps3[:, 0:SL3], ALU.mult), inc=sV,
                waits=[(sV, maps_v), (dG, x1_stored)] if ch == 0 else [])
            p.op("vector", lambda e, ch=ch: e.tensor_sub(
                x1_sb[:, ch * SL3:(ch + 1) * SL3], x1_sb[:, ch * SL3:(ch + 1) * SL3],
                maps3[:, SL3:2 * SL3]), inc=sV)
            p.op("vector", lambda e, ch=ch: e.tensor_add(
                x1_sb[:, ch * SL3:(ch + 1) * SL3], x1_sb[:, ch * SL3:(ch + 1) * SL3],
                x_sl[:, ch * SL3:(ch + 1) * SL3]), inc=sV,
                waits=[(dS, ld_x)] if ch == 0 else [])
        x1_v = p.cnt[sV]
        maps_free = x1_v
        x1_stored = p.op("gpsimd", lambda e, s=s: e.dma_start(
            out=E.sp_x1[:, s * SL3:(s + 1) * SL3].rearrange("(g p) t -> p g t", p=128),
            in_=x1_sb.rearrange("p (g t) -> p g t", t=SL3)), inc=dG, inck=16,
            waits=[(sV, x1_v)])

    E.p3_done_dG = p.cnt[dG]
    if nphases == 3:
        p.op("sync", lambda e: e.nop(), waits=[(dG, E.p3_done_dG)])
    px.close()


def _phase4(E, nphases):
    nc, p = E.nc, E.p
    p.fence()
    dS, dG, sT, sV, sS, sG = E.dS, E.dG, E.sT, E.sV, E.sS, E.sG
    px = ExitStack()
    sbl = lambda n, s, d: px.enter_context(nc.sbuf_tensor(n, s, d))
    xn2 = sbl("xn2", [128, 8 * 4096], BF16)
    xn2lo = sbl("xn2lo", [128, 8 * 4096], BF16)

    pxa = ExitStack()
    sba = lambda n, s, d: pxa.enter_context(nc.sbuf_tensor(n, s, d))
    x1_sl = sba("x1_sl", [128, 8 * SL], F32)
    x1bf = sba("x1bf", [128, 8 * SL], BF16)
    x1sq = sba("x1sq", [128, 8 * SL], BF16)
    rows4 = sba("rows4", [1, 5 * SL], F32)
    maps4 = sba("maps4", [128, 2 * SL], F32)
    tch4 = sba("tch4", [128, SL], F32)
    xn2f = sba("xn2f", [128, SL], F32)
    p4a = pxa.enter_context(nc.psum_tensor("p4a", [128, 512], F32))
    pstA = pxa.enter_context(nc.psum_tensor("pstA4", [1, 512], F32))
    pstB = pxa.enter_context(nc.psum_tensor("pstB4", [1, 512], F32))
    mu_r = rows4[0:1, 0:SL]; var_r = rows4[0:1, SL:2 * SL]
    r1_r = rows4[0:1, 2 * SL:3 * SL]; mur_r = rows4[0:1, 3 * SL:4 * SL]
    sc_r = rows4[0:1, 4 * SL:5 * SL]

    cast_free_t = 0
    stats_free = 0
    rows_free_t = 0
    maps_free = 0
    pm_free = None
    for s in range(NSL):
        ld = p.op("sync", lambda e, s=s: e.dma_start(
            out=x1_sl.rearrange("p (g t) -> p g t", t=SL),
            in_=E.sp_x1[:, s * SL:(s + 1) * SL].rearrange("(g p) t -> p g t", p=128)),
            inc=dS, inck=16, waits=[(dG, E.p3_done_dG), (sV, maps_free)])
        for ch in range(8):
            p.op("vector", lambda e, ch=ch: e.tensor_copy(
                x1bf[:, ch * SL:(ch + 1) * SL], x1_sl[:, ch * SL:(ch + 1) * SL]), inc=sV,
                waits=[(dS, ld), (sT, cast_free_t)])
            p.op("scalar", lambda e, ch=ch: e.activation(
                x1sq[:, ch * SL:(ch + 1) * SL], x1_sl[:, ch * SL:(ch + 1) * SL], AF.Square),
                inc=sS, waits=[(dS, ld), (sT, cast_free_t)])
        cast_v = p.cnt[sV]; cast_s = p.cnt[sS]
        for ch in range(8):
            p.op("tensor", lambda e, ch=ch: e.matmul(
                pstA[0:1, :], E.ones128b[:, 0:1], x1bf[:, ch * SL:(ch + 1) * SL],
                start=(ch == 0), stop=(ch == 7)), inc=(sT if ch == 7 else None),
                waits=[(sV, cast_v), (sV, stats_free)] if ch == 0 else [])
        for ch in range(8):
            p.op("tensor", lambda e, ch=ch: e.matmul(
                pstB[0:1, :], E.ones128b[:, 0:1], x1sq[:, ch * SL:(ch + 1) * SL],
                start=(ch == 0), stop=(ch == 7)), inc=(sT if ch == 7 else None),
                waits=[(sS, cast_s)] if ch == 0 else [])
        g1 = p.cnt[sT]
        cast_free_t = g1
        p.op("vector", lambda e: e.tensor_scalar(mu_r, pstA[0:1, :], 1.0 / 1024, None, ALU.mult),
             inc=sV, waits=[(sT, g1), (sT, rows_free_t)])
        p.op("vector", lambda e: e.tensor_scalar(var_r, pstB[0:1, :], 1.0 / 1024, None, ALU.mult),
             inc=sV)
        stats_free = p.cnt[sV]
        ns = p.op("scalar", lambda e: e.activation(sc_r, mu_r, AF.Square), inc=sS,
                  waits=[(sV, stats_free), (sT, rows_free_t)])
        nv = p.op("vector", lambda e: e.tensor_sub(var_r, var_r, sc_r), inc=sV, waits=[(sS, ns)])
        ns = p.op("scalar", lambda e: e.activation(sc_r, var_r, AF.Ln, bias=E.epsb[0:1, 0:1]),
                  inc=sS, waits=[(sV, nv)])
        ns = p.op("scalar", lambda e: e.activation(r1_r, sc_r, AF.Exp, scale=-0.5), inc=sS)
        rows_v = p.cnt[sV]; rows_s = p.cnt[sS]
        prev_ev = None
        for i, row in enumerate([mu_r, r1_r]):
            w = [(sS, rows_s), (sV, rows_v)]
            if i == 0:
                w += [pm_free, (sV, maps_free)]
            else:
                w.append((sV, prev_ev))
            n = p.op("tensor", lambda e, row=row: e.matmul(
                p4a[:, 0:512], E.ones1f[0:1, :], row, start=True, stop=True), inc=sT, waits=w)
            prev_ev = p.op("vector", lambda e, i=i: e.tensor_copy(
                maps4[:, i * SL:(i + 1) * SL], p4a[:, 0:512]), inc=sV, waits=[(sT, n)])
        maps_v = p.cnt[sV]
        rows_free_t = p.cnt[sT]
        pm_free = (sV, maps_v)
        for ch in range(8):
            p.op("vector", lambda e, ch=ch: e.tensor_sub(
                tch4[:, :], x1_sl[:, ch * SL:(ch + 1) * SL], maps4[:, 0:SL]), inc=sV,
                waits=[(sV, maps_v)] if ch == 0 else [])
            p.op("vector", lambda e, ch=ch: e.tensor_mul(
                xn2f[:, :], tch4[:, :], maps4[:, SL:2 * SL]), inc=sV)
            p.op("vector", lambda e, ch=ch, s=s: e.tensor_copy(
                xn2[:, ch * 4096 + s * SL: ch * 4096 + (s + 1) * SL], xn2f[:, :]), inc=sV)
            p.op("vector", lambda e, ch=ch, s=s: e.tensor_sub(
                xn2lo[:, ch * 4096 + s * SL: ch * 4096 + (s + 1) * SL], xn2f[:, :],
                xn2[:, ch * 4096 + s * SL: ch * 4096 + (s + 1) * SL]), inc=sV)
        maps_free = p.cnt[sV]
    xn2_done_v = p.cnt[sV]
    pxa.close()
    p.fence()

    w1_sb = sbl("w1_sb", [128, 8 * 1024], BF16)
    w2_sb = sbl("w2_sb", [128, 8 * 1024], BF16)
    nwT_sb = sbl("nwT_sb", [128, 8 * 15], BF16)
    nwTlo_sb = sbl("nwTlo_sb", [128, 8 * 15], BF16)
    nwTst = sbl("nwTst", [128, 8 * 15], F32)
    nb_sb = sbl("nb_sb", [15, 1], F32)
    cbit = sbl("cbit", [16, 512], BF16)
    b1c_sb = sbl("b1c_sb", [128, 8], F32)
    b2_sb = sbl("b2_sb", [16, 1024], BF16)
    hid = sbl("hid", [128, 8 * 512], BF16)
    out_sb = sbl("out_sb", [128, 8 * 512], BF16)
    selT = sbl("selT", [16, 512], BF16)
    LEAF = sbl("LEAF", [128, 512], BF16)
    pF = px.enter_context(nc.psum_tensor("pF", [128, 2048], F32))
    pG = px.enter_context(nc.psum_tensor("pG", [128, 1536], F32))

    wload_free = [0, 0, 0]
    nwv_prev = 0
    for bi, br in enumerate("qkv"):
        ff = E.fff[br]
        wl = []
        wld = [(sT, wload_free[0])]
        for kc in range(8):
            wl.append(p.op("gpsimd", lambda e, kc=kc, ff=ff: e.dma_start(
                out=w1_sb[:, kc * 1024:(kc + 1) * 1024],
                in_=ff["w1"][kc * 128:(kc + 1) * 128, :]), inc=dG, inck=16, waits=wld))
            wld = []
        for kc in range(8):
            wl.append(p.op("gpsimd", lambda e, kc=kc, ff=ff: e.dma_start(
                out=w2_sb[:, kc * 1024:(kc + 1) * 1024],
                in_=ff["w2"][kc * 128:(kc + 1) * 128, :]), inc=dG, inck=16))
        for kc in range(8):
            wl.append(p.op("gpsimd", lambda e, kc=kc, ff=ff: e.dma_start(
                out=nwTst[:, kc * 15:(kc + 1) * 15],
                in_=ff["nwT"][kc * 128:(kc + 1) * 128, :]), inc=dG, inck=16,
                waits=[(sV, nwv_prev)] if kc == 0 else []))
        wl.append(p.op("gpsimd", lambda e, ff=ff: e.dma_start(out=nb_sb[:, :], in_=ff["nb"][:, :]),
                       inc=dG, inck=16))
        wl.append(p.op("gpsimd", lambda e, ff=ff: e.dma_start(out=b1c_sb[:, :], in_=ff["b1c"][:, :]),
                       inc=dG, inck=16))
        wl.append(p.op("gpsimd", lambda e, ff=ff: e.dma_start(out=b2_sb[:, :], in_=ff["b2"][:, :]),
                       inc=dG, inck=16))
        w_ld = p.cnt[dG]
        p.op("vector", lambda e: e.tensor_copy(nwT_sb[:, :], nwTst[:, :]), inc=sV,
             waits=[(dG, w_ld), (sT, wload_free[0])])
        nwv = p.op("vector", lambda e: e.tensor_sub(nwTlo_sb[:, :], nwTst[:, :], nwT_sb[:, :]),
                   inc=sV)
        nwv_prev = nwv

        pF_free = [None] * 4
        pG_free = [None] * 3
        sel_free_t = 0
        leaf_free_v = 0
        hid_free_t = 0
        out_stored = 0
        for s in range(NSL):
            w = [(sV, xn2_done_v), (dG, w_ld), (sV, nwv)]
            if pF_free[0] is not None:
                w.append(pF_free[0])
            first = True
            for wmat, xoff in ((nwT_sb, 0), (nwT_sb, 1), (nwTlo_sb, 0)):
                xsrc = xn2 if xoff == 0 else xn2lo
                for kc in range(8):
                    last = (not first) and (wmat is nwTlo_sb) and kc == 7
                    p.op("tensor", lambda e, s=s, kc=kc, wmat=wmat, xsrc=xsrc, first=first, last=last: e.matmul(
                        pF[0:15, 0:512],
                        wmat[:, kc * 15:(kc + 1) * 15],
                        xsrc[:, kc * 4096 + s * 512: kc * 4096 + (s + 1) * 512],
                        start=first, stop=last), inc=(sT if last else None),
                        waits=w if first else [])
                    first = False
            g = p.cnt[sT]
            tapw = [(dG, getattr(E, "p4_tap", 0))] if (E.dbg and br == "q" and s == 1) else []
            ncb = p.op("vector", lambda e: e.tensor_scalar(
                cbit[0:15, :], pF[0:15, 0:512], nb_sb[0:15, 0:1], 0.0, ALU.add, ALU.is_ge),
                inc=sV, waits=[(sT, g)] + tapw)
            pF_free[0] = (sV, ncb)
            w = [(sV, ncb)]
            if pF_free[1] is not None:
                w.append(pF_free[1])
            p.op("tensor", lambda e: e.matmul(
                pF[0:16, 512:1024], E.Abit[0:15, :], cbit[0:15, :],
                start=True, stop=False), inc=None, waits=w)
            n = p.op("tensor", lambda e: e.matmul(
                pF[0:16, 512:1024], E.negrow[0:1, :], E.ones512b[0:1, :],
                start=False, stop=True), inc=sT)
            nst = p.op("vector", lambda e: e.tensor_scalar(
                selT[0:16, :], pF[0:16, 512:1024], 4.0, None, ALU.is_equal), inc=sV,
                waits=[(sT, n)])
            pF_free[1] = (sV, nst)
            sel_free_t = 0
            w = [(sV, nst), (sV, leaf_free_v)]
            if pG_free[0] is not None:
                w.append(pG_free[0])
            n = p.op("tensor", lambda e: e.matmul(
                pG[:, 0:512], E.IDX16[:, :], selT[0:16, :], start=True, stop=True), inc=sT,
                waits=w)
            nlf = p.op("vector", lambda e: e.tensor_copy(LEAF[:, :], pG[:, 0:512]), inc=sV,
                       waits=[(sT, n)])
            pG_free[0] = (sV, nlf)
            mask_v = []
            for hk in range(8):
                r = hk % 3
                w = []
                if hk == 0:
                    w = [(sV, xn2_done_v)]
                if pG_free[r] is not None:
                    w.append(pG_free[r])
                for kc in range(8):
                    p.op("tensor", lambda e, s=s, hk=hk, kc=kc, r=r: e.matmul(
                        pG[:, r * 512:(r + 1) * 512],
                        w1_sb[:, kc * 1024 + hk * 128: kc * 1024 + (hk + 1) * 128],
                        xn2[:, kc * 4096 + s * 512: kc * 4096 + (s + 1) * 512],
                        start=(kc == 0), stop=(kc == 7)), inc=(sT if kc == 7 else None),
                        waits=w if kc == 0 else [])
                g = p.cnt[sT]
                ns = p.op("scalar", lambda e, hk=hk, r=r: e.activation(
                    hid[:, hk * 512:(hk + 1) * 512], pG[:, r * 512:(r + 1) * 512], AF.Relu,
                    bias=b1c_sb[:, hk:hk + 1]), inc=sS,
                    waits=[(sT, g), (sT, hid_free_t)] if hk == 0 else [(sT, g)])
                nm = p.op("vector", lambda e, hk=hk: e.scalar_tensor_tensor(
                    hid[:, hk * 512:(hk + 1) * 512], LEAF[:, :], E.leafof[:, hk:hk + 1],
                    hid[:, hk * 512:(hk + 1) * 512], ALU.is_equal, ALU.mult), inc=sV,
                    waits=[(sS, ns), (sV, nlf)])
                mask_v.append(nm)
                pG_free[r] = (sV, nm)
            hid_v = p.cnt[sV]
            leaf_free_v = hid_v
            if br in "qk":
                spill = E.sp_q if br == "q" else E.sp_k
                for mt in range(8):
                    r = mt % 4
                    w = [(sV, hid_v)] if mt == 0 else []
                    if pF_free[r] is not None:
                        w.append(pF_free[r])
                    for kc in range(8):
                        p.op("tensor", lambda e, mt=mt, kc=kc, r=r: e.matmul(
                            pF[:, r * 512:(r + 1) * 512],
                            w2_sb[:, kc * 1024 + mt * 128: kc * 1024 + (mt + 1) * 128],
                            hid[:, kc * 512:(kc + 1) * 512],
                            start=(kc == 0), stop=False), inc=None,
                            waits=w if kc == 0 else [])
                    p.op("tensor", lambda e, mt=mt, r=r: e.matmul(
                        pF[:, r * 512:(r + 1) * 512], b2_sb[:, mt * 128:(mt + 1) * 128],
                        selT[0:16, :], start=False, stop=True), inc=sT)
                    g = p.cnt[sT]
                    nn = p.op("vector", lambda e, mt=mt, r=r: e.tensor_copy(
                        out_sb[:, mt * 512:(mt + 1) * 512], pF[:, r * 512:(r + 1) * 512]),
                        inc=sV, waits=[(sT, g), (dG, out_stored)] if mt == 0 else [(sT, g)])
                    pF_free[r] = (sV, nn)
                ev = p.cnt[sV]
                hid_free_t = p.cnt[sT]
                out_stored = p.op("gpsimd", lambda e, s=s, spill=spill: e.dma_start(
                    out=spill[:, s * 512:(s + 1) * 512].rearrange("(g p) t -> p g t", p=128),
                    in_=out_sb.rearrange("p (g t) -> p g t", t=512)), inc=dG, inck=16,
                    waits=[(sV, ev)])
            else:
                for tt in range(4):
                    for dv in range(2):
                        r = (tt * 2 + dv) % 4
                        w = [(sV, hid_v)] if (tt == 0 and dv == 0) else []
                        if pF_free[r] is not None:
                            w.append(pF_free[r])
                        for kc in range(8):
                            p.op("tensor", lambda e, tt=tt, dv=dv, kc=kc, r=r: e.matmul(
                                pF[:, r * 512:(r + 1) * 512],
                                hid[:, kc * 512 + tt * 128: kc * 512 + (tt + 1) * 128],
                                w2_sb[:, kc * 1024 + dv * 512: kc * 1024 + (dv + 1) * 512],
                                start=(kc == 0), stop=False), inc=None,
                                waits=w if kc == 0 else [])
                        p.op("tensor", lambda e, tt=tt, dv=dv, r=r: e.matmul(
                            pF[:, r * 512:(r + 1) * 512], selT[0:16, tt * 128:(tt + 1) * 128],
                            b2_sb[:, dv * 512:(dv + 1) * 512], start=False, stop=True), inc=sT)
                        g = p.cnt[sT]
                        nn = p.op("vector", lambda e, tt=tt, dv=dv, r=r: e.tensor_copy(
                            out_sb[:, tt * 1024 + dv * 512: tt * 1024 + (dv + 1) * 512],
                            pF[:, r * 512:(r + 1) * 512]), inc=sV,
                            waits=[(sT, g), (dG, out_stored)] if (tt == 0 and dv == 0) else [(sT, g)])
                        pF_free[r] = (sV, nn)
                ev = p.cnt[sV]
                hid_free_t = p.cnt[sT]
                out_stored = p.op("gpsimd", lambda e, s=s: e.dma_start(
                    out=E.sp_v[s * 512:(s + 1) * 512, :].rearrange("(t p) d -> p t d", p=128),
                    in_=out_sb.rearrange("p (t d) -> p t d", d=1024)), inc=dG, inck=16,
                    waits=[(sV, ev)])
            if E.dbg and br == "q" and s == 0:
                p.op("gpsimd", lambda e: e.dma_start(
                    out=E.t_xn2.rearrange("p (c t) -> p c t", t=512),
                    in_=xn2.rearrange("p (c t) -> p c t", t=4096)[:, :, 0:512]),
                    inc=dG, inck=16, waits=[(sV, xn2_done_v)])
                p.op("gpsimd", lambda e: e.dma_start(out=E.t_selT[:, :], in_=selT[0:16, :]),
                     inc=dG, inck=16, waits=[(sV, nst)])
                p.op("gpsimd", lambda e: e.dma_start(out=E.t_leaf[:, :], in_=LEAF[:, :]),
                     inc=dG, inck=16, waits=[(sV, nlf)])
                p.op("gpsimd", lambda e: e.dma_start(out=E.t_hid[:, :], in_=hid[:, :]),
                     inc=dG, inck=16, waits=[(sV, hid_v)])
                p.op("gpsimd", lambda e: e.dma_start(out=E.t_fout[:, :], in_=out_sb[:, :]),
                     inc=dG, inck=16, waits=[(sV, ev)])
                E.p4_tap = p.cnt[dG]
        wload_free[0] = p.cnt[sT]

    E.p4_done_dG = p.cnt[dG]
    if nphases == 4:
        p.op("sync", lambda e: e.nop(), waits=[(dG, E.p4_done_dG)])
    px.close()


def _phase5(E):
    nc, p = E.nc, E.p
    p.fence()
    dS, dG, sT, sV, sS, sG = E.dS, E.dG, E.sT, E.sV, E.sS, E.sG
    px = ExitStack()
    sbl = lambda n, s, d: px.enter_context(nc.sbuf_tensor(n, s, d))
    k_sb = sbl("k_sb", [128, 8 * 4096], BF16)
    v_sb = sbl("v_sb", [128, 32 * 1024], BF16)
    q_sb = sbl("q_sb", [128, 8 * 256], BF16)
    A_sb = sbl("A_sb", [128, 4096], F32)
    P_sb = sbl("P_sb", [128, 4096], BF16)
    PT_sb = sbl("PT_sb", [128, 32 * 256], BF16)
    u_sb = sbl("u_sb", [128, 8 * 512], F32)
    rows5 = sbl("rows5", [1, 4 * 512], F32)
    maps5 = sbl("maps5", [128, 2 * 512], F32)
    o_sb = sbl("o_sb", [128, 2 * 512], F32)
    dcol = sbl("dcol", [128, 4], F32)
    dbf = sbl("dbf", [128, 4], BF16)
    mcol = sbl("mcol", [128, 2], F32)
    pX = px.enter_context(nc.psum_tensor("pX", [128, 2048], F32))
    pY = px.enter_context(nc.psum_tensor("pY", [128, 2048], F32))

    sums_r = rows5[0:1, 0:512]; var_r = rows5[0:1, 512:1024]
    mur_r = rows5[0:1, 512:1024]
    drow_r = rows5[0:1, 1024:1536]
    rstd_r = rows5[0:1, 1024:1536]
    sc_r = rows5[0:1, 1536:2048]

    kv = []
    for kc in range(8):
        kv.append(p.op("sync", lambda e, kc=kc: e.dma_start(
            out=k_sb[:, kc * 4096:(kc + 1) * 4096],
            in_=E.sp_k[kc * 128:(kc + 1) * 128, :]), inc=dS, inck=16,
            waits=[(dG, E.p4_done_dG)]))
    for i in range(32):
        kv.append(p.op("sync", lambda e, i=i: e.dma_start(
            out=v_sb[:, i * 1024:(i + 1) * 1024],
            in_=E.sp_v[i * 128:(i + 1) * 128, :]), inc=dS, inck=16))
    kv_ld = p.cnt[dS]

    pX_free = [None] * 4
    pY_free = [None] * 4
    A_guard_dG = 0
    pst_free = None
    q_free_t = 0
    A_free_s = 0
    P_free_d = 0
    PT_free_t = 0
    u_free_v = 0
    P_guard_t = 0
    rows_free_t = 0
    maps_free_v = 0
    o_stored = [0, 0]
    mcol_free_s = 0
    dcol_free_v = 0

    for qb in range(NSL):
        exp_s = [0, 0, 0, 0]
        tr_d = [0, 0, 0, 0]
        pv_t = [0, 0]
        for half in range(2):
            ld_q = p.op("sync", lambda e, qb=qb, half=half: e.dma_start(
                out=q_sb.rearrange("p (g t) -> p g t", t=256),
                in_=E.sp_q[:, qb * 512 + half * 256:qb * 512 + (half + 1) * 256]
                    .rearrange("(g p) t -> p g t", p=128)),
                inc=dS, inck=16, waits=[(sT, q_free_t)])
            for sub in range(2):
                qt = half * 2 + sub
                for ks in range(8):
                    r = ks % 4
                    w = [(dS, ld_q), (dS, kv_ld)]
                    if qt == 0 and ks == 0:
                        w.append((sS, A_free_s))
                    if pX_free[r] is not None:
                        w.append(pX_free[r])
                    for kc in range(8):
                        p.op("tensor", lambda e, qb=qb, sub=sub, ks=ks, kc=kc, r=r: e.matmul(
                            pX[:, r * 512:(r + 1) * 512],
                            q_sb[:, kc * 256 + sub * 128:kc * 256 + (sub + 1) * 128],
                            k_sb[:, kc * 4096 + ks * 512:kc * 4096 + (ks + 1) * 512],
                            start=(kc == 0), stop=(kc == 7)), inc=(sT if kc == 7 else None),
                            waits=w if kc == 0 else [])
                    g = p.cnt[sT]
                    nn = p.op("vector", lambda e, ks=ks, r=r: e.tensor_copy(
                        A_sb[:, ks * 512:(ks + 1) * 512], pX[:, r * 512:(r + 1) * 512]),
                        inc=sV, waits=[(sT, g), (sS, A_free_s), (dG, A_guard_dG)]
                        if ks == 0 else [(sT, g)])
                    pX_free[r] = (sV, nn)
                a_v = p.cnt[sV]
                nn = p.op("vector", lambda e: e.reduce_max(
                    mcol[:, 0:1], A_sb[:, :], axis=AXX, negate=True), inc=sV,
                    waits=[(sS, mcol_free_s)])
                p.op("vector", lambda e: e.tensor_copy(mcol[:, 1:2], mcol[:, 0:1]), inc=sV)
                p.op("vector", lambda e: e.tensor_copy(mcol[:, 1:2], mcol[:, 0:1]), inc=sV)
                nn = p.op("vector", lambda e: e.tensor_scalar(
                    A_sb[:, :], A_sb[:, :], mcol[:, 0:1], SCALE_ATT, ALU.add, ALU.mult),
                    inc=sV)
                ns = p.op("scalar", lambda e, qt=qt: e.activation(
                    P_sb[:, :], A_sb[:, :], AF.Exp,
                    accum_out=dcol[:, qt:qt + 1]), inc=sS,
                    waits=[(sV, nn), (dS, P_free_d), (sV, dcol_free_v), (sT, P_guard_t)])
                exp_s[qt] = ns
                A_free_s = ns
                mcol_free_s = ns
                if E.dbg and qb == 0 and qt == 0:
                    A_guard_dG = p.op("gpsimd", lambda e: e.dma_start(
                        out=E.t_A5[:, :], in_=A_sb[:, :]), inc=dG, inck=16,
                        waits=[(sV, a_v), (sS, ns)])
                tr = 0
                for kc in range(32):
                    w = [(sS, ns)]
                    if kc == 0:
                        w.append((sT, PT_free_t))
                    tr = p.op("sync", lambda e, kc=kc, sub=sub: e.dma_start_transpose(
                        PT_sb[:, kc * 256 + sub * 128: kc * 256 + sub * 128 + 128],
                        P_sb[:, kc * 128:(kc + 1) * 128]), inc=dS, inck=16, waits=w)
                tr_d[qt] = tr
                P_free_d = tr
            for dv in range(8):
                r = dv % 4
                w = [(dS, tr_d[half * 2]), (dS, tr_d[half * 2 + 1]), (dS, kv_ld)]
                if pY_free[r] is not None:
                    w.append(pY_free[r])
                for kc in range(32):
                    p.op("tensor", lambda e, dv=dv, kc=kc, r=r: e.matmul(
                        pY[:, r * 512:r * 512 + 256],
                        v_sb[:, kc * 1024 + dv * 128: kc * 1024 + (dv + 1) * 128],
                        PT_sb[:, kc * 256:(kc + 1) * 256],
                        start=(kc == 0), stop=(kc == 31)), inc=(sT if kc == 31 else None),
                        waits=w if kc == 0 else [])
                g = p.cnt[sT]
                nn = p.op("vector", lambda e, dv=dv, half=half, r=r: e.tensor_copy(
                    u_sb[:, dv * 512 + half * 256: dv * 512 + half * 256 + 256],
                    pY[:, r * 512:r * 512 + 256]), inc=sV,
                    waits=[(sT, g), (sV, u_free_v)] if dv == 0 else [(sT, g)])
                pY_free[r] = (sV, nn)
            pv_t[half] = p.cnt[sT]
            PT_free_t = pv_t[half]
            q_free_t = p.cnt[sT]
        u_v = p.cnt[sV]

        for ch in range(8):
            p.op("scalar", lambda e, ch=ch: e.activation(
                P_sb[:, ch * 512:(ch + 1) * 512], u_sb[:, ch * 512:(ch + 1) * 512], AF.Square),
                inc=sS, waits=[(sV, u_v), (dS, P_free_d)] if ch == 0 else [])
        usq_s = p.cnt[sS]
        for ch in range(8):
            w = []
            if ch == 0:
                w = [(sS, usq_s)]
                if pst_free is not None:
                    w.append(pst_free)
            p.op("tensor", lambda e, ch=ch: e.matmul(
                pX[0:1, 0:512], E.ones128b[:, 0:1], P_sb[:, ch * 512:(ch + 1) * 512],
                start=(ch == 0), stop=(ch == 7)), inc=(sT if ch == 7 else None), waits=w)
        for ch in range(8):
            p.op("tensor", lambda e, ch=ch: e.matmul(
                pX[0:1, 512:1024], E.ones128f[:, 0:1], u_sb[:, ch * 512:(ch + 1) * 512],
                start=(ch == 0), stop=(ch == 7)), inc=(sT if ch == 7 else None),
                waits=[(sV, u_v)] if ch == 0 else [])
        g1 = p.cnt[sT]
        P_guard_t = g1
        nv = p.op("vector", lambda e: e.tensor_copy(dbf[:, :], dcol[:, :]), inc=sV,
                  waits=[(sS, exp_s[3])])
        dcol_free_v = nv
        for qt in range(4):
            p.op("tensor", lambda e, qt=qt: e.matmul(
                pX[0:1, 1024 + qt * 128: 1024 + (qt + 1) * 128], dbf[:, qt:qt + 1],
                E.I128b[:, :], start=True, stop=True), inc=sT,
                waits=[(sV, nv)] if qt == 0 else [])
        g2 = p.cnt[sT]
        p.op("vector", lambda e: e.tensor_scalar(var_r, pX[0:1, 0:512], 1.0 / 1024, None,
                                                 ALU.mult), inc=sV,
             waits=[(sT, g2), (sT, rows_free_t)])
        p.op("vector", lambda e: e.tensor_scalar(sums_r, pX[0:1, 512:1024], 1.0 / 1024, None,
                                                 ALU.mult), inc=sV)
        p.op("vector", lambda e: e.tensor_copy(drow_r, pX[0:1, 1024:1536]), inc=sV)
        st_ev = p.cnt[sV]
        pst_free = (sV, st_ev)
        ns = p.op("scalar", lambda e: e.activation(sc_r, sums_r, AF.Square), inc=sS,
                  waits=[(sV, st_ev)])
        nv = p.op("vector", lambda e: e.tensor_sub(var_r, var_r, sc_r), inc=sV, waits=[(sS, ns)])
        ns = p.op("scalar", lambda e: e.activation(sc_r, drow_r, AF.Square), inc=sS,
                  waits=[(sV, nv)])
        nv = p.op("vector", lambda e: e.scalar_tensor_tensor(
            sc_r, sc_r, EPS_LN, var_r, ALU.mult, ALU.add), inc=sV, waits=[(sS, ns)])
        ns = p.op("scalar", lambda e: e.activation(sc_r, sc_r, AF.Ln), inc=sS, waits=[(sV, nv)])
        ns = p.op("scalar", lambda e: e.activation(rstd_r, sc_r, AF.Exp, scale=-0.5), inc=sS)
        nv = p.op("vector", lambda e: e.tensor_mul(mur_r, sums_r, rstd_r), inc=sV,
                  waits=[(sS, ns)])
        rows_v = p.cnt[sV]
        prev_ev = None
        for i, row in enumerate([rstd_r, mur_r]):
            w = [(sV, rows_v)]
            if i == 0:
                w += [(sV, maps_free_v)]
                if pX_free[3] is not None:
                    w.append(pX_free[3])
            else:
                w.append((sV, prev_ev))
            n = p.op("tensor", lambda e, row=row: e.matmul(
                pX[:, 1536:2048], E.ones1f[0:1, :], row, start=True, stop=True), inc=sT, waits=w)
            prev_ev = p.op("vector", lambda e, i=i: e.tensor_copy(
                maps5[:, i * 512:(i + 1) * 512], pX[:, 1536:2048]), inc=sV, waits=[(sT, n)])
        maps_v = p.cnt[sV]
        rows_free_t = p.cnt[sT]
        pX_free[3] = (sV, maps_v)
        ld_x1 = p.op("sync", lambda e, qb=qb: e.dma_start(
            out=A_sb.rearrange("p (g t) -> p g t", t=512),
            in_=E.sp_x1[:, qb * 512:(qb + 1) * 512].rearrange("(g p) t -> p g t", p=128)),
            inc=dS, inck=16, waits=[(sS, exp_s[3])])
        for dv in range(8):
            hb = dv % 2
            w = [(sV, maps_v), (dG, o_stored[hb])]
            p.op("vector", lambda e, dv=dv, hb=hb: e.tensor_tensor(
                o_sb[:, hb * 512:(hb + 1) * 512], u_sb[:, dv * 512:(dv + 1) * 512],
                maps5[:, 0:512], ALU.mult), inc=sV, waits=w)
            p.op("vector", lambda e, dv=dv, hb=hb: e.tensor_sub(
                o_sb[:, hb * 512:(hb + 1) * 512], o_sb[:, hb * 512:(hb + 1) * 512],
                maps5[:, 512:1024]), inc=sV)
            nn = p.op("vector", lambda e, dv=dv, hb=hb: e.tensor_add(
                o_sb[:, hb * 512:(hb + 1) * 512], o_sb[:, hb * 512:(hb + 1) * 512],
                A_sb[:, dv * 512:(dv + 1) * 512]), inc=sV,
                waits=[(dS, ld_x1)] if dv == 0 else [])
            o_stored[hb] = p.op("gpsimd", lambda e, qb=qb, dv=dv, hb=hb: e.dma_start(
                out=E.out_t[dv * 128:(dv + 1) * 128, qb * 512:(qb + 1) * 512],
                in_=o_sb[:, hb * 512:(hb + 1) * 512]), inc=dG, inck=16,
                waits=[(sV, nn)])
        if E.dbg and qb == 0:
            p.op("gpsimd", lambda e: e.dma_start(out=E.t_u5[:, :], in_=u_sb[:, :]),
                 inc=dG, inck=16, waits=[(sV, u_v)])
            p.op("gpsimd", lambda e: e.dma_start(out=E.t_d5[:, :], in_=dcol[:, :]),
                 inc=dG, inck=16, waits=[(sS, exp_s[3])])
            p.op("gpsimd", lambda e: e.dma_start(out=E.t_r5[0:1, :], in_=rows5[0:1, :]),
                 inc=dG, inck=16, waits=[(sV, rows_v)])
            p.op("gpsimd", lambda e: e.dma_start(out=E.t_pt5[:, :], in_=PT_sb[:, :]),
                 inc=dG, inck=16, waits=[(dS, tr_d[3])])
            E.p5_tap = p.cnt[dG]
        u_free_v = p.cnt[sV]
        maps_free_v = p.cnt[sV]
        A_free_s = exp_s[3]
        pX_free = [(sV, p.cnt[sV])] * 4
        if E.dbg and qb == 0:
            pX_free = [(dG, E.p5_tap)] * 4
            u_free_v = 0
            pY_free = [(dG, E.p5_tap)] * 4

    p.op("sync", lambda e: e.nop(), waits=[(dG, p.cnt[dG]), (dS, p.cnt[dS]), (E.dC, p.cnt[E.dC])])
    px.close()



def _prep_weights(inputs):
    f32 = lambda a: np.ascontiguousarray(np.asarray(a, np.float32))
    w = {}
    w["w_in"] = f32(inputs["L0_attn_w_in"])
    w["conv_wl"] = f32(np.asarray(inputs["L0_attn_conv_w"]).reshape(18, 128, 4).transpose(1, 0, 2).reshape(128, 72))
    w["conv_bl"] = f32(np.asarray(inputs["L0_attn_conv_b"]).reshape(18, 128).T)
    w["dt_bias"] = f32(np.asarray(inputs["L0_attn_dt_bias"]).reshape(32, 1))
    w["a_log"] = f32(np.asarray(inputs["L0_attn_a_log"]).reshape(32, 1))
    w["d_skip"] = f32(np.asarray(inputs["L0_attn_d_skip"]).reshape(32, 1))
    w["normw"] = f32(np.asarray(inputs["L0_attn_norm_w"]).reshape(16, 128).T)
    w["w_out"] = f32(inputs["L0_attn_w_out"])
    w["w_fc"] = f32(inputs["L0_attn_w_fc"])
    w["w_proj"] = f32(inputs["L0_attn_w_proj"])
    for pp in "qkv":
        w[f"{pp}_nwT"] = f32(np.asarray(inputs[f"L0_ffn_{pp}_node_w"]).T)
        w[f"{pp}_nb"] = f32(np.asarray(inputs[f"L0_ffn_{pp}_node_b"]).reshape(15, 1))
        w[f"{pp}_w1"] = f32(np.asarray(inputs[f"L0_ffn_{pp}_w1"]).transpose(1, 0, 2).reshape(1024, 1024))
        w[f"{pp}_b1c"] = f32(np.asarray(inputs[f"L0_ffn_{pp}_b1"]).reshape(1024).reshape(8, 128).T)
        w[f"{pp}_w2"] = f32(np.asarray(inputs[f"L0_ffn_{pp}_w2"]).reshape(1024, 1024))
        w[f"{pp}_b2"] = f32(inputs[f"L0_ffn_{pp}_b2"])
    return w


def kernel(**inputs):
    x = np.asarray(inputs["L0_attn_x"], np.float32)
    w = _prep_weights(inputs)
    nc, counts, nw = build(nphases=5, dbg=False)
    in_maps = []
    for c in range(8):
        m = dict(w)
        m["xt"] = np.ascontiguousarray(x[c].T)
        in_maps.append(m)
    res = run_bass_kernel_spmd(nc, in_maps, core_ids=list(range(8)))
    out = np.stack([np.asarray(r["o"], np.float32).T for r in res.results], axis=0)
    return np.ascontiguousarray(out)


# version 02
# speedup vs baseline: 1.1107x; 1.1107x over previous
import numpy as np
import ml_dtypes
from contextlib import ExitStack
import concourse.bass as bass
import concourse.mybir as mybir
from concourse.bass_utils import run_bass_kernel_spmd

F32 = mybir.dt.float32
BF16 = mybir.dt.bfloat16
AF = mybir.ActivationFunctionType
ALU = mybir.AluOpType
AXX = mybir.AxisListType.X
BF = ml_dtypes.bfloat16

D = 1024; L = 4096; SL = 512; NSL = L // SL
CH = 256; NCH = L // CH
DIN = 2048; CONVDIM = 2304; DINPROJ = 4384
NH = 32; HD = 64; DS = 128
NZT = DIN // 128
NBCT = CONVDIM // 128
NMT = 35
EPS_LN = 1e-5
EPS_OUT = 1.1920929e-07
EPS_GATE = 1e-5
SCALE_ATT = 1.0 / 32.0

ENGS = ("sync", "gpsimd", "tensor", "vector", "scalar")


class Prog:

    def __init__(self, nc, ctx):
        self.nc = nc
        self.ctx = ctx
        self.items = {e: [] for e in ENGS}
        self.sems = {}
        self.cnt = {}
        self.asyncs = set()
        self.prod = {}

    def sem(self, name, producer, is_async=False):
        self.sems[name] = self.ctx.enter_context(self.nc.semaphore(name))
        self.cnt[name] = 0
        self.prod[name] = producer
        if is_async:
            self.asyncs.add(name)
        return name

    def op(self, eng, fn, inc=None, inck=1, waits=()):
        w = []
        for item in waits:
            if item is None:
                continue
            s, v = item
            if v is None:
                continue
            if isinstance(v, tuple):
                s, v = v
            assert s in self.cnt, s
            assert v <= self.cnt[s], (s, v, self.cnt[s])
            if v > 0:
                w.append((s, v))
        n = None
        if inc is not None:
            self.cnt[inc] += inck
            n = self.cnt[inc]
        self.items[eng].append((fn, inc, inck, tuple(w)))
        return n

    def emit(self):
        nwait = 0
        with self.nc.Block() as block:
            for engname in ENGS:
                items = self.items[engname]
                if not items:
                    continue

                def mk(items, engname):
                    def body(eng):
                        nonlocal nwait
                        seen = {}
                        own = {}
                        for fn, inc, inck, waits in items:
                            for s, v in waits:
                                if s not in self.asyncs and self.prod[s] == engname:
                                    assert v <= own.get(s, 0), (s, v, own.get(s, 0), engname)
                                    continue
                                if v <= seen.get(s, 0):
                                    continue
                                eng.wait_ge(self.sems[s], v)
                                seen[s] = v
                                nwait += 1
                            ins = fn(eng)
                            if inc is not None:
                                ins.then_inc(self.sems[inc], inck)
                                own[inc] = own.get(inc, 0) + inck
                    return body

                getattr(block, engname)(mk(items, engname))
        return nwait

    def fence(self):
        snap = [(s, self.cnt[s]) for s in self.sems if self.cnt[s] > 0]
        for e in ENGS:
            self.op(e, lambda en: en.nop(), waits=snap)

    def n_ops(self):
        return {e: len(v) for e, v in self.items.items()}


def _consts():
    c = {}
    c["ones1f"] = np.ones((1, 128), np.float32)
    c["ones1b"] = np.ones((1, 128), dtype=BF)
    c["ones128b"] = np.ones((128, 1), dtype=BF)
    c["I32b"] = np.eye(32, dtype=BF)
    c["I128b"] = np.eye(128, dtype=BF)
    c["eyeA"] = np.eye(256, dtype=BF)[0:128].copy()
    c["eyeB"] = np.eye(256, dtype=BF)[128:256].copy()
    c["REPb"] = (np.arange(2048)[None, :] // 64 == np.arange(32)[:, None]).astype(BF)
    c["REPf"] = (np.arange(2048)[None, :] // 64 == np.arange(32)[:, None]).astype(np.float32)
    c["IDX16"] = np.tile(np.arange(16, dtype=np.float32)[:, None], (1, 128)).astype(BF)
    c["leafof"] = ((np.arange(8)[None, :] * 128 + np.arange(128)[:, None]) // 64).astype(np.float32)
    c["ones32f"] = np.ones((32, 256), np.float32)
    c["ones128f"] = np.ones((128, 1), np.float32)
    A = np.zeros((15, 16), np.float32)
    neg = np.zeros((1, 16), np.float32)
    for l in range(16):
        b = [(l >> 3) & 1, (l >> 2) & 1, (l >> 1) & 1, l & 1]
        nodes = [0, 1 + b[0], 3 + 2 * b[0] + b[1], 7 + 4 * b[0] + 2 * b[1] + b[2]]
        for k in range(4):
            A[nodes[k], l] += 1.0 if b[k] else -1.0
            if not b[k]:
                neg[0, l] += 1.0
    c["Abit"] = A.astype(BF)
    c["negrow"] = neg.astype(BF)
    c["ones512b"] = np.ones((1, 512), dtype=BF)
    return c


class Env:
    pass


def build(nphases=5, dbg=False):
    nc = bass.Bass("TRN2", target_bir_lowering=False)
    E = Env()
    E.nc = nc

    E.xt = nc.dram_tensor("xt", [D, L], F32, kind="ExternalInput")
    E.w_in = nc.dram_tensor("w_in", [D, DINPROJ], F32, kind="ExternalInput")
    E.conv_wl = nc.dram_tensor("conv_wl", [128, 72], F32, kind="ExternalInput")
    E.conv_bl = nc.dram_tensor("conv_bl", [128, 18], F32, kind="ExternalInput")
    E.dt_bias = nc.dram_tensor("dt_bias", [32, 1], F32, kind="ExternalInput")
    E.a_log = nc.dram_tensor("a_log", [32, 1], F32, kind="ExternalInput")
    E.d_skip = nc.dram_tensor("d_skip", [32, 1], F32, kind="ExternalInput")
    E.normw = nc.dram_tensor("normw", [128, 16], F32, kind="ExternalInput")
    E.w_out = nc.dram_tensor("w_out", [DIN, D], F32, kind="ExternalInput")
    E.w_fc = nc.dram_tensor("w_fc", [D, 4 * D], F32, kind="ExternalInput")
    E.w_proj = nc.dram_tensor("w_proj", [4 * D, D], F32, kind="ExternalInput")
    E.fff = {}
    for pp in "qkv":
        E.fff[pp] = dict(
            nwT=nc.dram_tensor(f"{pp}_nwT", [D, 15], F32, kind="ExternalInput"),
            nb=nc.dram_tensor(f"{pp}_nb", [15, 1], F32, kind="ExternalInput"),
            w1=nc.dram_tensor(f"{pp}_w1", [D, D], F32, kind="ExternalInput"),
            b1c=nc.dram_tensor(f"{pp}_b1c", [128, 8], F32, kind="ExternalInput"),
            w2=nc.dram_tensor(f"{pp}_w2", [D, D], F32, kind="ExternalInput"),
            b2=nc.dram_tensor(f"{pp}_b2", [16, D], F32, kind="ExternalInput"),
        )
    E.out_t = nc.dram_tensor("o", [D, L], F32, kind="ExternalOutput")

    spk = "ExternalOutput" if dbg else "Internal"
    E.sp_xn = nc.dram_tensor("sp_xn", [D, L], F32, kind=spk)
    E.sp_silz = nc.dram_tensor("sp_silz", [DIN, L], BF16, kind=spk)
    E.sp_xc = nc.dram_tensor("sp_xc", [CONVDIM, L], BF16, kind=spk)
    E.sp_dtraw = nc.dram_tensor("sp_dtraw", [32, L], F32, kind=spk)
    E.sp_h = nc.dram_tensor("sp_h", [D, L], F32, kind=spk)
    E.sp_x1 = nc.dram_tensor("sp_x1", [D, L], F32, kind=spk)
    E.sp_q = nc.dram_tensor("sp_q", [D, L], BF16, kind=spk)
    E.sp_k = nc.dram_tensor("sp_k", [D, L], BF16, kind=spk)
    E.sp_v = nc.dram_tensor("sp_v", [L, D], BF16, kind=spk)
    E.sp_sm = nc.dram_tensor("sp_sm", [32, 1024], BF16, kind="Internal")
    if dbg:
        E.t_dsm = nc.dram_tensor("t_dsm", [32, 5 * 256], F32, kind="ExternalOutput")
        E.t_u = nc.dram_tensor("t_u", [128, 4096], F32, kind="ExternalOutput")
        E.t_ud = nc.dram_tensor("t_ud", [128, 4096], F32, kind="ExternalOutput")
        E.t_cbt = nc.dram_tensor("t_cbt", [128, 512], F32, kind="ExternalOutput")
        E.t_mexa = nc.dram_tensor("t_mexa", [128, 4096], F32, kind="ExternalOutput")
        E.t_G = nc.dram_tensor("t_G", [128, 2048], F32, kind="ExternalOutput")
        E.t_g = nc.dram_tensor("t_g", [128, 4096], F32, kind="ExternalOutput")
        E.t_prev = nc.dram_tensor("t_prev", [128, 2048], F32, kind="ExternalOutput")
        E.t_msb = nc.dram_tensor("t_msb", [128, 512], F32, kind="ExternalOutput")
        E.t_gf = nc.dram_tensor("t_gf", [128, 4096], F32, kind="ExternalOutput")
        E.t_xn2 = nc.dram_tensor("t_xn2", [128, 8 * 512], F32, kind="ExternalOutput")
        E.t_rt = nc.dram_tensor("t_rt", [128, 480], F32, kind="ExternalOutput")
        E.t_sel = nc.dram_tensor("t_sel", [128, 64], F32, kind="ExternalOutput")
        E.t_selT = nc.dram_tensor("t_selT", [16, 512], F32, kind="ExternalOutput")
        E.t_leaf = nc.dram_tensor("t_leaf", [128, 512], F32, kind="ExternalOutput")
        E.t_hid = nc.dram_tensor("t_hid", [128, 4096], F32, kind="ExternalOutput")
        E.t_fout = nc.dram_tensor("t_fout", [128, 4096], F32, kind="ExternalOutput")
        E.t_A5 = nc.dram_tensor("t_A5", [128, 4096], F32, kind="ExternalOutput")
        E.t_u5 = nc.dram_tensor("t_u5", [128, 4096], F32, kind="ExternalOutput")
        E.t_d5 = nc.dram_tensor("t_d5", [128, 4], F32, kind="ExternalOutput")
        E.t_r5 = nc.dram_tensor("t_r5", [1, 2048], F32, kind="ExternalOutput")
        E.t_pt5 = nc.dram_tensor("t_pt5", [128, 8192], F32, kind="ExternalOutput")
    E.dbg = dbg
    E.dbg_guard = 0

    cst = {k: nc.inline_tensor(v, f"c_{k}") for k, v in _consts().items()}

    ctx = ExitStack()
    p = Prog(nc, ctx)
    E.p = p
    dS = p.sem("dS", "sync", True)
    dG = p.sem("dG", "gpsimd", True)
    dC = p.sem("dC", "scalar", True)
    sT = p.sem("sT", "tensor")
    sV = p.sem("sV", "vector")
    sS = p.sem("sS", "scalar")
    sG = p.sem("sG", "gpsimd")
    E.dS, E.dG, E.dC, E.sT, E.sV, E.sS, E.sG = dS, dG, dC, sT, sV, sS, sG

    sb = lambda n, s, d: ctx.enter_context(nc.sbuf_tensor(n, s, d))
    E.sb = sb

    E.ones1f = sb("ones1f", [1, 128], F32)
    E.ones1b = sb("ones1b", [1, 128], BF16)
    E.ones128b = sb("ones128b", [128, 1], BF16)
    E.I32b = sb("I32b", [32, 32], BF16)
    E.I128b = sb("I128b", [128, 128], BF16)
    E.IDX16 = sb("IDX16", [16, 128], BF16)
    E.leafof = sb("leafof", [128, 8], F32)
    E.Abit = sb("Abit", [15, 16], BF16)
    E.negrow = sb("negrow", [1, 16], BF16)
    E.ones512b = sb("ones512b", [1, 512], BF16)
    E.smalls = sb("smalls", [32, 8], F32)
    E.ones128f = sb("ones128f", [128, 1], F32)
    E.cst = cst
    for t, cc in [(E.ones1f, "ones1f"), (E.ones1b, "ones1b"), (E.ones128b, "ones128b"),
                  (E.I32b, "I32b"), (E.I128b, "I128b"),
                  (E.IDX16, "IDX16"),
                  (E.leafof, "leafof"),
                  (E.Abit, "Abit"), (E.negrow, "negrow"), (E.ones512b, "ones512b"),
                  (E.ones128f, "ones128f")]:
        p.op("sync", lambda e, t=t, cc=cc: e.dma_start(out=t[:, :], in_=cst[cc][:, :]), inc=dS, inck=16)
    p.op("sync", lambda e: e.dma_start(out=E.smalls[:, 0:1], in_=E.dt_bias[:, :]), inc=dS, inck=16)
    p.op("sync", lambda e: e.dma_start(out=E.smalls[:, 3:4], in_=E.a_log[:, :]), inc=dS, inck=16)
    p.op("sync", lambda e: e.dma_start(out=E.smalls[:, 2:3], in_=E.d_skip[:, :]), inc=dS, inck=16)
    E.c_loaded = p.cnt[dS]
    E.epsb = sb("epsb", [128, 4], F32)
    for i, v in enumerate([EPS_LN, EPS_OUT, EPS_GATE, 1.0]):
        p.op("vector", lambda e, i=i, v=v: e.memset(E.epsb[:, i:i+1], v), inc=sV)
    E.epsb_ready = p.cnt[sV]
    n = p.op("scalar", lambda e: e.activation(E.smalls[0:32, 4:5], E.smalls[0:32, 3:4], AF.Exp),
             inc=sS, waits=[(dS, E.c_loaded)])
    p.op("vector", lambda e: e.tensor_scalar(E.smalls[0:32, 1:2], E.smalls[0:32, 4:5], -1.0, None, ALU.mult),
         inc=sV, waits=[(sS, n)])

    _phase1(E, nphases)
    if nphases >= 2:
        _phase2(E, nphases)
    if nphases >= 3:
        _phase3(E, nphases)
    if nphases >= 4:
        _phase4(E, nphases)
    if nphases >= 5:
        _phase5(E)

    nw = p.emit()
    ctx.close()
    return nc, p.n_ops(), nw


def _phase1(E, nphases):
    nc, p = E.nc, E.p
    dS, dG, sT, sV, sS = E.dS, E.dG, E.sT, E.sV, E.sS
    px = ExitStack()
    sbl = lambda n, s, d: px.enter_context(nc.sbuf_tensor(n, s, d))
    psl = lambda n, s: px.enter_context(nc.psum_tensor(n, s, F32))
    w_in_sb = sbl("w_in_sb", [128, 8 * DINPROJ], BF16)
    convw = sbl("convw", [128, 72], F32)
    convb = sbl("convb", [128, 18], F32)
    diag = sbl("diag", [128, 72 * 128], BF16)
    xbuf = sbl("xbuf", [128, 2 * 8 * SL], F32)
    xbf = sbl("xbf", [128, 8 * SL], BF16)
    rows1 = sbl("rows1", [1, 5 * SL], F32)
    mapsb = sbl("mapsb", [128, 3 * SL], F32)
    tch = sbl("tch", [128, SL], F32)
    xnb = sbl("xnb", [128, 2 * SL], F32)
    ain_sb = sbl("ain_sb", [128, 8 * SL], BF16)
    xsq = ain_sb
    xbc_sb = sbl("xbc_sb", [128, NBCT * 516], BF16)
    szb = sbl("szb", [128, 2 * SL], BF16)
    xcb = sbl("xcb", [128, 2 * SL], BF16)
    dtraw_sb = sbl("dtraw_sb", [32, SL], F32)
    E.p1_slice_dG = [0] * NSL
    pzx = psl("pzx", [128, 2048])
    pconv = psl("pconv", [128, 1024])
    pstatA = psl("pstatA", [1, 512])
    pstatB = psl("pstatB", [1, 512])
    if True:
        mu_r = rows1[0:1, 0:SL]
        var_r = rows1[0:1, SL:2*SL]; r1_r = rows1[0:1, 2*SL:3*SL]
        r2_r = rows1[0:1, 3*SL:4*SL]
        sc1_r = rows1[0:1, 4*SL:5*SL]
        ms_r = var_r; q_r = var_r

        for kc in range(8):
            p.op("gpsimd", lambda e, kc=kc: e.dma_start(
                out=w_in_sb[:, kc * DINPROJ:(kc + 1) * DINPROJ],
                in_=E.w_in[kc * 128:(kc + 1) * 128, :]), inc=dG, inck=16)
        p.op("gpsimd", lambda e: e.dma_start(out=convw[:, :], in_=E.conv_wl[:, :]), inc=dG, inck=16)
        p.op("gpsimd", lambda e: e.dma_start(out=convb[:, :], in_=E.conv_bl[:, :]), inc=dG, inck=16)
        w1_loaded = p.cnt[dG]

        for c in range(NBCT):
            for k in range(4):
                p.op("vector", lambda e, c=c, k=k: e.tensor_scalar(
                    diag[:, (c * 4 + k) * 128:(c * 4 + k + 1) * 128],
                    E.I128b[:, :], convw[:, c * 4 + k:c * 4 + k + 1], None, ALU.mult),
                    inc=sV, waits=[(dG, w1_loaded), (dS, E.c_loaded)])
        p.op("vector", lambda e: e.memset(xbc_sb[:, :], 0.0), inc=sV)
        diag_done = p.cnt[sV]

        xbuf_free = [0, 0]
        pzx_free = [None, None, None, None]
        pconv_free = [None, None]
        stats_free = 0
        rows_free_t = 0
        xbf_free_t = 0
        ain_free_t = 0
        xn_stored = [0, 0]
        silz_stored = [0, 0]
        dtraw_stored = 0
        xc_stored = [0, 0]
        mapsb_free = 0
        xc_prev_conv = 0

        def load_x(s):
            b = s % 2
            return p.op("sync", lambda e, s=s, b=b: e.dma_start(
                out=xbuf[:, b * 8 * SL:(b + 1) * 8 * SL].rearrange("p (c t) -> p c t", t=SL),
                in_=E.xt[:, s * SL:(s + 1) * SL].rearrange("(c p) t -> p c t", p=128)),
                inc=dS, inck=16, waits=[(sV, xbuf_free[b])])

        x_ld = [None] * (NSL + 1)
        x_ld[0] = load_x(0)

        for s in range(NSL):
            b = s % 2
            xv = lambda ch, b=b: xbuf[:, (b * 8 + ch) * SL:(b * 8 + ch + 1) * SL]
            if s + 1 < NSL:
                x_ld[s + 1] = load_x(s + 1)

            for ch in range(8):
                p.op("vector", lambda e, ch=ch, xv=xv: e.tensor_copy(
                    xbf[:, ch * SL:(ch + 1) * SL], xv(ch)), inc=sV,
                    waits=[(dS, x_ld[s]), (sT, xbf_free_t)])
                p.op("scalar", lambda e, ch=ch, xv=xv: e.activation(
                    xsq[:, ch * SL:(ch + 1) * SL], xv(ch), AF.Square), inc=sS,
                    waits=[(dS, x_ld[s]), (sT, xbf_free_t), (sT, ain_free_t)])
            xbf_done = p.cnt[sV]; xsq_done = p.cnt[sS]

            for ch in range(8):
                p.op("tensor", lambda e, ch=ch: e.matmul(
                    pstatA[0:1, :], E.ones128b[:, 0:1], xbf[:, ch * SL:(ch + 1) * SL],
                    start=(ch == 0), stop=(ch == 7)),
                    inc=(sT if ch == 7 else None),
                    waits=[(sV, xbf_done), (sV, stats_free)] if ch == 0 else [])
            for ch in range(8):
                p.op("tensor", lambda e, ch=ch: e.matmul(
                    pstatB[0:1, :], E.ones128b[:, 0:1], xsq[:, ch * SL:(ch + 1) * SL],
                    start=(ch == 0), stop=(ch == 7)),
                    inc=(sT if ch == 7 else None),
                    waits=[(sS, xsq_done)] if ch == 0 else [])
            stats_mm = p.cnt[sT]
            xbf_free_t = stats_mm

            p.op("vector", lambda e: e.tensor_scalar(mu_r, pstatA[0:1, :], 1.0 / 1024, None, ALU.mult),
                 inc=sV, waits=[(sT, stats_mm), (sT, rows_free_t)])
            p.op("vector", lambda e: e.tensor_scalar(ms_r, pstatB[0:1, :], 1.0 / 1024, None, ALU.mult),
                 inc=sV)
            stats_evaced = p.cnt[sV]
            p.op("scalar", lambda e: e.activation(sc1_r, mu_r, AF.Square), inc=sS,
                 waits=[(sV, stats_evaced), (sT, rows_free_t)])
            p.op("vector", lambda e: e.tensor_sub(var_r, ms_r, sc1_r), inc=sV,
                 waits=[(sS, p.cnt[sS])])
            p.op("scalar", lambda e: e.activation(sc1_r, var_r, AF.Ln, bias=E.epsb[0:1, 0:1]), inc=sS,
                 waits=[(sV, p.cnt[sV]), (sV, E.epsb_ready)])
            p.op("scalar", lambda e: e.activation(r1_r, sc1_r, AF.Exp, scale=-0.5), inc=sS)
            p.op("scalar", lambda e: e.activation(sc1_r, r1_r, AF.Square), inc=sS)
            p.op("vector", lambda e: e.tensor_mul(q_r, var_r, sc1_r), inc=sV,
                 waits=[(sS, p.cnt[sS])])
            p.op("scalar", lambda e: e.activation(sc1_r, q_r, AF.Ln, bias=E.epsb[0:1, 1:2]), inc=sS,
                 waits=[(sV, p.cnt[sV])])
            p.op("scalar", lambda e: e.activation(sc1_r, sc1_r, AF.Exp, scale=-0.5), inc=sS)
            p.op("vector", lambda e: e.tensor_mul(r2_r, r1_r, sc1_r), inc=sV,
                 waits=[(sS, p.cnt[sS])])
            rows_done_v = p.cnt[sV]
            rows_done_s = p.cnt[sS]

            prev_evac = None
            for i, (row, mi) in enumerate([(mu_r, 0), (r1_r, 1), (r2_r, 2)]):
                w = [(sV, rows_done_v), (sS, rows_done_s)]
                if i == 0:
                    w += [pzx_free[3], (sV, mapsb_free)]
                else:
                    w.append((sV, prev_evac))
                n = p.op("tensor", lambda e, row=row: e.matmul(
                    pzx[:, 1536:2048], E.ones1f[0:1, :], row, start=True, stop=True),
                    inc=sT, waits=w)
                prev_evac = p.op("vector", lambda e, mi=mi: e.tensor_copy(
                    mapsb[:, mi * SL:(mi + 1) * SL], pzx[:, 1536:2048]), inc=sV,
                    waits=[(sT, n)])
            maps_done = p.cnt[sV]
            rows_free_t = p.cnt[sT]
            pzx_free[3] = (sV, maps_done)

            for ch in range(8):
                hb = ch % 2
                wv = [(dG, xn_stored[hb])]
                if ch == 0:
                    wv += [(sV, maps_done), (sT, ain_free_t)]
                p.op("vector", lambda e, ch=ch, xv=xv: e.tensor_sub(tch[:, :], xv(ch), mapsb[:, 0:SL]),
                     inc=sV, waits=wv)
                n = p.op("vector", lambda e, ch=ch, hb=hb: e.tensor_mul(
                    xnb[:, hb * SL:(hb + 1) * SL], tch[:, :], mapsb[:, SL:2*SL]), inc=sV)
                p.op("vector", lambda e, ch=ch: e.tensor_mul(
                    ain_sb[:, ch * SL:(ch + 1) * SL], tch[:, :], mapsb[:, 2*SL:3*SL]), inc=sV)
                xn_stored[hb] = p.op("gpsimd", lambda e, s=s, ch=ch, hb=hb: e.dma_start(
                    out=E.sp_xn[ch * 128:(ch + 1) * 128, s * SL:(s + 1) * SL],
                    in_=xnb[:, hb * SL:(hb + 1) * SL]),
                    inc=dG, inck=16, waits=[(sV, n)])
            apply_done = p.cnt[sV]
            xbuf_free[b] = apply_done
            mapsb_free = apply_done

            for mt in range(NMT):
                r = mt % 4
                M = 128 if mt < NMT - 1 else 32
                w = []
                if mt == 0:
                    w += [(sV, apply_done), (dG, w1_loaded)]
                if pzx_free[r] is not None:
                    w.append(pzx_free[r])
                for kc in range(8):
                    p.op("tensor", lambda e, kc=kc, mt=mt, r=r, M=M: e.matmul(
                        pzx[0:M, r * 512:r * 512 + 512],
                        w_in_sb[:, kc * DINPROJ + mt * 128: kc * DINPROJ + mt * 128 + M],
                        ain_sb[:, kc * SL:(kc + 1) * SL],
                        start=(kc == 0), stop=(kc == 7)),
                        inc=(sT if kc == 7 else None),
                        waits=w if kc == 0 else [])
                g = p.cnt[sT]
                if mt < NZT:
                    hb = mt % 2
                    wz = [(sT, g), (dG, silz_stored[hb])]
                    n = p.op("scalar", lambda e, hb=hb, r=r: e.activation(
                        szb[:, hb * SL:(hb + 1) * SL], pzx[:, r * 512:r * 512 + 512], AF.Silu),
                        inc=sS, waits=wz)
                    pzx_free[r] = (sS, n)
                    silz_stored[hb] = p.op("gpsimd", lambda e, s=s, mt=mt, hb=hb: e.dma_start(
                        out=E.sp_silz[mt * 128:(mt + 1) * 128, s * SL:(s + 1) * SL],
                        in_=szb[:, hb * SL:(hb + 1) * SL]),
                        inc=dG, inck=16, waits=[(sS, n)])
                elif mt < NZT + NBCT:
                    c = mt - NZT
                    wz = [(sT, g)]
                    if c == 0:
                        wz += [(sT, xc_prev_conv)]
                    n = p.op("vector", lambda e, c=c, r=r: e.tensor_copy(
                        xbc_sb[:, c * 516 + 3:c * 516 + 515], pzx[:, r * 512:r * 512 + 512]),
                        inc=sV, waits=wz)
                    pzx_free[r] = (sV, n)
                else:
                    n = p.op("vector", lambda e, r=r: e.tensor_copy(
                        dtraw_sb[0:32, :], pzx[0:32, r * 512:r * 512 + 512]),
                        inc=sV, waits=[(sT, g), (dG, dtraw_stored)])
                    pzx_free[r] = (sV, n)
            zx_done_v = p.cnt[sV]; zx_done_s = p.cnt[sS]
            ain_free_t = p.cnt[sT]
            dtraw_stored = p.op("gpsimd", lambda e, s=s: e.dma_start(
                out=E.sp_dtraw[:, s * SL:(s + 1) * SL], in_=dtraw_sb[0:32, :]),
                inc=dG, inck=16, waits=[(sV, zx_done_v)])

            for c in range(NBCT):
                r = c % 2
                w = [(sV, zx_done_v), (sV, diag_done)]
                if pconv_free[r] is not None:
                    w.append(pconv_free[r])
                for k in range(4):
                    p.op("tensor", lambda e, c=c, k=k, r=r: e.matmul(
                        pconv[:, r * 512:r * 512 + 512],
                        diag[:, (c * 4 + k) * 128:(c * 4 + k + 1) * 128],
                        xbc_sb[:, c * 516 + k:c * 516 + k + 512],
                        start=(k == 0), stop=(k == 3)),
                        inc=(sT if k == 3 else None),
                        waits=w if k == 0 else [])
                g = p.cnt[sT]
                hb = c % 2
                wz = [(sT, g), (dG, xc_stored[hb])]
                n = p.op("scalar", lambda e, hb=hb, c=c, r=r: e.activation(
                    xcb[:, hb * SL:(hb + 1) * SL], pconv[:, r * 512:r * 512 + 512],
                    AF.Silu, bias=convb[:, c:c + 1]),
                    inc=sS, waits=wz)
                pconv_free[r] = (sS, n)
                xc_stored[hb] = p.op("gpsimd", lambda e, s=s, c=c, hb=hb: e.dma_start(
                    out=E.sp_xc[c * 128:(c + 1) * 128, s * SL:(s + 1) * SL],
                    in_=xcb[:, hb * SL:(hb + 1) * SL]),
                    inc=dG, inck=16, waits=[(sS, n)])
                p.op("vector", lambda e, c=c: e.tensor_copy(
                    xbc_sb[:, c * 516:c * 516 + 3], xbc_sb[:, c * 516 + 512:c * 516 + 515]),
                    inc=sV, waits=[(sT, g)])
            xc_prev_conv = p.cnt[sT]
            stats_free = stats_evaced
            E.p1_slice_dG[s] = p.cnt[dG]

        E.p1_spills = p.cnt[dG]
        if nphases == 1:
            p.op("sync", lambda e: e.nop(), waits=[(dG, E.p1_spills)])
    px.close()


def _phase2(E, nphases):
    nc, p = E.nc, E.p
    p.fence()
    dS, dG, sT, sV, sS, sG = E.dS, E.dG, E.sT, E.sV, E.sS, E.sG
    px = ExitStack()
    sbl = lambda n, s, d: px.enter_context(nc.sbuf_tensor(n, s, d))
    w_out_sb = sbl("w_out_sb", [128, 16 * 1024], BF16)
    normw_sb = sbl("normw_sb", [128, 16], F32)
    xctok = sbl("xctok", [128, 2 * 2176], BF16)
    bcfeat = sbl("bcfeat", [128, 512], BF16)
    siluzt = sbl("siluzt", [128, 2 * 2048], BF16)
    xn_c = sbl("xn_c", [128, 8 * 256], F32)
    dtraw_c = sbl("dtraw_c", [32, 256], F32)
    dsm = sbl("dsm", [32, 5 * 256], F32)
    dsb = sbl("dsb", [32, 4 * 256], BF16)
    cdcol = sbl("cdcol", [32, 1], F32)
    mEXA = sbl("mEXA", [128, 2 * 2048], BF16)
    mU = sbl("mU", [128, 2048], BF16)
    mSKIP = sbl("mSKIP", [128, 2048], BF16)
    mDECAY = sbl("mDECAY", [128, 2048], F32)
    u_tok = sbl("u_tok", [128, 2 * 2048], BF16)
    udec = sbl("udec", [128, 2 * 2048], BF16)
    CBT_sb = sbl("CBT_sb", [128, 512], BF16)
    expa_row = sbl("expa_row", [1, 2048], BF16)
    mapALL = sbl("mapALL", [128, 32 * 256], BF16)
    LT = sbl("LT", [128, 2 * 512], BF16)
    Msb = sbl("Msb", [128, 2 * 512], BF16)
    Abuf = sbl("Abuf", [128, 2048], F32)
    Bbuf = sbl("Bbuf", [128, 2048], F32)
    gtok = sbl("gtok", [128, 2 * 2048], BF16)
    gfeat = sbl("gfeat", [128, 16 * 256], BF16)
    hbuf = sbl("hbuf", [128, 2 * 256], F32)
    prev = sbl("prev", [128, 2048], F32)
    prevbf = sbl("prevbf", [128, 2048], BF16)
    rcol = sbl("rcol", [128, 4], F32)
    eyeA = sbl("eyeA2", [128, 256], BF16)
    eyeB = sbl("eyeB2", [128, 256], BF16)
    REPb = sbl("REPb2", [32, 2048], BF16)
    REPf = sbl("REPf2", [32, 2048], F32)
    ones32f = sbl("ones32f2", [32, 256], F32)
    for t, cc in [(eyeA, "eyeA"), (eyeB, "eyeB"), (REPb, "REPb"), (REPf, "REPf"),
                  (ones32f, "ones32f")]:
        p.op("sync", lambda e, t=t, cc=cc: e.dma_start(out=t[:, :], in_=E.cst[cc][:, :]),
             inc=dS, inck=16)
    E.c2_loaded = p.cnt[dS]
    pA = px.enter_context(nc.psum_tensor("pA", [128, 2048], F32))
    pB = px.enter_context(nc.psum_tensor("pB", [128, 2048], F32))

    s_abs = dsm[0:32, 0:256]; s_dt = dsm[0:32, 256:512]
    s_a = dsm[0:32, 512:768]; s_acum = dsm[0:32, 768:1024]
    s_dec = dsm[0:32, 1024:1280]
    b_dt = dsb[0:32, 0:256]; b_dtdec = dsb[0:32, 256:512]
    b_expa = dsb[0:32, 512:768]; b_expacum = dsb[0:32, 768:1024]

    p.op("gpsimd", lambda e: e.dma_start(out=normw_sb[:, :], in_=E.normw[:, :]), inc=dG, inck=16)
    for kc in range(16):
        p.op("gpsimd", lambda e, kc=kc: e.dma_start(
            out=w_out_sb[:, kc * 1024:(kc + 1) * 1024],
            in_=E.w_out[kc * 128:(kc + 1) * 128, :]), inc=dG, inck=16)
    nwl = p.cnt[dG]
    for kc in range(16):
        p.op("vector", lambda e, kc=kc: e.tensor_scalar(
            w_out_sb[:, kc * 1024:(kc + 1) * 1024], w_out_sb[:, kc * 1024:(kc + 1) * 1024],
            normw_sb[:, kc:kc + 1], None, ALU.mult),
            inc=sV, waits=[(dG, nwl)] if kc == 0 else [])
    wout_ready = p.cnt[sV]

    for i in range(4):
        w = [(dS, E.c_loaded), (dS, E.c2_loaded)] if i == 0 else []
        n = p.op("tensor", lambda e, i=i: e.matmul(
            pB[0:1, i * 512:(i + 1) * 512], E.smalls[0:32, 2:3], REPf[:, i * 512:(i + 1) * 512],
            start=True, stop=True), inc=sT, waits=w)
    nv = p.op("vector", lambda e: e.tensor_copy(Abuf[0:1, :], pB[0:1, :]), inc=sV, waits=[(sT, n)])
    for i in range(4):
        w = [(sV, nv)] if i == 0 else []
        n = p.op("tensor", lambda e, i=i: e.matmul(
            pA[:, i * 512:(i + 1) * 512], E.ones1f[0:1, :], Abuf[0:1, i * 512:(i + 1) * 512],
            start=True, stop=True), inc=sT, waits=w)
    nv2 = p.op("vector", lambda e: e.tensor_copy(mSKIP[:, :], pA[:, :]), inc=sV, waits=[(sT, n)])
    skip_ready = nv2
    pA_free = (sV, nv2)
    pB_free = (sV, nv)

    def rep_map(lhs_slice, rep, dst_ap, wait_mm, wait_evac, ps):
        for i in range(4):
            w = list(wait_mm) if i == 0 else []
            n = p.op("tensor", lambda e, i=i, lhs_slice=lhs_slice, rep=rep, ps=ps: e.matmul(
                ps[:, i * 512:(i + 1) * 512], lhs_slice, rep[:, i * 512:(i + 1) * 512],
                start=True, stop=True), inc=sT, waits=w)
        return p.op("vector", lambda e, dst_ap=dst_ap, ps=ps: e.tensor_copy(dst_ap, ps[:, :]),
                    inc=sV, waits=[(sT, n)] + list(wait_evac))

    prev_ready_v = 0
    hb_free = [0, 0]
    gfeat_free_t = 0
    gtok_free_d = 0
    xn_free_v = 0
    xctok_free = [(0, 0), (0, 0)]
    siluz_free = [0, 0]
    bc_free_t = 0
    dtraw_free_s = 0
    mexa_free = [0, 0]
    utok_free_t = 0
    udec_free_t = 0
    mdecay_free_g = 0
    abuf_free = 0
    bbuf_free = 0
    prevbf_free_t = 0
    dsb_store = 0
    expa_row_free_t = [0, 0, 0, 0]
    mapall_free = (0, 0)

    for c in range(NCH):
        sl = c // 2
        p1rdy = E.p1_slice_dG[sl]
        ld_xct = []
        for tt in range(2):
            ld_xct.append(p.op("sync", lambda e, c=c, tt=tt: e.dma_start_transpose(
                xctok[:, tt * 2176:(tt + 1) * 2176],
                E.sp_xc[0:2176, c * 256 + tt * 128: c * 256 + (tt + 1) * 128]),
                inc=dS, inck=16,
                waits=[(dG, p1rdy), (sT, xctok_free[tt][0]), (sG, xctok_free[tt][1])]))
        ld_bc = p.op("sync", lambda e, c=c: e.dma_start(
            out=bcfeat[:, 0:512].rearrange("p (g t) -> p g t", t=256),
            in_=E.sp_xc[2048:2304, c * 256:(c + 1) * 256].rearrange("(g p) t -> p g t", p=128)),
            inc=dS, inck=16, waits=[(dG, p1rdy), (sT, bc_free_t)])
        ld_sz = []
        for tt in range(2):
            ld_sz.append(p.op("sync", lambda e, c=c, tt=tt: e.dma_start_transpose(
                siluzt[:, tt * 2048:(tt + 1) * 2048],
                E.sp_silz[0:2048, c * 256 + tt * 128: c * 256 + (tt + 1) * 128]),
                inc=dS, inck=16, waits=[(dG, p1rdy), (sV, siluz_free[tt])]))
        ld_xn = p.op("sync", lambda e, c=c: e.dma_start(
            out=xn_c.rearrange("p (g t) -> p g t", t=256),
            in_=E.sp_xn[:, c * 256:(c + 1) * 256].rearrange("(g p) t -> p g t", p=128)),
            inc=dS, inck=16, waits=[(dG, p1rdy), (sV, xn_free_v)])
        ld_dt = p.op("sync", lambda e, c=c: e.dma_start(
            out=dtraw_c[:, :], in_=E.sp_dtraw[0:32, c * 256:(c + 1) * 256]),
            inc=dS, inck=16, waits=[(dG, p1rdy), (sS, dtraw_free_s)])

        p.op("scalar", lambda e: e.activation(s_abs, dtraw_c[:, :], AF.Abs, bias=E.smalls[0:32, 0:1]),
             inc=sS, waits=[(dS, ld_dt), (dG, dsb_store)])
        p.op("scalar", lambda e: e.activation(s_abs, s_abs, AF.Exp, scale=-1.0), inc=sS)
        p.op("scalar", lambda e: e.activation(s_abs, s_abs, AF.Ln, bias=E.epsb[0:32, 3:4]), inc=sS)
        p.op("scalar", lambda e: e.activation(s_dt, dtraw_c[:, :], AF.Relu, bias=E.smalls[0:32, 0:1]),
             inc=sS)
        dtraw_free_s = p.cnt[sS]
        ns = p.cnt[sS]
        p.op("vector", lambda e: e.tensor_add(s_dt, s_dt, s_abs), inc=sV,
             waits=[(sS, ns), (dG, dsb_store)])
        p.op("vector", lambda e: e.tensor_copy(b_dt, s_dt), inc=sV)
        p.op("vector", lambda e: e.tensor_scalar(s_a, s_dt, E.smalls[0:32, 1:2], None, ALU.mult),
             inc=sV)
        p.op("vector", lambda e: e.reduce_sum(E.smalls[0:32, 5:6], s_a, axis=AXX), inc=sV)
        p.op("vector", lambda e: e.tensor_tensor_scan(s_acum, ones32f[:, :], s_a, 0.0,
                                                      ALU.mult, ALU.add), inc=sV)
        nv = p.cnt[sV]
        p.op("scalar", lambda e: e.activation(b_expa, s_a, AF.Exp), inc=sS, waits=[(sV, nv)])
        p.op("scalar", lambda e: e.activation(b_expacum, s_acum, AF.Exp), inc=sS)
        p.op("scalar", lambda e: e.activation(cdcol[0:32, :], E.smalls[0:32, 5:6], AF.Exp), inc=sS)
        p.op("vector", lambda e: e.tensor_scalar(s_dec, s_acum, E.smalls[0:32, 5:6], -1.0,
                                                 ALU.subtract, ALU.mult), inc=sV)
        nv = p.cnt[sV]
        p.op("scalar", lambda e: e.activation(s_dec, s_dec, AF.Exp), inc=sS,
             waits=[(sV, nv)])
        ns = p.cnt[sS]
        p.op("vector", lambda e: e.tensor_tensor(b_dtdec, s_dt, s_dec, ALU.mult), inc=sV,
             waits=[(sS, ns)])
        dt_small_v = p.cnt[sV]; dt_small_s = p.cnt[sS]

        dsb_store = p.op("gpsimd", lambda e: e.dma_start(out=E.sp_sm[:, :], in_=dsb[0:32, :]),
                         inc=dG, inck=16, waits=[(sV, dt_small_v), (sS, dt_small_s)])

        prevbf_done = 0
        if c > 0:
            prevbf_done = p.op("vector", lambda e: e.tensor_copy(prevbf[:, :], prev[:, :]),
                               inc=sV, waits=[(sV, prev_ready_v)])

        mexa_ev = []
        for tt in range(2):
            nn = rep_map(dsb[0:32, 768 + tt * 128:768 + (tt + 1) * 128], REPb,
                         mEXA[:, tt * 2048:(tt + 1) * 2048],
                         [(sS, dt_small_s), pA_free], [(sV, mexa_free[tt])], pA)
            pA_free = (sV, nn)
            mexa_ev.append(nn)
        upool = 0
        for j, (off, dst, freet) in enumerate(((0, u_tok, utok_free_t), (256, udec, udec_free_t))):
            for tt in range(2):
                nn = rep_map(dsb[0:32, off + tt * 128:off + (tt + 1) * 128], REPb,
                             mU[:, :], [(sV, dt_small_v), pA_free], [(sG, upool)], pA)
                pA_free = (sV, nn)
                upool = p.op("gpsimd", lambda e, dst=dst, tt=tt: e.tensor_tensor(
                    dst[:, tt * 2048:(tt + 1) * 2048], xctok[:, tt * 2176:tt * 2176 + 2048],
                    mU[:, :], ALU.mult), inc=sG,
                    waits=[(sV, nn), (dS, ld_xct[tt]), (sT, freet)])
        upool_done = p.cnt[sG]
        for i in range(4):
            w = [(sS, dt_small_s), pB_free] if i == 0 else []
            n = p.op("tensor", lambda e, i=i: e.matmul(
                pB[0:1, i * 512:(i + 1) * 512], cdcol[0:32, :], REPf[:, i * 512:(i + 1) * 512],
                start=True, stop=True), inc=sT, waits=w)
        drow_done = p.op("vector", lambda e: e.tensor_copy(Abuf[0:1, :], pB[0:1, :]), inc=sV,
                         waits=[(sT, n), (sG, mdecay_free_g), (sV, abuf_free)])
        for i in range(4):
            w = [(sV, drow_done), pA_free] if i == 0 else []
            n = p.op("tensor", lambda e, i=i: e.matmul(
                pA[:, i * 512:(i + 1) * 512], E.ones1f[0:1, :], Abuf[0:1, i * 512:(i + 1) * 512],
                start=True, stop=True), inc=sT, waits=w)
        mdecay_done = p.op("vector", lambda e: e.tensor_copy(mDECAY[:, :], pA[:, :]), inc=sV,
                           waits=[(sT, n)])
        pA_free = (sV, mdecay_done)

        map_evs = []
        for grp in range(4):
            ld_row = p.op("sync", lambda e, grp=grp: e.dma_start(
                out=expa_row.rearrange("a (h t) -> a h t", t=256),
                in_=E.sp_sm.rearrange("h (j t) -> j h t", t=256)[2:3, grp * 8:(grp + 1) * 8, :]),
                inc=dS, inck=16,
                waits=[(dG, dsb_store), (sT, expa_row_free_t[(grp - 1) % 4]),
                       (sT, expa_row_free_t[grp])])
            for i in range(4):
                w = [(dS, ld_row)]
                if grp == 0 and i == 0:
                    w += [pA_free, (sV, mapall_free[0])]
                n = p.op("tensor", lambda e, i=i: e.matmul(
                    pA[:, i * 512:(i + 1) * 512], E.ones1b[0:1, :],
                    expa_row[0:1, i * 512:(i + 1) * 512], start=True, stop=True), inc=sT,
                    waits=w)
            expa_row_free_t[grp] = n
            nn = p.op("vector", lambda e, grp=grp: e.tensor_copy(
                mapALL[:, grp * 2048:(grp + 1) * 2048], pA[:, :]), inc=sV, waits=[(sT, n)])
            map_evs.append(nn)
        pA_free = (sV, map_evs[-1])
        mapall_done = map_evs[-1]

        for st in range(2):
            w = [(dS, ld_bc), pA_free] if st == 0 else []
            n = p.op("tensor", lambda e, st=st: e.matmul(
                pA[0:128, st * 256:(st + 1) * 256],
                bcfeat[:, st * 128:(st + 1) * 128],
                bcfeat[:, 256:512], start=True, stop=True), inc=sT, waits=w)
        cbt_done = p.op("vector", lambda e: e.tensor_copy(CBT_sb[:, :], pA[:, 0:512]), inc=sV,
                        waits=[(sT, n)])
        pA_free = (sV, cbt_done)

        if c > 0:
            y0 = []
            for tt in range(2):
                w0 = [(sV, prevbf_done), (dS, ld_bc), pB_free]
                for h in range(NH):
                    n3 = p.op("tensor", lambda e, tt=tt, h=h: e.matmul(
                        pB[:, h * 64:(h + 1) * 64],
                        bcfeat[:, 256 + tt * 128: 384 + tt * 128],
                        prevbf[:, h * 64:(h + 1) * 64], start=True, stop=True), inc=sT,
                        waits=w0 if h == 0 else [])
                    w0 = []
                dstb = Abuf if tt == 0 else Bbuf
                nn = p.op("vector", lambda e, tt=tt, dstb=dstb: e.scalar_tensor_tensor(
                    dstb[:, :], pB[:, :], 1.0, mEXA[:, tt * 2048:(tt + 1) * 2048],
                    ALU.mult, ALU.mult), inc=sV,
                    waits=[(sT, n3), (sV, mexa_ev[tt]), (sV, bbuf_free)])
                pB_free = (sV, nn)
                y0.append(nn)
            y0a, y0b = y0
            prevbf_free_t = p.cnt[sT]
        else:
            tcur = p.cnt[sT]
            y0a = p.op("vector", lambda e: e.memset(Abuf[:, :], 0.0), inc=sV,
                       waits=[(sT, tcur), (sV, abuf_free)])
            y0b = p.op("vector", lambda e: e.memset(Bbuf[:, :], 0.0), inc=sV,
                       waits=[(sV, bbuf_free)])
        mexa_free = [y0a, y0b]

        msb_free_t = [0, 0]
        lt_free_g = [0, 0]
        e1_last = 0
        scan_v = {}
        mprime_g = {}
        for h in range(NH):
            par = h % 2
            w = [(sV, mapall_done), (sG, lt_free_g[par]), (dS, E.c2_loaded)]
            p.op("vector", lambda e, h=h, par=par: e.tensor_tensor_scan(
                LT[:, par * 512:par * 512 + 256], mapALL[:, h * 256:(h + 1) * 256],
                eyeA[:, :], 0.0, ALU.mult, ALU.add), inc=sV, waits=w)
            nv = p.op("vector", lambda e, h=h, par=par: e.tensor_tensor_scan(
                LT[:, par * 512 + 256:par * 512 + 512], mapALL[:, h * 256:(h + 1) * 256],
                eyeB[:, :], 0.0, ALU.mult, ALU.add), inc=sV)
            scan_v[h] = nv
            ng = p.op("gpsimd", lambda e, par=par: e.tensor_tensor(
                Msb[:, par * 512:(par + 1) * 512], CBT_sb[:, :],
                LT[:, par * 512:(par + 1) * 512], ALU.mult), inc=sG,
                waits=[(sV, nv), (sV, cbt_done), (sT, msb_free_t[par])])
            lt_free_g[par] = ng
            mprime_g[h] = ng
            w = [(sG, ng), (sG, upool_done)]
            if h == 0:
                w += [pA_free]
            p.op("tensor", lambda e, h=h, par=par: e.matmul(
                pA[:, h * 64:(h + 1) * 64], Msb[:, par * 512:par * 512 + 128],
                u_tok[:, h * 64:(h + 1) * 64], start=True, stop=True), inc=sT, waits=w)
            p.op("tensor", lambda e, h=h, par=par: e.matmul(
                pB[:, h * 64:(h + 1) * 64], Msb[:, par * 512 + 128:par * 512 + 256],
                u_tok[:, h * 64:(h + 1) * 64], start=True, stop=False), inc=sT,
                waits=[pB_free] if h == 0 else [])
            e1_last = p.op("tensor", lambda e, h=h, par=par: e.matmul(
                pB[:, h * 64:(h + 1) * 64], Msb[:, par * 512 + 384:par * 512 + 512],
                u_tok[:, 2048 + h * 64: 2048 + (h + 1) * 64], start=False, stop=True), inc=sT)
            msb_free_t[par] = e1_last
        y1a = p.op("vector", lambda e: e.tensor_add(Abuf[:, :], pA[:, :], Abuf[:, :]), inc=sV,
                   waits=[(sT, e1_last), (sV, y0a)])
        y1b = p.op("vector", lambda e: e.tensor_add(Bbuf[:, :], pB[:, :], Bbuf[:, :]), inc=sV,
                   waits=[(sV, y0b)])
        pA_free = (sV, y1a)
        pB_free = (sV, y1b)
        utok_free_t = p.cnt[sT]

        g_done = []
        skip_g = []
        for tt in range(2):
            buf = Abuf if tt == 0 else Bbuf
            yev = y1a if tt == 0 else y1b
            ng = p.op("gpsimd", lambda e, tt=tt: e.tensor_tensor(
                gtok[:, tt * 2048:(tt + 1) * 2048], xctok[:, tt * 2176:tt * 2176 + 2048],
                mSKIP[:, :], ALU.mult), inc=sG,
                waits=[(sV, skip_ready), (dS, ld_xct[tt]), (dS, gtok_free_d)])
            skip_g.append(ng)
            p.op("vector", lambda e, buf=buf, tt=tt: e.tensor_add(
                buf[:, :], buf[:, :], gtok[:, tt * 2048:(tt + 1) * 2048]), inc=sV,
                waits=[(sG, ng), (sV, yev)])
            nn = p.op("vector", lambda e, buf=buf, tt=tt: e.tensor_tensor(
                buf[:, :], buf[:, :], siluzt[:, tt * 2048:(tt + 1) * 2048], ALU.mult), inc=sV,
                waits=[(dS, ld_sz[tt])])
            siluz_free[tt] = nn
            ns = p.op("scalar", lambda e, buf=buf, tt=tt: e.activation(
                gtok[:, tt * 2048:(tt + 1) * 2048], buf[:, :], AF.Square,
                accum_out=rcol[:, tt:tt + 1]), inc=sS, waits=[(sV, nn), (sG, ng)])
            ns = p.op("scalar", lambda e, tt=tt: e.activation(
                rcol[:, 2 + tt:3 + tt], rcol[:, tt:tt + 1], AF.Ln, scale=1.0 / 2048,
                bias=E.epsb[:, 2:3]), inc=sS)
            ns = p.op("scalar", lambda e, tt=tt: e.activation(
                rcol[:, 2 + tt:3 + tt], rcol[:, 2 + tt:3 + tt], AF.Exp, scale=-0.5), inc=sS)
            ng2 = p.op("vector", lambda e, buf=buf, tt=tt: e.tensor_scalar(
                gtok[:, tt * 2048:(tt + 1) * 2048], buf[:, :], rcol[:, 2 + tt:3 + tt],
                None, ALU.mult), inc=sV, waits=[(sS, ns)])
            g_done.append(ng2)
        abuf_free = g_done[0]
        bbuf_free = g_done[1]

        if E.dbg and c == 0:
            p.op("gpsimd", lambda e: e.dma_start(out=E.t_dsm[:, :], in_=dsm[0:32, :]),
                 inc=dG, inck=16, waits=[(sV, dt_small_v), (sS, dt_small_s)])
            p.op("gpsimd", lambda e: e.dma_start(out=E.t_u[:, :], in_=u_tok[:, :]),
                 inc=dG, inck=16, waits=[(sG, upool_done)])
            p.op("gpsimd", lambda e: e.dma_start(out=E.t_ud[:, :], in_=udec[:, :]),
                 inc=dG, inck=16, waits=[(sG, upool_done)])
            p.op("gpsimd", lambda e: e.dma_start(out=E.t_cbt[:, :], in_=CBT_sb[:, :]),
                 inc=dG, inck=16, waits=[(sV, cbt_done)])
            p.op("gpsimd", lambda e: e.dma_start(out=E.t_mexa[:, :], in_=mEXA[:, :]),
                 inc=dG, inck=16, waits=[(sV, mexa_ev[1])])
            p.op("gpsimd", lambda e: e.dma_start(out=E.t_msb[:, :], in_=Msb[:, 512:1024]),
                 inc=dG, inck=16, waits=[(sG, mprime_g[31])])
            p.op("gpsimd", lambda e: e.dma_start(out=E.t_g[:, :], in_=gtok[:, :]),
                 inc=dG, inck=16, waits=[(sV, g_done[1])])
            E.dbg_guard = p.cnt[dG]

        tr = 0
        for tt in range(2):
            for hk in range(16):
                tr = p.op("sync", lambda e, tt=tt, hk=hk: e.dma_start_transpose(
                    gfeat[:, hk * 256 + tt * 128: hk * 256 + (tt + 1) * 128],
                    gtok[:, tt * 2048 + hk * 128: tt * 2048 + (hk + 1) * 128]),
                    inc=dS, inck=16,
                    waits=[(sV, g_done[tt]), (sT, gfeat_free_t)])
        gtok_free_d = tr

        st_regs = []
        for st in range(2):
            psreg = pA if st == 0 else pB
            w = [(sG, upool_done), (dS, ld_xct[st]), (pA_free if st == 0 else pB_free)]
            for h in range(NH):
                n2 = p.op("tensor", lambda e, st=st, h=h, psreg=psreg: e.matmul(
                    psreg[:, h * 64:(h + 1) * 64], xctok[:, st * 2176 + 2048: st * 2176 + 2176],
                    udec[:, st * 2048 + h * 64: st * 2048 + (h + 1) * 64],
                    start=True, stop=True), inc=sT, waits=w if h == 0 else [])
                w = []
            st_regs.append(n2)
        udec_free_t = p.cnt[sT]
        bc_free_t = p.cnt[sT]
        e2t = p.cnt[sT]
        xctok_free = [(e2t, skip_g[0]), (e2t, skip_g[1])]

        if c == 0:
            p.op("vector", lambda e: e.tensor_copy(prev[:, :], pA[:, :]), inc=sV,
                 waits=[(sT, st_regs[0])])
            prev_ready_v = p.op("vector", lambda e: e.tensor_add(prev[:, :], pB[:, :], prev[:, :]),
                                inc=sV, waits=[(sT, st_regs[1])])
        else:
            ng = p.op("gpsimd", lambda e: e.tensor_tensor(prev[:, :], prev[:, :], mDECAY[:, :],
                                                          ALU.mult), inc=sG,
                      waits=[(sV, mdecay_done), (sT, prevbf_free_t)])
            mdecay_free_g = ng
            p.op("vector", lambda e: e.tensor_add(prev[:, :], pA[:, :], prev[:, :]), inc=sV,
                 waits=[(sT, st_regs[0]), (sG, ng)])
            prev_ready_v = p.op("vector", lambda e: e.tensor_add(prev[:, :], pB[:, :], prev[:, :]),
                                inc=sV, waits=[(sT, st_regs[1])])
        pA_free = (sV, prev_ready_v)
        pB_free = (sV, prev_ready_v)

        wout_evacs = []
        for mt in range(8):
            r = mt % 4
            if mt == 0:
                w = [pA_free, (dS, tr), (dG, nwl), (sV, wout_ready)]
            elif mt >= 4:
                w = [wout_evacs[mt - 4]]
            else:
                w = []
            for kc in range(16):
                p.op("tensor", lambda e, mt=mt, kc=kc, r=r: e.matmul(
                    pA[:, r * 512: r * 512 + 256],
                    w_out_sb[:, kc * 1024 + mt * 128: kc * 1024 + (mt + 1) * 128],
                    gfeat[:, kc * 256:(kc + 1) * 256],
                    start=(kc == 0), stop=(kc == 15)),
                    inc=(sT if kc == 15 else None),
                    waits=w if kc == 0 else [])
            g = p.cnt[sT]
            hb = mt % 2
            nn = p.op("vector", lambda e, mt=mt, r=r, hb=hb: e.tensor_add(
                hbuf[:, hb * 256:(hb + 1) * 256], pA[:, r * 512:r * 512 + 256],
                xn_c[:, mt * 256:(mt + 1) * 256]), inc=sV,
                waits=[(sT, g), (dS, ld_xn), (dG, hb_free[hb])])
            wout_evacs.append((sV, nn))
            hb_free[hb] = p.op("gpsimd", lambda e, c=c, mt=mt, hb=hb: e.dma_start(
                out=E.sp_h[mt * 128:(mt + 1) * 128, c * 256:(c + 1) * 256],
                in_=hbuf[:, hb * 256:(hb + 1) * 256]), inc=dG, inck=16,
                waits=[(sV, nn)])
        gfeat_free_t = p.cnt[sT]
        xn_free_v = wout_evacs[-1][1]
        pA_free = wout_evacs[-1]
        mapall_free = (scan_v[31], 0)
        E.p2_chunk_dG = getattr(E, "p2_chunk_dG", [0] * NCH)
        E.p2_chunk_dG[c] = p.cnt[dG]

    E.p2_done_dG = p.cnt[dG]
    if nphases == 2:
        p.op("sync", lambda e: e.nop(), waits=[(dG, E.p2_done_dG)])
    px.close()


def _phase3(E, nphases):
    nc, p = E.nc, E.p
    p.fence()
    dS, dG, sT, sV, sS, sG = E.dS, E.dG, E.sT, E.sV, E.sS, E.sG
    SL3 = 256
    px = ExitStack()
    sbl = lambda n, s, d: px.enter_context(nc.sbuf_tensor(n, s, d))
    wfc_sb = sbl("wfc_sb", [128, 8 * 4096], BF16)
    wproj_sb = sbl("wproj_sb", [128, 32 * 1024], BF16)
    h_sl = sbl("h_sl", [128, 8 * SL3], F32)
    hbf = sbl("hbf", [128, 8 * SL3], BF16)
    r_sb = sbl("r_sb", [128, 32 * SL3], BF16)
    hsq = r_sb
    ao_sb = sbl("ao_sb", [128, 8 * SL3], F32)
    x_sl = sbl("x_sl", [128, 8 * SL3], F32)
    x1_sb = sbl("x1_sb", [128, 8 * SL3], F32)
    rows3 = sbl("rows3", [1, 6 * SL3], F32)
    maps3 = sbl("maps3", [128, 3 * SL3], F32)
    pP3 = px.enter_context(nc.psum_tensor("pP3", [128, 2048], F32))
    pQ3 = px.enter_context(nc.psum_tensor("pQ3", [128, 1024], F32))
    pstA = px.enter_context(nc.psum_tensor("pstA3", [1, 256], F32))
    pstB = px.enter_context(nc.psum_tensor("pstB3", [1, 256], F32))

    mu_r = rows3[0:1, 0:SL3]
    var_r = rows3[0:1, SL3:2 * SL3]
    r1_r = rows3[0:1, 2 * SL3:3 * SL3]
    mur_r = rows3[0:1, 3 * SL3:4 * SL3]
    sc_r = rows3[0:1, 4 * SL3:5 * SL3]
    rstd2_r = rows3[0:1, 5 * SL3:6 * SL3]

    for kc in range(8):
        p.op("gpsimd", lambda e, kc=kc: e.dma_start(
            out=wfc_sb[:, kc * 4096:(kc + 1) * 4096],
            in_=E.w_fc[kc * 128:(kc + 1) * 128, :]), inc=dG, inck=16)
    for kc in range(32):
        p.op("gpsimd", lambda e, kc=kc: e.dma_start(
            out=wproj_sb[:, kc * 1024:(kc + 1) * 1024],
            in_=E.w_proj[kc * 128:(kc + 1) * 128, :]), inc=dG, inck=16)
    w_loaded = p.cnt[dG]

    hbuf_free = [0, 0]
    hbf_free_t = 0
    stats_free = 0
    rows_free_t = 0
    maps_free = 0
    r_free_t = 0
    pP_free = [None] * 4
    pQ_free = None
    ao_free = 0
    x1_stored = 0
    aocast_free_t = 0
    hsl_free_s = 0
    ao_sq_s = 0

    for s in range(16):
        ld_h = p.op("sync", lambda e, s=s: e.dma_start(
            out=h_sl.rearrange("p (g t) -> p g t", t=SL3),
            in_=E.sp_h[:, s * SL3:(s + 1) * SL3].rearrange("(g p) t -> p g t", p=128)),
            inc=dS, inck=16, waits=[(dG, E.p2_done_dG), (sV, ao_free), (sS, hsl_free_s)])
        ld_x = p.op("sync", lambda e, s=s: e.dma_start(
            out=x_sl.rearrange("p (g t) -> p g t", t=SL3),
            in_=E.xt[:, s * SL3:(s + 1) * SL3].rearrange("(g p) t -> p g t", p=128)),
            inc=dS, inck=16, waits=[(sV, x1_stored if False else 0), (sV, maps_free)])

        for ch in range(8):
            p.op("vector", lambda e, ch=ch: e.tensor_copy(
                hbf[:, ch * SL3:(ch + 1) * SL3], h_sl[:, ch * SL3:(ch + 1) * SL3]), inc=sV,
                waits=[(dS, ld_h), (sT, hbf_free_t)])
            p.op("scalar", lambda e, ch=ch: e.activation(
                hsq[:, ch * SL3:(ch + 1) * SL3], h_sl[:, ch * SL3:(ch + 1) * SL3], AF.Square),
                inc=sS, waits=[(dS, ld_h), (sT, r_free_t)])
        cast_v = p.cnt[sV]; cast_s = p.cnt[sS]
        for ch in range(8):
            p.op("tensor", lambda e, ch=ch: e.matmul(
                pstA[0:1, :], E.ones128b[:, 0:1], hsq[:, ch * SL3:(ch + 1) * SL3],
                start=(ch == 0), stop=(ch == 7)), inc=(sT if ch == 7 else None),
                waits=[(sS, cast_s), (sV, stats_free)] if ch == 0 else [])
        g1 = p.cnt[sT]
        nv = p.op("vector", lambda e: e.tensor_copy(sc_r, pstA[0:1, :]), inc=sV,
                  waits=[(sT, g1), (sT, rows_free_t)])
        ns = p.op("scalar", lambda e: e.activation(rstd2_r, sc_r, AF.Ln, scale=1.0 / 1024,
                                                   bias=E.epsb[0:1, 1:2]), inc=sS,
                  waits=[(sV, nv)])
        ns = p.op("scalar", lambda e: e.activation(rstd2_r, rstd2_r, AF.Exp, scale=-1.0), inc=sS)
        stats_evA = nv
        w = [(sS, ns)]
        if pQ_free is not None:
            w.append(pQ_free)
        n = p.op("tensor", lambda e: e.matmul(pQ3[:, 0:256], E.ones1f[0:1, :], rstd2_r,
                                              start=True, stop=True), inc=sT, waits=w)
        m_rstd2 = p.op("vector", lambda e: e.tensor_copy(maps3[:, 2 * SL3:3 * SL3], pQ3[:, 0:256]),
                       inc=sV, waits=[(sT, n)])
        pQ_free = (sV, m_rstd2)

        for mt in range(32):
            r = mt % 4
            w = [(sV, cast_v)] if mt == 0 else []
            if pP_free[r] is not None:
                w.append(pP_free[r])
            for kc in range(8):
                p.op("tensor", lambda e, mt=mt, kc=kc, r=r: e.matmul(
                    pP3[:, r * 512:r * 512 + 256],
                    wfc_sb[:, kc * 4096 + mt * 128: kc * 4096 + (mt + 1) * 128],
                    hbf[:, kc * SL3:(kc + 1) * SL3],
                    start=(kc == 0), stop=(kc == 7)), inc=(sT if kc == 7 else None),
                    waits=(w + [(dG, w_loaded)]) if kc == 0 else [])
            g = p.cnt[sT]
            n = p.op("scalar", lambda e, mt=mt, r=r: e.activation(
                r_sb[:, mt * SL3:(mt + 1) * SL3], pP3[:, r * 512:r * 512 + 256], AF.Relu),
                inc=sS, waits=[(sT, g)])
            pP_free[r] = (sS, n)
        hbf_free_t = p.cnt[sT]
        relu_s = p.cnt[sS]
        for q8 in range(8):
            p.op("vector", lambda e, q8=q8: e.tensor_tensor(
                r_sb[:, q8 * 1024:(q8 + 1) * 1024], r_sb[:, q8 * 1024:(q8 + 1) * 1024],
                r_sb[:, q8 * 1024:(q8 + 1) * 1024], ALU.mult), inc=sV,
                waits=[(sS, relu_s)] if q8 == 0 else [])
        sq_v = p.cnt[sV]

        ao_evs = []
        for mt in range(8):
            r = mt % 4
            w = [(sV, sq_v)] if mt == 0 else []
            if pP_free[r] is not None:
                w.append(pP_free[r])
            for kc in range(32):
                p.op("tensor", lambda e, mt=mt, kc=kc, r=r: e.matmul(
                    pP3[:, r * 512:r * 512 + 256],
                    wproj_sb[:, kc * 1024 + mt * 128: kc * 1024 + (mt + 1) * 128],
                    r_sb[:, kc * SL3:(kc + 1) * SL3],
                    start=(kc == 0), stop=(kc == 31)), inc=(sT if kc == 31 else None),
                    waits=w if kc == 0 else [])
            g = p.cnt[sT]
            nn = p.op("vector", lambda e, mt=mt, r=r: e.tensor_tensor(
                ao_sb[:, mt * SL3:(mt + 1) * SL3], pP3[:, r * 512:r * 512 + 256],
                maps3[:, 2 * SL3:3 * SL3], ALU.mult), inc=sV,
                waits=[(sT, g), (sV, m_rstd2), (sS, ao_sq_s)] if mt == 0 else [(sT, g)])
            nn = p.op("vector", lambda e, mt=mt: e.tensor_add(
                ao_sb[:, mt * SL3:(mt + 1) * SL3], ao_sb[:, mt * SL3:(mt + 1) * SL3],
                h_sl[:, mt * SL3:(mt + 1) * SL3]), inc=sV)
            ao_evs.append((sV, nn))
            pP_free[r] = (sV, nn)
        r_free_t = p.cnt[sT]
        ao_v = p.cnt[sV]
        ao_free = ao_v

        for ch in range(8):
            p.op("vector", lambda e, ch=ch: e.tensor_copy(
                r_sb[:, ch * SL3:(ch + 1) * SL3], ao_sb[:, ch * SL3:(ch + 1) * SL3]), inc=sV,
                waits=[(sV, ao_v), (sT, r_free_t)] if ch == 0 else [])
            p.op("scalar", lambda e, ch=ch: e.activation(
                r_sb[:, 2048 + ch * SL3:2048 + (ch + 1) * SL3],
                ao_sb[:, ch * SL3:(ch + 1) * SL3], AF.Square), inc=sS,
                waits=[(sV, ao_v), (sT, r_free_t)] if ch == 0 else [])
        c2v = p.cnt[sV]; c2s = p.cnt[sS]
        for ch in range(8):
            p.op("tensor", lambda e, ch=ch: e.matmul(
                pstA[0:1, :], E.ones128b[:, 0:1], r_sb[:, ch * SL3:(ch + 1) * SL3],
                start=(ch == 0), stop=(ch == 7)), inc=(sT if ch == 7 else None),
                waits=[(sV, c2v), (sV, stats_evA)] if ch == 0 else [])
        for ch in range(8):
            p.op("tensor", lambda e, ch=ch: e.matmul(
                pstB[0:1, :], E.ones128b[:, 0:1], r_sb[:, 2048 + ch * SL3:2048 + (ch + 1) * SL3],
                start=(ch == 0), stop=(ch == 7)), inc=(sT if ch == 7 else None),
                waits=[(sS, c2s)] if ch == 0 else [])
        g2 = p.cnt[sT]
        aocast_free_t = g2
        r_free_t = g2
        ao_sq_s = c2s
        hsl_free_s = c2s
        p.op("vector", lambda e: e.tensor_scalar(mu_r, pstA[0:1, :], 1.0 / 1024, None, ALU.mult),
             inc=sV, waits=[(sT, g2)])
        p.op("vector", lambda e: e.tensor_scalar(var_r, pstB[0:1, :], 1.0 / 1024, None, ALU.mult),
             inc=sV)
        stats_free = p.cnt[sV]
        ns = p.op("scalar", lambda e: e.activation(sc_r, mu_r, AF.Square), inc=sS,
                  waits=[(sV, stats_free)])
        nv = p.op("vector", lambda e: e.tensor_sub(var_r, var_r, sc_r), inc=sV,
                  waits=[(sS, ns)])
        ns = p.op("scalar", lambda e: e.activation(sc_r, var_r, AF.Ln, bias=E.epsb[0:1, 0:1]),
                  inc=sS, waits=[(sV, nv)])
        ns = p.op("scalar", lambda e: e.activation(r1_r, sc_r, AF.Exp, scale=-0.5), inc=sS)
        nv = p.op("vector", lambda e: e.tensor_mul(mur_r, mu_r, r1_r), inc=sV, waits=[(sS, ns)])
        rows_v = p.cnt[sV]
        prev_ev = None
        for i, row in enumerate([r1_r, mur_r]):
            w = [(sV, rows_v), pQ_free] if i == 0 else [(sV, prev_ev)]
            n = p.op("tensor", lambda e, row=row: e.matmul(
                pQ3[:, 0:256], E.ones1f[0:1, :], row, start=True, stop=True), inc=sT, waits=w)
            prev_ev = p.op("vector", lambda e, i=i: e.tensor_copy(
                maps3[:, i * SL3:(i + 1) * SL3], pQ3[:, 0:256]), inc=sV, waits=[(sT, n)])
        maps_v = p.cnt[sV]
        rows_free_t = p.cnt[sT]
        pQ_free = (sV, maps_v)
        maps_free_new = []
        for ch in range(8):
            p.op("vector", lambda e, ch=ch: e.tensor_tensor(
                x1_sb[:, ch * SL3:(ch + 1) * SL3], ao_sb[:, ch * SL3:(ch + 1) * SL3],
                maps3[:, 0:SL3], ALU.mult), inc=sV,
                waits=[(sV, maps_v), (dG, x1_stored)] if ch == 0 else [])
            p.op("vector", lambda e, ch=ch: e.tensor_sub(
                x1_sb[:, ch * SL3:(ch + 1) * SL3], x1_sb[:, ch * SL3:(ch + 1) * SL3],
                maps3[:, SL3:2 * SL3]), inc=sV)
            p.op("vector", lambda e, ch=ch: e.tensor_add(
                x1_sb[:, ch * SL3:(ch + 1) * SL3], x1_sb[:, ch * SL3:(ch + 1) * SL3],
                x_sl[:, ch * SL3:(ch + 1) * SL3]), inc=sV,
                waits=[(dS, ld_x)] if ch == 0 else [])
        x1_v = p.cnt[sV]
        maps_free = x1_v
        x1_stored = p.op("gpsimd", lambda e, s=s: e.dma_start(
            out=E.sp_x1[:, s * SL3:(s + 1) * SL3].rearrange("(g p) t -> p g t", p=128),
            in_=x1_sb.rearrange("p (g t) -> p g t", t=SL3)), inc=dG, inck=16,
            waits=[(sV, x1_v)])

    E.p3_done_dG = p.cnt[dG]
    if nphases == 3:
        p.op("sync", lambda e: e.nop(), waits=[(dG, E.p3_done_dG)])
    px.close()


def _phase4(E, nphases):
    nc, p = E.nc, E.p
    p.fence()
    dS, dG, sT, sV, sS, sG = E.dS, E.dG, E.sT, E.sV, E.sS, E.sG
    px = ExitStack()
    sbl = lambda n, s, d: px.enter_context(nc.sbuf_tensor(n, s, d))
    xn2 = sbl("xn2", [128, 8 * 4096], BF16)
    xn2lo = sbl("xn2lo", [128, 8 * 4096], BF16)

    pxa = ExitStack()
    sba = lambda n, s, d: pxa.enter_context(nc.sbuf_tensor(n, s, d))
    x1_sl = sba("x1_sl", [128, 8 * SL], F32)
    x1bf = sba("x1bf", [128, 8 * SL], BF16)
    x1sq = sba("x1sq", [128, 8 * SL], BF16)
    rows4 = sba("rows4", [1, 5 * SL], F32)
    maps4 = sba("maps4", [128, 2 * SL], F32)
    tch4 = sba("tch4", [128, SL], F32)
    xn2f = sba("xn2f", [128, SL], F32)
    p4a = pxa.enter_context(nc.psum_tensor("p4a", [128, 512], F32))
    pstA = pxa.enter_context(nc.psum_tensor("pstA4", [1, 512], F32))
    pstB = pxa.enter_context(nc.psum_tensor("pstB4", [1, 512], F32))
    mu_r = rows4[0:1, 0:SL]; var_r = rows4[0:1, SL:2 * SL]
    r1_r = rows4[0:1, 2 * SL:3 * SL]; mur_r = rows4[0:1, 3 * SL:4 * SL]
    sc_r = rows4[0:1, 4 * SL:5 * SL]

    cast_free_t = 0
    stats_free = 0
    rows_free_t = 0
    maps_free = 0
    pm_free = None
    for s in range(NSL):
        ld = p.op("sync", lambda e, s=s: e.dma_start(
            out=x1_sl.rearrange("p (g t) -> p g t", t=SL),
            in_=E.sp_x1[:, s * SL:(s + 1) * SL].rearrange("(g p) t -> p g t", p=128)),
            inc=dS, inck=16, waits=[(dG, E.p3_done_dG), (sV, maps_free)])
        for ch in range(8):
            p.op("vector", lambda e, ch=ch: e.tensor_copy(
                x1bf[:, ch * SL:(ch + 1) * SL], x1_sl[:, ch * SL:(ch + 1) * SL]), inc=sV,
                waits=[(dS, ld), (sT, cast_free_t)])
            p.op("scalar", lambda e, ch=ch: e.activation(
                x1sq[:, ch * SL:(ch + 1) * SL], x1_sl[:, ch * SL:(ch + 1) * SL], AF.Square),
                inc=sS, waits=[(dS, ld), (sT, cast_free_t)])
        cast_v = p.cnt[sV]; cast_s = p.cnt[sS]
        for ch in range(8):
            p.op("tensor", lambda e, ch=ch: e.matmul(
                pstA[0:1, :], E.ones128b[:, 0:1], x1bf[:, ch * SL:(ch + 1) * SL],
                start=(ch == 0), stop=(ch == 7)), inc=(sT if ch == 7 else None),
                waits=[(sV, cast_v), (sV, stats_free)] if ch == 0 else [])
        for ch in range(8):
            p.op("tensor", lambda e, ch=ch: e.matmul(
                pstB[0:1, :], E.ones128b[:, 0:1], x1sq[:, ch * SL:(ch + 1) * SL],
                start=(ch == 0), stop=(ch == 7)), inc=(sT if ch == 7 else None),
                waits=[(sS, cast_s)] if ch == 0 else [])
        g1 = p.cnt[sT]
        cast_free_t = g1
        p.op("vector", lambda e: e.tensor_scalar(mu_r, pstA[0:1, :], 1.0 / 1024, None, ALU.mult),
             inc=sV, waits=[(sT, g1), (sT, rows_free_t)])
        p.op("vector", lambda e: e.tensor_scalar(var_r, pstB[0:1, :], 1.0 / 1024, None, ALU.mult),
             inc=sV)
        stats_free = p.cnt[sV]
        ns = p.op("scalar", lambda e: e.activation(sc_r, mu_r, AF.Square), inc=sS,
                  waits=[(sV, stats_free), (sT, rows_free_t)])
        nv = p.op("vector", lambda e: e.tensor_sub(var_r, var_r, sc_r), inc=sV, waits=[(sS, ns)])
        ns = p.op("scalar", lambda e: e.activation(sc_r, var_r, AF.Ln, bias=E.epsb[0:1, 0:1]),
                  inc=sS, waits=[(sV, nv)])
        ns = p.op("scalar", lambda e: e.activation(r1_r, sc_r, AF.Exp, scale=-0.5), inc=sS)
        rows_v = p.cnt[sV]; rows_s = p.cnt[sS]
        prev_ev = None
        for i, row in enumerate([mu_r, r1_r]):
            w = [(sS, rows_s), (sV, rows_v)]
            if i == 0:
                w += [pm_free, (sV, maps_free)]
            else:
                w.append((sV, prev_ev))
            n = p.op("tensor", lambda e, row=row: e.matmul(
                p4a[:, 0:512], E.ones1f[0:1, :], row, start=True, stop=True), inc=sT, waits=w)
            prev_ev = p.op("vector", lambda e, i=i: e.tensor_copy(
                maps4[:, i * SL:(i + 1) * SL], p4a[:, 0:512]), inc=sV, waits=[(sT, n)])
        maps_v = p.cnt[sV]
        rows_free_t = p.cnt[sT]
        pm_free = (sV, maps_v)
        for ch in range(8):
            p.op("vector", lambda e, ch=ch: e.tensor_sub(
                tch4[:, :], x1_sl[:, ch * SL:(ch + 1) * SL], maps4[:, 0:SL]), inc=sV,
                waits=[(sV, maps_v)] if ch == 0 else [])
            p.op("vector", lambda e, ch=ch: e.tensor_mul(
                xn2f[:, :], tch4[:, :], maps4[:, SL:2 * SL]), inc=sV)
            p.op("vector", lambda e, ch=ch, s=s: e.tensor_copy(
                xn2[:, ch * 4096 + s * SL: ch * 4096 + (s + 1) * SL], xn2f[:, :]), inc=sV)
            p.op("vector", lambda e, ch=ch, s=s: e.tensor_sub(
                xn2lo[:, ch * 4096 + s * SL: ch * 4096 + (s + 1) * SL], xn2f[:, :],
                xn2[:, ch * 4096 + s * SL: ch * 4096 + (s + 1) * SL]), inc=sV)
        maps_free = p.cnt[sV]
    xn2_done_v = p.cnt[sV]
    pxa.close()
    p.fence()

    w1_sb = sbl("w1_sb", [128, 8 * 1024], BF16)
    w2_sb = sbl("w2_sb", [128, 8 * 1024], BF16)
    nwT_sb = sbl("nwT_sb", [128, 8 * 15], BF16)
    nwTlo_sb = sbl("nwTlo_sb", [128, 8 * 15], BF16)
    nwTst = sbl("nwTst", [128, 8 * 15], F32)
    nb_sb = sbl("nb_sb", [15, 1], F32)
    cbit = sbl("cbit", [16, 512], BF16)
    b1c_sb = sbl("b1c_sb", [128, 8], F32)
    b2_sb = sbl("b2_sb", [16, 1024], BF16)
    hid = sbl("hid", [128, 8 * 512], BF16)
    out_sb = sbl("out_sb", [128, 8 * 512], BF16)
    selT = sbl("selT", [16, 512], BF16)
    LEAF = sbl("LEAF", [128, 512], BF16)
    pF = px.enter_context(nc.psum_tensor("pF", [128, 2048], F32))
    pG = px.enter_context(nc.psum_tensor("pG", [128, 1536], F32))

    wload_free = [0, 0, 0]
    nwv_prev = 0
    for bi, br in enumerate("qkv"):
        ff = E.fff[br]
        wl = []
        wld = [(sT, wload_free[0])]
        for kc in range(8):
            wl.append(p.op("gpsimd", lambda e, kc=kc, ff=ff: e.dma_start(
                out=w1_sb[:, kc * 1024:(kc + 1) * 1024],
                in_=ff["w1"][kc * 128:(kc + 1) * 128, :]), inc=dG, inck=16, waits=wld))
            wld = []
        for kc in range(8):
            wl.append(p.op("gpsimd", lambda e, kc=kc, ff=ff: e.dma_start(
                out=w2_sb[:, kc * 1024:(kc + 1) * 1024],
                in_=ff["w2"][kc * 128:(kc + 1) * 128, :]), inc=dG, inck=16))
        for kc in range(8):
            wl.append(p.op("gpsimd", lambda e, kc=kc, ff=ff: e.dma_start(
                out=nwTst[:, kc * 15:(kc + 1) * 15],
                in_=ff["nwT"][kc * 128:(kc + 1) * 128, :]), inc=dG, inck=16,
                waits=[(sV, nwv_prev)] if kc == 0 else []))
        wl.append(p.op("gpsimd", lambda e, ff=ff: e.dma_start(out=nb_sb[:, :], in_=ff["nb"][:, :]),
                       inc=dG, inck=16))
        wl.append(p.op("gpsimd", lambda e, ff=ff: e.dma_start(out=b1c_sb[:, :], in_=ff["b1c"][:, :]),
                       inc=dG, inck=16))
        wl.append(p.op("gpsimd", lambda e, ff=ff: e.dma_start(out=b2_sb[:, :], in_=ff["b2"][:, :]),
                       inc=dG, inck=16))
        w_ld = p.cnt[dG]
        p.op("vector", lambda e: e.tensor_copy(nwT_sb[:, :], nwTst[:, :]), inc=sV,
             waits=[(dG, w_ld), (sT, wload_free[0])])
        nwv = p.op("vector", lambda e: e.tensor_sub(nwTlo_sb[:, :], nwTst[:, :], nwT_sb[:, :]),
                   inc=sV)
        nwv_prev = nwv

        pF_free = [None] * 4
        pG_free = [None] * 3
        sel_free_t = 0
        leaf_free_v = 0
        hid_free_t = 0
        out_stored = 0
        for s in range(NSL):
            w = [(sV, xn2_done_v), (dG, w_ld), (sV, nwv)]
            if pF_free[0] is not None:
                w.append(pF_free[0])
            first = True
            for wmat, xoff in ((nwT_sb, 0), (nwT_sb, 1), (nwTlo_sb, 0)):
                xsrc = xn2 if xoff == 0 else xn2lo
                for kc in range(8):
                    last = (not first) and (wmat is nwTlo_sb) and kc == 7
                    p.op("tensor", lambda e, s=s, kc=kc, wmat=wmat, xsrc=xsrc, first=first, last=last: e.matmul(
                        pF[0:15, 0:512],
                        wmat[:, kc * 15:(kc + 1) * 15],
                        xsrc[:, kc * 4096 + s * 512: kc * 4096 + (s + 1) * 512],
                        start=first, stop=last), inc=(sT if last else None),
                        waits=w if first else [])
                    first = False
            g = p.cnt[sT]
            tapw = [(dG, getattr(E, "p4_tap", 0))] if (E.dbg and br == "q" and s == 1) else []
            ncb = p.op("vector", lambda e: e.tensor_scalar(
                cbit[0:15, :], pF[0:15, 0:512], nb_sb[0:15, 0:1], 0.0, ALU.add, ALU.is_ge),
                inc=sV, waits=[(sT, g)] + tapw)
            pF_free[0] = (sV, ncb)
            w = [(sV, ncb)]
            if pF_free[1] is not None:
                w.append(pF_free[1])
            p.op("tensor", lambda e: e.matmul(
                pF[0:16, 512:1024], E.Abit[0:15, :], cbit[0:15, :],
                start=True, stop=False), inc=None, waits=w)
            n = p.op("tensor", lambda e: e.matmul(
                pF[0:16, 512:1024], E.negrow[0:1, :], E.ones512b[0:1, :],
                start=False, stop=True), inc=sT)
            nst = p.op("vector", lambda e: e.tensor_scalar(
                selT[0:16, :], pF[0:16, 512:1024], 4.0, None, ALU.is_equal), inc=sV,
                waits=[(sT, n)])
            pF_free[1] = (sV, nst)
            sel_free_t = 0
            w = [(sV, nst), (sV, leaf_free_v)]
            if pG_free[0] is not None:
                w.append(pG_free[0])
            n = p.op("tensor", lambda e: e.matmul(
                pG[:, 0:512], E.IDX16[:, :], selT[0:16, :], start=True, stop=True), inc=sT,
                waits=w)
            nlf = p.op("vector", lambda e: e.tensor_copy(LEAF[:, :], pG[:, 0:512]), inc=sV,
                       waits=[(sT, n)])
            pG_free[0] = (sV, nlf)
            mask_v = []
            for hk in range(8):
                r = hk % 3
                w = []
                if hk == 0:
                    w = [(sV, xn2_done_v)]
                if pG_free[r] is not None:
                    w.append(pG_free[r])
                for kc in range(8):
                    p.op("tensor", lambda e, s=s, hk=hk, kc=kc, r=r: e.matmul(
                        pG[:, r * 512:(r + 1) * 512],
                        w1_sb[:, kc * 1024 + hk * 128: kc * 1024 + (hk + 1) * 128],
                        xn2[:, kc * 4096 + s * 512: kc * 4096 + (s + 1) * 512],
                        start=(kc == 0), stop=(kc == 7)), inc=(sT if kc == 7 else None),
                        waits=w if kc == 0 else [])
                g = p.cnt[sT]
                ns = p.op("scalar", lambda e, hk=hk, r=r: e.activation(
                    hid[:, hk * 512:(hk + 1) * 512], pG[:, r * 512:(r + 1) * 512], AF.Relu,
                    bias=b1c_sb[:, hk:hk + 1]), inc=sS,
                    waits=[(sT, g), (sT, hid_free_t)] if hk == 0 else [(sT, g)])
                nm = p.op("vector", lambda e, hk=hk: e.scalar_tensor_tensor(
                    hid[:, hk * 512:(hk + 1) * 512], LEAF[:, :], E.leafof[:, hk:hk + 1],
                    hid[:, hk * 512:(hk + 1) * 512], ALU.is_equal, ALU.mult), inc=sV,
                    waits=[(sS, ns), (sV, nlf)])
                mask_v.append(nm)
                pG_free[r] = (sV, nm)
            hid_v = p.cnt[sV]
            leaf_free_v = hid_v
            if br in "qk":
                spill = E.sp_q if br == "q" else E.sp_k
                for mt in range(8):
                    r = mt % 4
                    w = [(sV, hid_v)] if mt == 0 else []
                    if pF_free[r] is not None:
                        w.append(pF_free[r])
                    for kc in range(8):
                        p.op("tensor", lambda e, mt=mt, kc=kc, r=r: e.matmul(
                            pF[:, r * 512:(r + 1) * 512],
                            w2_sb[:, kc * 1024 + mt * 128: kc * 1024 + (mt + 1) * 128],
                            hid[:, kc * 512:(kc + 1) * 512],
                            start=(kc == 0), stop=False), inc=None,
                            waits=w if kc == 0 else [])
                    p.op("tensor", lambda e, mt=mt, r=r: e.matmul(
                        pF[:, r * 512:(r + 1) * 512], b2_sb[:, mt * 128:(mt + 1) * 128],
                        selT[0:16, :], start=False, stop=True), inc=sT)
                    g = p.cnt[sT]
                    nn = p.op("vector", lambda e, mt=mt, r=r: e.tensor_copy(
                        out_sb[:, mt * 512:(mt + 1) * 512], pF[:, r * 512:(r + 1) * 512]),
                        inc=sV, waits=[(sT, g), (dG, out_stored)] if mt == 0 else [(sT, g)])
                    pF_free[r] = (sV, nn)
                ev = p.cnt[sV]
                hid_free_t = p.cnt[sT]
                out_stored = p.op("gpsimd", lambda e, s=s, spill=spill: e.dma_start(
                    out=spill[:, s * 512:(s + 1) * 512].rearrange("(g p) t -> p g t", p=128),
                    in_=out_sb.rearrange("p (g t) -> p g t", t=512)), inc=dG, inck=16,
                    waits=[(sV, ev)])
            else:
                for tt in range(4):
                    for dv in range(2):
                        r = (tt * 2 + dv) % 4
                        w = [(sV, hid_v)] if (tt == 0 and dv == 0) else []
                        if pF_free[r] is not None:
                            w.append(pF_free[r])
                        for kc in range(8):
                            p.op("tensor", lambda e, tt=tt, dv=dv, kc=kc, r=r: e.matmul(
                                pF[:, r * 512:(r + 1) * 512],
                                hid[:, kc * 512 + tt * 128: kc * 512 + (tt + 1) * 128],
                                w2_sb[:, kc * 1024 + dv * 512: kc * 1024 + (dv + 1) * 512],
                                start=(kc == 0), stop=False), inc=None,
                                waits=w if kc == 0 else [])
                        p.op("tensor", lambda e, tt=tt, dv=dv, r=r: e.matmul(
                            pF[:, r * 512:(r + 1) * 512], selT[0:16, tt * 128:(tt + 1) * 128],
                            b2_sb[:, dv * 512:(dv + 1) * 512], start=False, stop=True), inc=sT)
                        g = p.cnt[sT]
                        nn = p.op("vector", lambda e, tt=tt, dv=dv, r=r: e.tensor_copy(
                            out_sb[:, tt * 1024 + dv * 512: tt * 1024 + (dv + 1) * 512],
                            pF[:, r * 512:(r + 1) * 512]), inc=sV,
                            waits=[(sT, g), (dG, out_stored)] if (tt == 0 and dv == 0) else [(sT, g)])
                        pF_free[r] = (sV, nn)
                ev = p.cnt[sV]
                hid_free_t = p.cnt[sT]
                out_stored = p.op("gpsimd", lambda e, s=s: e.dma_start(
                    out=E.sp_v[s * 512:(s + 1) * 512, :].rearrange("(t p) d -> p t d", p=128),
                    in_=out_sb.rearrange("p (t d) -> p t d", d=1024)), inc=dG, inck=16,
                    waits=[(sV, ev)])
            if E.dbg and br == "q" and s == 0:
                p.op("gpsimd", lambda e: e.dma_start(
                    out=E.t_xn2.rearrange("p (c t) -> p c t", t=512),
                    in_=xn2.rearrange("p (c t) -> p c t", t=4096)[:, :, 0:512]),
                    inc=dG, inck=16, waits=[(sV, xn2_done_v)])
                p.op("gpsimd", lambda e: e.dma_start(out=E.t_selT[:, :], in_=selT[0:16, :]),
                     inc=dG, inck=16, waits=[(sV, nst)])
                p.op("gpsimd", lambda e: e.dma_start(out=E.t_leaf[:, :], in_=LEAF[:, :]),
                     inc=dG, inck=16, waits=[(sV, nlf)])
                p.op("gpsimd", lambda e: e.dma_start(out=E.t_hid[:, :], in_=hid[:, :]),
                     inc=dG, inck=16, waits=[(sV, hid_v)])
                p.op("gpsimd", lambda e: e.dma_start(out=E.t_fout[:, :], in_=out_sb[:, :]),
                     inc=dG, inck=16, waits=[(sV, ev)])
                E.p4_tap = p.cnt[dG]
        wload_free[0] = p.cnt[sT]

    E.p4_done_dG = p.cnt[dG]
    if nphases == 4:
        p.op("sync", lambda e: e.nop(), waits=[(dG, E.p4_done_dG)])
    px.close()


def _phase5(E):
    nc, p = E.nc, E.p
    p.fence()
    dS, dG, sT, sV, sS, sG = E.dS, E.dG, E.sT, E.sV, E.sS, E.sG
    px = ExitStack()
    sbl = lambda n, s, d: px.enter_context(nc.sbuf_tensor(n, s, d))
    k_sb = sbl("k_sb", [128, 8 * 4096], BF16)
    v_sb = sbl("v_sb", [128, 32 * 1024], BF16)
    q_sb = sbl("q_sb", [128, 8 * 256], BF16)
    A_sb = sbl("A_sb", [128, 4096], F32)
    P_sb = sbl("P_sb", [128, 4096], BF16)
    PT_sb = sbl("PT_sb", [128, 32 * 256], BF16)
    u_sb = sbl("u_sb", [128, 8 * 512], F32)
    rows5 = sbl("rows5", [1, 4 * 512], F32)
    maps5 = sbl("maps5", [128, 2 * 512], F32)
    o_sb = sbl("o_sb", [128, 2 * 512], F32)
    dcol = sbl("dcol", [128, 4], F32)
    dbf = sbl("dbf", [128, 4], BF16)
    mcol = sbl("mcol", [128, 2], F32)
    pX = px.enter_context(nc.psum_tensor("pX", [128, 2048], F32))
    pY = px.enter_context(nc.psum_tensor("pY", [128, 2048], F32))

    sums_r = rows5[0:1, 0:512]; var_r = rows5[0:1, 512:1024]
    mur_r = rows5[0:1, 512:1024]
    drow_r = rows5[0:1, 1024:1536]
    rstd_r = rows5[0:1, 1024:1536]
    sc_r = rows5[0:1, 1536:2048]

    kv = []
    for kc in range(8):
        kv.append(p.op("sync", lambda e, kc=kc: e.dma_start(
            out=k_sb[:, kc * 4096:(kc + 1) * 4096],
            in_=E.sp_k[kc * 128:(kc + 1) * 128, :]), inc=dS, inck=16,
            waits=[(dG, E.p4_done_dG)]))
    for i in range(32):
        kv.append(p.op("sync", lambda e, i=i: e.dma_start(
            out=v_sb[:, i * 1024:(i + 1) * 1024],
            in_=E.sp_v[i * 128:(i + 1) * 128, :]), inc=dS, inck=16))
    kv_ld = p.cnt[dS]

    pX_free = [None] * 4
    pY_free = [None] * 4
    A_guard_dG = 0
    pst_free = None
    q_free_t = 0
    A_free_s = 0
    P_free_d = 0
    PT_free_t = 0
    u_free_v = 0
    P_guard_t = 0
    rows_free_t = 0
    maps_free_v = 0
    o_stored = [0, 0]
    mcol_free_s = 0
    dcol_free_v = 0

    for qb in range(NSL):
        exp_s = [0, 0, 0, 0]
        tr_d = [0, 0, 0, 0]
        pv_t = [0, 0]
        for half in range(2):
            ld_q = p.op("sync", lambda e, qb=qb, half=half: e.dma_start(
                out=q_sb.rearrange("p (g t) -> p g t", t=256),
                in_=E.sp_q[:, qb * 512 + half * 256:qb * 512 + (half + 1) * 256]
                    .rearrange("(g p) t -> p g t", p=128)),
                inc=dS, inck=16, waits=[(sT, q_free_t)])
            for sub in range(2):
                qt = half * 2 + sub
                for ks in range(8):
                    r = ks % 4
                    w = [(dS, ld_q), (dS, kv_ld)]
                    if qt == 0 and ks == 0:
                        w.append((sS, A_free_s))
                    if pX_free[r] is not None:
                        w.append(pX_free[r])
                    for kc in range(8):
                        p.op("tensor", lambda e, qb=qb, sub=sub, ks=ks, kc=kc, r=r: e.matmul(
                            pX[:, r * 512:(r + 1) * 512],
                            q_sb[:, kc * 256 + sub * 128:kc * 256 + (sub + 1) * 128],
                            k_sb[:, kc * 4096 + ks * 512:kc * 4096 + (ks + 1) * 512],
                            start=(kc == 0), stop=(kc == 7)), inc=(sT if kc == 7 else None),
                            waits=w if kc == 0 else [])
                    g = p.cnt[sT]
                    nn = p.op("vector", lambda e, ks=ks, r=r: e.tensor_copy(
                        A_sb[:, ks * 512:(ks + 1) * 512], pX[:, r * 512:(r + 1) * 512]),
                        inc=sV, waits=[(sT, g), (sS, A_free_s), (dG, A_guard_dG)]
                        if ks == 0 else [(sT, g)])
                    pX_free[r] = (sV, nn)
                a_v = p.cnt[sV]
                nn = p.op("vector", lambda e: e.reduce_max(
                    mcol[:, 0:1], A_sb[:, :], axis=AXX, negate=True), inc=sV,
                    waits=[(sS, mcol_free_s)])
                p.op("vector", lambda e: e.tensor_copy(mcol[:, 1:2], mcol[:, 0:1]), inc=sV)
                p.op("vector", lambda e: e.tensor_copy(mcol[:, 1:2], mcol[:, 0:1]), inc=sV)
                nn = p.op("vector", lambda e: e.tensor_scalar(
                    A_sb[:, :], A_sb[:, :], mcol[:, 0:1], SCALE_ATT, ALU.add, ALU.mult),
                    inc=sV)
                ns = p.op("scalar", lambda e, qt=qt: e.activation(
                    P_sb[:, :], A_sb[:, :], AF.Exp,
                    accum_out=dcol[:, qt:qt + 1]), inc=sS,
                    waits=[(sV, nn), (dS, P_free_d), (sV, dcol_free_v), (sT, P_guard_t)])
                exp_s[qt] = ns
                A_free_s = ns
                mcol_free_s = ns
                if E.dbg and qb == 0 and qt == 0:
                    A_guard_dG = p.op("gpsimd", lambda e: e.dma_start(
                        out=E.t_A5[:, :], in_=A_sb[:, :]), inc=dG, inck=16,
                        waits=[(sV, a_v), (sS, ns)])
                tr = 0
                for kc in range(32):
                    w = [(sS, ns)]
                    if kc == 0:
                        w.append((sT, PT_free_t))
                    tr = p.op("sync", lambda e, kc=kc, sub=sub: e.dma_start_transpose(
                        PT_sb[:, kc * 256 + sub * 128: kc * 256 + sub * 128 + 128],
                        P_sb[:, kc * 128:(kc + 1) * 128]), inc=dS, inck=16, waits=w)
                tr_d[qt] = tr
                P_free_d = tr
            for dv in range(8):
                r = dv % 4
                w = [(dS, tr_d[half * 2]), (dS, tr_d[half * 2 + 1]), (dS, kv_ld)]
                if pY_free[r] is not None:
                    w.append(pY_free[r])
                for kc in range(32):
                    p.op("tensor", lambda e, dv=dv, kc=kc, r=r: e.matmul(
                        pY[:, r * 512:r * 512 + 256],
                        v_sb[:, kc * 1024 + dv * 128: kc * 1024 + (dv + 1) * 128],
                        PT_sb[:, kc * 256:(kc + 1) * 256],
                        start=(kc == 0), stop=(kc == 31)), inc=(sT if kc == 31 else None),
                        waits=w if kc == 0 else [])
                g = p.cnt[sT]
                nn = p.op("vector", lambda e, dv=dv, half=half, r=r: e.tensor_copy(
                    u_sb[:, dv * 512 + half * 256: dv * 512 + half * 256 + 256],
                    pY[:, r * 512:r * 512 + 256]), inc=sV,
                    waits=[(sT, g), (sV, u_free_v)] if dv == 0 else [(sT, g)])
                pY_free[r] = (sV, nn)
            pv_t[half] = p.cnt[sT]
            PT_free_t = pv_t[half]
            q_free_t = p.cnt[sT]
        u_v = p.cnt[sV]

        for ch in range(8):
            p.op("scalar", lambda e, ch=ch: e.activation(
                P_sb[:, ch * 512:(ch + 1) * 512], u_sb[:, ch * 512:(ch + 1) * 512], AF.Square),
                inc=sS, waits=[(sV, u_v), (dS, P_free_d)] if ch == 0 else [])
        usq_s = p.cnt[sS]
        for ch in range(8):
            w = []
            if ch == 0:
                w = [(sS, usq_s)]
                if pst_free is not None:
                    w.append(pst_free)
            p.op("tensor", lambda e, ch=ch: e.matmul(
                pX[0:1, 0:512], E.ones128b[:, 0:1], P_sb[:, ch * 512:(ch + 1) * 512],
                start=(ch == 0), stop=(ch == 7)), inc=(sT if ch == 7 else None), waits=w)
        for ch in range(8):
            p.op("tensor", lambda e, ch=ch: e.matmul(
                pX[0:1, 512:1024], E.ones128f[:, 0:1], u_sb[:, ch * 512:(ch + 1) * 512],
                start=(ch == 0), stop=(ch == 7)), inc=(sT if ch == 7 else None),
                waits=[(sV, u_v)] if ch == 0 else [])
        g1 = p.cnt[sT]
        P_guard_t = g1
        nv = p.op("vector", lambda e: e.tensor_copy(dbf[:, :], dcol[:, :]), inc=sV,
                  waits=[(sS, exp_s[3])])
        dcol_free_v = nv
        for qt in range(4):
            p.op("tensor", lambda e, qt=qt: e.matmul(
                pX[0:1, 1024 + qt * 128: 1024 + (qt + 1) * 128], dbf[:, qt:qt + 1],
                E.I128b[:, :], start=True, stop=True), inc=sT,
                waits=[(sV, nv)] if qt == 0 else [])
        g2 = p.cnt[sT]
        p.op("vector", lambda e: e.tensor_scalar(var_r, pX[0:1, 0:512], 1.0 / 1024, None,
                                                 ALU.mult), inc=sV,
             waits=[(sT, g2), (sT, rows_free_t)])
        p.op("vector", lambda e: e.tensor_scalar(sums_r, pX[0:1, 512:1024], 1.0 / 1024, None,
                                                 ALU.mult), inc=sV)
        p.op("vector", lambda e: e.tensor_copy(drow_r, pX[0:1, 1024:1536]), inc=sV)
        st_ev = p.cnt[sV]
        pst_free = (sV, st_ev)
        ns = p.op("scalar", lambda e: e.activation(sc_r, sums_r, AF.Square), inc=sS,
                  waits=[(sV, st_ev)])
        nv = p.op("vector", lambda e: e.tensor_sub(var_r, var_r, sc_r), inc=sV, waits=[(sS, ns)])
        ns = p.op("scalar", lambda e: e.activation(sc_r, drow_r, AF.Square), inc=sS,
                  waits=[(sV, nv)])
        nv = p.op("vector", lambda e: e.scalar_tensor_tensor(
            sc_r, sc_r, EPS_LN, var_r, ALU.mult, ALU.add), inc=sV, waits=[(sS, ns)])
        ns = p.op("scalar", lambda e: e.activation(sc_r, sc_r, AF.Ln), inc=sS, waits=[(sV, nv)])
        ns = p.op("scalar", lambda e: e.activation(rstd_r, sc_r, AF.Exp, scale=-0.5), inc=sS)
        nv = p.op("vector", lambda e: e.tensor_mul(mur_r, sums_r, rstd_r), inc=sV,
                  waits=[(sS, ns)])
        rows_v = p.cnt[sV]
        prev_ev = None
        for i, row in enumerate([rstd_r, mur_r]):
            w = [(sV, rows_v)]
            if i == 0:
                w += [(sV, maps_free_v)]
                if pX_free[3] is not None:
                    w.append(pX_free[3])
            else:
                w.append((sV, prev_ev))
            n = p.op("tensor", lambda e, row=row: e.matmul(
                pX[:, 1536:2048], E.ones1f[0:1, :], row, start=True, stop=True), inc=sT, waits=w)
            prev_ev = p.op("vector", lambda e, i=i: e.tensor_copy(
                maps5[:, i * 512:(i + 1) * 512], pX[:, 1536:2048]), inc=sV, waits=[(sT, n)])
        maps_v = p.cnt[sV]
        rows_free_t = p.cnt[sT]
        pX_free[3] = (sV, maps_v)
        ld_x1 = p.op("sync", lambda e, qb=qb: e.dma_start(
            out=A_sb.rearrange("p (g t) -> p g t", t=512),
            in_=E.sp_x1[:, qb * 512:(qb + 1) * 512].rearrange("(g p) t -> p g t", p=128)),
            inc=dS, inck=16, waits=[(sS, exp_s[3])])
        for dv in range(8):
            hb = dv % 2
            w = [(sV, maps_v), (dG, o_stored[hb])]
            p.op("vector", lambda e, dv=dv, hb=hb: e.tensor_tensor(
                o_sb[:, hb * 512:(hb + 1) * 512], u_sb[:, dv * 512:(dv + 1) * 512],
                maps5[:, 0:512], ALU.mult), inc=sV, waits=w)
            p.op("vector", lambda e, dv=dv, hb=hb: e.tensor_sub(
                o_sb[:, hb * 512:(hb + 1) * 512], o_sb[:, hb * 512:(hb + 1) * 512],
                maps5[:, 512:1024]), inc=sV)
            nn = p.op("vector", lambda e, dv=dv, hb=hb: e.tensor_add(
                o_sb[:, hb * 512:(hb + 1) * 512], o_sb[:, hb * 512:(hb + 1) * 512],
                A_sb[:, dv * 512:(dv + 1) * 512]), inc=sV,
                waits=[(dS, ld_x1)] if dv == 0 else [])
            o_stored[hb] = p.op("gpsimd", lambda e, qb=qb, dv=dv, hb=hb: e.dma_start(
                out=E.out_t[dv * 128:(dv + 1) * 128, qb * 512:(qb + 1) * 512],
                in_=o_sb[:, hb * 512:(hb + 1) * 512]), inc=dG, inck=16,
                waits=[(sV, nn)])
        if E.dbg and qb == 0:
            p.op("gpsimd", lambda e: e.dma_start(out=E.t_u5[:, :], in_=u_sb[:, :]),
                 inc=dG, inck=16, waits=[(sV, u_v)])
            p.op("gpsimd", lambda e: e.dma_start(out=E.t_d5[:, :], in_=dcol[:, :]),
                 inc=dG, inck=16, waits=[(sS, exp_s[3])])
            p.op("gpsimd", lambda e: e.dma_start(out=E.t_r5[0:1, :], in_=rows5[0:1, :]),
                 inc=dG, inck=16, waits=[(sV, rows_v)])
            p.op("gpsimd", lambda e: e.dma_start(out=E.t_pt5[:, :], in_=PT_sb[:, :]),
                 inc=dG, inck=16, waits=[(dS, tr_d[3])])
            E.p5_tap = p.cnt[dG]
        u_free_v = p.cnt[sV]
        maps_free_v = p.cnt[sV]
        A_free_s = exp_s[3]
        pX_free = [(sV, p.cnt[sV])] * 4
        if E.dbg and qb == 0:
            pX_free = [(dG, E.p5_tap)] * 4
            u_free_v = 0
            pY_free = [(dG, E.p5_tap)] * 4

    p.op("sync", lambda e: e.nop(), waits=[(dG, p.cnt[dG]), (dS, p.cnt[dS]), (E.dC, p.cnt[E.dC])])
    px.close()



def _prep_weights(inputs):
    f32 = lambda a: np.ascontiguousarray(np.asarray(a, np.float32))
    w = {}
    w["w_in"] = f32(inputs["L0_attn_w_in"])
    w["conv_wl"] = f32(np.asarray(inputs["L0_attn_conv_w"]).reshape(18, 128, 4).transpose(1, 0, 2).reshape(128, 72))
    w["conv_bl"] = f32(np.asarray(inputs["L0_attn_conv_b"]).reshape(18, 128).T)
    w["dt_bias"] = f32(np.asarray(inputs["L0_attn_dt_bias"]).reshape(32, 1))
    w["a_log"] = f32(np.asarray(inputs["L0_attn_a_log"]).reshape(32, 1))
    w["d_skip"] = f32(np.asarray(inputs["L0_attn_d_skip"]).reshape(32, 1))
    w["normw"] = f32(np.asarray(inputs["L0_attn_norm_w"]).reshape(16, 128).T)
    w["w_out"] = f32(inputs["L0_attn_w_out"])
    w["w_fc"] = f32(inputs["L0_attn_w_fc"])
    w["w_proj"] = f32(inputs["L0_attn_w_proj"])
    for pp in "qkv":
        w[f"{pp}_nwT"] = f32(np.asarray(inputs[f"L0_ffn_{pp}_node_w"]).T)
        w[f"{pp}_nb"] = f32(np.asarray(inputs[f"L0_ffn_{pp}_node_b"]).reshape(15, 1))
        w[f"{pp}_w1"] = f32(np.asarray(inputs[f"L0_ffn_{pp}_w1"]).transpose(1, 0, 2).reshape(1024, 1024))
        w[f"{pp}_b1c"] = f32(np.asarray(inputs[f"L0_ffn_{pp}_b1"]).reshape(1024).reshape(8, 128).T)
        w[f"{pp}_w2"] = f32(np.asarray(inputs[f"L0_ffn_{pp}_w2"]).reshape(1024, 1024))
        w[f"{pp}_b2"] = f32(inputs[f"L0_ffn_{pp}_b2"])
    return w


def kernel(**inputs):
    x = np.asarray(inputs["L0_attn_x"], np.float32)
    w = _prep_weights(inputs)
    nc, counts, nw = build(nphases=5, dbg=False)
    in_maps = []
    for c in range(8):
        m = dict(w)
        m["xt"] = np.ascontiguousarray(x[c].T)
        in_maps.append(m)
    res = run_bass_kernel_spmd(nc, in_maps, core_ids=list(range(8)))
    out = np.stack([np.asarray(r["o"], np.float32).T for r in res.results], axis=0)
    return np.ascontiguousarray(out)


# version 06
# speedup vs baseline: 1.2780x; 1.1507x over previous
import numpy as np
import ml_dtypes
from contextlib import ExitStack
import concourse.bass as bass
import concourse.mybir as mybir
from concourse.bass_utils import run_bass_kernel_spmd

F32 = mybir.dt.float32
BF16 = mybir.dt.bfloat16
AF = mybir.ActivationFunctionType
ALU = mybir.AluOpType
AXX = mybir.AxisListType.X
BF = ml_dtypes.bfloat16

D = 1024; L = 4096; SL = 512; NSL = L // SL
CH = 256; NCH = L // CH
DIN = 2048; CONVDIM = 2304; DINPROJ = 4384
NH = 32; HD = 64; DS = 128
NZT = DIN // 128
NBCT = CONVDIM // 128
NMT = 35
EPS_LN = 1e-5
EPS_OUT = 1.1920929e-07
EPS_GATE = 1e-5
SCALE_ATT = 1.0 / 32.0

ENGS = ("sync", "gpsimd", "tensor", "vector", "scalar")


class Prog:

    def __init__(self, nc, ctx):
        self.nc = nc
        self.ctx = ctx
        self.items = {e: [] for e in ENGS}
        self.sems = {}
        self.cnt = {}
        self.asyncs = set()
        self.prod = {}

    def sem(self, name, producer, is_async=False):
        self.sems[name] = self.ctx.enter_context(self.nc.semaphore(name))
        self.cnt[name] = 0
        self.prod[name] = producer
        if is_async:
            self.asyncs.add(name)
        return name

    def op(self, eng, fn, inc=None, inck=1, waits=()):
        w = []
        for item in waits:
            if item is None:
                continue
            s, v = item
            if v is None:
                continue
            if isinstance(v, tuple):
                s, v = v
            assert s in self.cnt, s
            assert v <= self.cnt[s], (s, v, self.cnt[s])
            if v > 0:
                w.append((s, v))
        n = None
        if inc is not None:
            self.cnt[inc] += inck
            n = self.cnt[inc]
        self.items[eng].append((fn, inc, inck, tuple(w)))
        return n

    def emit(self):
        nwait = 0
        with self.nc.Block() as block:
            for engname in ENGS:
                items = self.items[engname]
                if not items:
                    continue

                def mk(items, engname):
                    def body(eng):
                        nonlocal nwait
                        seen = {}
                        own = {}
                        for fn, inc, inck, waits in items:
                            is_dma = inc is not None and inc in self.asyncs
                            for s, v in waits:
                                if (s not in self.asyncs and self.prod[s] == engname
                                        and not is_dma):
                                    assert v <= own.get(s, 0), (s, v, own.get(s, 0), engname)
                                    continue
                                if v <= seen.get(s, 0):
                                    continue
                                eng.wait_ge(self.sems[s], v)
                                seen[s] = v
                                nwait += 1
                            ins = fn(eng)
                            if inc is not None:
                                ins.then_inc(self.sems[inc], inck)
                                own[inc] = own.get(inc, 0) + inck
                    return body

                getattr(block, engname)(mk(items, engname))
        return nwait

    def fence(self):
        snap = [(s, self.cnt[s]) for s in self.sems if self.cnt[s] > 0]
        for e in ENGS:
            self.op(e, lambda en: en.nop(), waits=snap)

    def n_ops(self):
        return {e: len(v) for e, v in self.items.items()}


def _consts():
    c = {}
    c["ones1f"] = np.ones((1, 128), np.float32)
    c["ones1b"] = np.ones((1, 128), dtype=BF)
    c["ones128b"] = np.ones((128, 1), dtype=BF)
    c["I32b"] = np.eye(32, dtype=BF)
    c["I128b"] = np.eye(128, dtype=BF)
    c["eyeA"] = np.eye(256, dtype=BF)[0:128].copy()
    c["eyeB"] = np.eye(256, dtype=BF)[128:256].copy()
    c["REPb"] = (np.arange(2048)[None, :] // 64 == np.arange(32)[:, None]).astype(BF)
    c["REPf"] = (np.arange(2048)[None, :] // 64 == np.arange(32)[:, None]).astype(np.float32)
    c["IDX16"] = np.tile(np.arange(16, dtype=np.float32)[:, None], (1, 128)).astype(BF)
    c["leafof"] = ((np.arange(8)[None, :] * 128 + np.arange(128)[:, None]) // 64).astype(np.float32)
    c["ones32f"] = np.ones((32, 256), np.float32)
    c["ones128f"] = np.ones((128, 1), np.float32)
    c["eyeArep"] = np.tile(np.eye(256, dtype=BF)[0:128], (1, 4)).copy()
    c["eyeBrep"] = np.tile(np.eye(256, dtype=BF)[128:256], (1, 4)).copy()
    A = np.zeros((15, 16), np.float32)
    neg = np.zeros((1, 16), np.float32)
    for l in range(16):
        b = [(l >> 3) & 1, (l >> 2) & 1, (l >> 1) & 1, l & 1]
        nodes = [0, 1 + b[0], 3 + 2 * b[0] + b[1], 7 + 4 * b[0] + 2 * b[1] + b[2]]
        for k in range(4):
            A[nodes[k], l] += 1.0 if b[k] else -1.0
            if not b[k]:
                neg[0, l] += 1.0
    c["Abit"] = A.astype(BF)
    c["negrow"] = neg.astype(BF)
    c["ones512b"] = np.ones((1, 512), dtype=BF)
    return c


class Env:
    pass


def build(nphases=5, dbg=False):
    nc = bass.Bass("TRN2", target_bir_lowering=False)
    E = Env()
    E.nc = nc

    E.xt = nc.dram_tensor("xt", [D, L], F32, kind="ExternalInput")
    E.w_in = nc.dram_tensor("w_in", [D, DINPROJ], F32, kind="ExternalInput")
    E.conv_wl = nc.dram_tensor("conv_wl", [128, 72], F32, kind="ExternalInput")
    E.conv_bl = nc.dram_tensor("conv_bl", [128, 18], F32, kind="ExternalInput")
    E.dt_bias = nc.dram_tensor("dt_bias", [32, 1], F32, kind="ExternalInput")
    E.a_log = nc.dram_tensor("a_log", [32, 1], F32, kind="ExternalInput")
    E.d_skip = nc.dram_tensor("d_skip", [32, 1], F32, kind="ExternalInput")
    E.normw = nc.dram_tensor("normw", [128, 16], F32, kind="ExternalInput")
    E.w_out = nc.dram_tensor("w_out", [DIN, D], F32, kind="ExternalInput")
    E.w_fc = nc.dram_tensor("w_fc", [D, 4 * D], F32, kind="ExternalInput")
    E.w_proj = nc.dram_tensor("w_proj", [4 * D, D], F32, kind="ExternalInput")
    E.fff = {}
    for pp in "qkv":
        E.fff[pp] = dict(
            nwT=nc.dram_tensor(f"{pp}_nwT", [D, 15], F32, kind="ExternalInput"),
            nb=nc.dram_tensor(f"{pp}_nb", [15, 1], F32, kind="ExternalInput"),
            w1=nc.dram_tensor(f"{pp}_w1", [D, D], F32, kind="ExternalInput"),
            b1c=nc.dram_tensor(f"{pp}_b1c", [128, 8], F32, kind="ExternalInput"),
            w2=nc.dram_tensor(f"{pp}_w2", [D, D], F32, kind="ExternalInput"),
            b2=nc.dram_tensor(f"{pp}_b2", [16, D], F32, kind="ExternalInput"),
        )
    E.out_t = nc.dram_tensor("o", [D, L], F32, kind="ExternalOutput")

    spk = "ExternalOutput" if dbg else "Internal"
    E.sp_xn = nc.dram_tensor("sp_xn", [D, L], F32, kind=spk)
    E.sp_silz = nc.dram_tensor("sp_silz", [DIN, L], BF16, kind=spk)
    E.sp_xc = nc.dram_tensor("sp_xc", [CONVDIM, L], BF16, kind=spk)
    E.sp_dtraw = nc.dram_tensor("sp_dtraw", [32, L], F32, kind=spk)
    E.sp_h = nc.dram_tensor("sp_h", [D, L], F32, kind=spk)
    E.sp_x1 = nc.dram_tensor("sp_x1", [D, L], F32, kind=spk)
    E.sp_q = nc.dram_tensor("sp_q", [D, L], BF16, kind=spk)
    E.sp_k = nc.dram_tensor("sp_k", [D, L], BF16, kind=spk)
    E.sp_v = nc.dram_tensor("sp_v", [L, D], BF16, kind=spk)
    E.sp_sm = nc.dram_tensor("sp_sm", [32, 1024], BF16, kind="Internal")
    if dbg:
        E.t_dsm = nc.dram_tensor("t_dsm", [32, 5 * 256], F32, kind="ExternalOutput")
        E.t_u = nc.dram_tensor("t_u", [128, 4096], F32, kind="ExternalOutput")
        E.t_ud = nc.dram_tensor("t_ud", [128, 4096], F32, kind="ExternalOutput")
        E.t_cbt = nc.dram_tensor("t_cbt", [128, 512], F32, kind="ExternalOutput")
        E.t_mexa = nc.dram_tensor("t_mexa", [128, 4096], F32, kind="ExternalOutput")
        E.t_G = nc.dram_tensor("t_G", [128, 2048], F32, kind="ExternalOutput")
        E.t_g = nc.dram_tensor("t_g", [128, 4096], F32, kind="ExternalOutput")
        E.t_prev = nc.dram_tensor("t_prev", [128, 2048], F32, kind="ExternalOutput")
        E.t_msb = nc.dram_tensor("t_msb", [128, 512], F32, kind="ExternalOutput")
        E.t_gf = nc.dram_tensor("t_gf", [128, 4096], F32, kind="ExternalOutput")
        E.t_xn2 = nc.dram_tensor("t_xn2", [128, 8 * 512], F32, kind="ExternalOutput")
        E.t_rt = nc.dram_tensor("t_rt", [128, 480], F32, kind="ExternalOutput")
        E.t_sel = nc.dram_tensor("t_sel", [128, 64], F32, kind="ExternalOutput")
        E.t_selT = nc.dram_tensor("t_selT", [16, 512], F32, kind="ExternalOutput")
        E.t_leaf = nc.dram_tensor("t_leaf", [128, 512], F32, kind="ExternalOutput")
        E.t_hid = nc.dram_tensor("t_hid", [128, 4096], F32, kind="ExternalOutput")
        E.t_fout = nc.dram_tensor("t_fout", [128, 4096], F32, kind="ExternalOutput")
        E.t_A5 = nc.dram_tensor("t_A5", [128, 4096], F32, kind="ExternalOutput")
        E.t_u5 = nc.dram_tensor("t_u5", [128, 4096], F32, kind="ExternalOutput")
        E.t_d5 = nc.dram_tensor("t_d5", [128, 4], F32, kind="ExternalOutput")
        E.t_r5 = nc.dram_tensor("t_r5", [1, 2048], F32, kind="ExternalOutput")
        E.t_pt5 = nc.dram_tensor("t_pt5", [128, 8192], F32, kind="ExternalOutput")
    E.dbg = dbg
    E.dbg_guard = 0

    cst = {k: nc.inline_tensor(v, f"c_{k}") for k, v in _consts().items()}

    ctx = ExitStack()
    p = Prog(nc, ctx)
    E.p = p
    dS = p.sem("dS", "sync", True)
    dG = p.sem("dG", "gpsimd", True)
    dC = p.sem("dC", "scalar", True)
    sT = p.sem("sT", "tensor")
    sV = p.sem("sV", "vector")
    sS = p.sem("sS", "scalar")
    sG = p.sem("sG", "gpsimd")
    E.dS, E.dG, E.dC, E.sT, E.sV, E.sS, E.sG = dS, dG, dC, sT, sV, sS, sG

    sb = lambda n, s, d: ctx.enter_context(nc.sbuf_tensor(n, s, d))
    E.sb = sb

    E.ones1f = sb("ones1f", [1, 128], F32)
    E.ones1b = sb("ones1b", [1, 128], BF16)
    E.ones128b = sb("ones128b", [128, 1], BF16)
    E.I32b = sb("I32b", [32, 32], BF16)
    E.I128b = sb("I128b", [128, 128], BF16)
    E.IDX16 = sb("IDX16", [16, 128], BF16)
    E.leafof = sb("leafof", [128, 8], F32)
    E.Abit = sb("Abit", [15, 16], BF16)
    E.negrow = sb("negrow", [1, 16], BF16)
    E.ones512b = sb("ones512b", [1, 512], BF16)
    E.smalls = sb("smalls", [32, 8], F32)
    E.ones128f = sb("ones128f", [128, 1], F32)
    E.cst = cst
    for t, cc in [(E.ones1f, "ones1f"), (E.ones1b, "ones1b"), (E.ones128b, "ones128b"),
                  (E.I32b, "I32b"), (E.I128b, "I128b"),
                  (E.IDX16, "IDX16"),
                  (E.leafof, "leafof"),
                  (E.Abit, "Abit"), (E.negrow, "negrow"), (E.ones512b, "ones512b"),
                  (E.ones128f, "ones128f")]:
        p.op("sync", lambda e, t=t, cc=cc: e.dma_start(out=t[:, :], in_=cst[cc][:, :]), inc=dS, inck=16)
    p.op("sync", lambda e: e.dma_start(out=E.smalls[:, 0:1], in_=E.dt_bias[:, :]), inc=dS, inck=16)
    p.op("sync", lambda e: e.dma_start(out=E.smalls[:, 3:4], in_=E.a_log[:, :]), inc=dS, inck=16)
    p.op("sync", lambda e: e.dma_start(out=E.smalls[:, 2:3], in_=E.d_skip[:, :]), inc=dS, inck=16)
    E.c_loaded = p.cnt[dS]
    E.epsb = sb("epsb", [128, 4], F32)
    for i, v in enumerate([EPS_LN, EPS_OUT, EPS_GATE, 1.0]):
        p.op("vector", lambda e, i=i, v=v: e.memset(E.epsb[:, i:i+1], v), inc=sV)
    E.epsb_ready = p.cnt[sV]
    n = p.op("scalar", lambda e: e.activation(E.smalls[0:32, 4:5], E.smalls[0:32, 3:4], AF.Exp),
             inc=sS, waits=[(dS, E.c_loaded)])
    p.op("vector", lambda e: e.tensor_scalar(E.smalls[0:32, 1:2], E.smalls[0:32, 4:5], -1.0, None, ALU.mult),
         inc=sV, waits=[(sS, n)])

    _phase1(E, nphases)
    if nphases >= 2:
        _phase2(E, nphases)
    if nphases >= 3:
        _phase3(E, nphases)
    if nphases >= 4:
        _phase4(E, nphases)
    if nphases >= 5:
        _phase5(E)

    nw = p.emit()
    ctx.close()
    return nc, p.n_ops(), nw


def _phase1(E, nphases):
    nc, p = E.nc, E.p
    dS, dG, sT, sV, sS = E.dS, E.dG, E.sT, E.sV, E.sS
    px = ExitStack()
    sbl = lambda n, s, d: px.enter_context(nc.sbuf_tensor(n, s, d))
    psl = lambda n, s: px.enter_context(nc.psum_tensor(n, s, F32))
    w_in_sb = sbl("w_in_sb", [128, 8 * DINPROJ], BF16)
    convw = sbl("convw", [128, 72], F32)
    convb = sbl("convb", [128, 18], F32)
    diag = sbl("diag", [128, 72 * 128], BF16)
    xbuf = sbl("xbuf", [128, 2 * 8 * SL], F32)
    xbf = sbl("xbf", [128, 8 * SL], BF16)
    rows1 = sbl("rows1", [1, 5 * SL], F32)
    mapsb = sbl("mapsb", [128, 3 * SL], F32)
    tch = sbl("tch", [128, SL], F32)
    xnb = sbl("xnb", [128, 2 * SL], F32)
    ain_sb = sbl("ain_sb", [128, 8 * SL], BF16)
    xsq = ain_sb
    xbc_sb = sbl("xbc_sb", [128, NBCT * 516], BF16)
    szb = sbl("szb", [128, 2 * SL], BF16)
    xcb = sbl("xcb", [128, 2 * SL], BF16)
    dtraw_sb = sbl("dtraw_sb", [32, SL], F32)
    E.p1_slice_dG = [0] * NSL
    pzx = psl("pzx", [128, 2048])
    pconv = psl("pconv", [128, 1024])
    pstatA = psl("pstatA", [1, 512])
    pstatB = psl("pstatB", [1, 512])
    if True:
        mu_r = rows1[0:1, 0:SL]
        var_r = rows1[0:1, SL:2*SL]; r1_r = rows1[0:1, 2*SL:3*SL]
        r2_r = rows1[0:1, 3*SL:4*SL]
        sc1_r = rows1[0:1, 4*SL:5*SL]
        ms_r = var_r; q_r = var_r

        for kc in range(8):
            p.op("gpsimd", lambda e, kc=kc: e.dma_start(
                out=w_in_sb[:, kc * DINPROJ:(kc + 1) * DINPROJ],
                in_=E.w_in[kc * 128:(kc + 1) * 128, :]), inc=dG, inck=16)
        p.op("gpsimd", lambda e: e.dma_start(out=convw[:, :], in_=E.conv_wl[:, :]), inc=dG, inck=16)
        p.op("gpsimd", lambda e: e.dma_start(out=convb[:, :], in_=E.conv_bl[:, :]), inc=dG, inck=16)
        w1_loaded = p.cnt[dG]

        for c in range(NBCT):
            for k in range(4):
                p.op("vector", lambda e, c=c, k=k: e.tensor_scalar(
                    diag[:, (c * 4 + k) * 128:(c * 4 + k + 1) * 128],
                    E.I128b[:, :], convw[:, c * 4 + k:c * 4 + k + 1], None, ALU.mult),
                    inc=sV, waits=[(dG, w1_loaded), (dS, E.c_loaded)])
        p.op("vector", lambda e: e.memset(xbc_sb[:, :], 0.0), inc=sV)
        diag_done = p.cnt[sV]

        xbuf_free = [0, 0]
        pzx_free = [None, None, None, None]
        pconv_free = [None, None]
        stats_free = 0
        rows_free_t = 0
        xbf_free_t = 0
        ain_free_t = 0
        xn_stored = [0, 0]
        silz_stored = [0, 0]
        dtraw_stored = 0
        xc_stored = [0, 0]
        mapsb_free = 0
        xc_prev_conv = 0

        def load_x(s):
            b = s % 2
            return p.op("sync", lambda e, s=s, b=b: e.dma_start(
                out=xbuf[:, b * 8 * SL:(b + 1) * 8 * SL].rearrange("p (c t) -> p c t", t=SL),
                in_=E.xt[:, s * SL:(s + 1) * SL].rearrange("(c p) t -> p c t", p=128)),
                inc=dS, inck=16, waits=[(sV, xbuf_free[b])])

        x_ld = [None] * (NSL + 1)
        x_ld[0] = load_x(0)

        for s in range(NSL):
            b = s % 2
            xv = lambda ch, b=b: xbuf[:, (b * 8 + ch) * SL:(b * 8 + ch + 1) * SL]
            if s + 1 < NSL:
                x_ld[s + 1] = load_x(s + 1)

            for ch in range(8):
                p.op("vector", lambda e, ch=ch, xv=xv: e.tensor_copy(
                    xbf[:, ch * SL:(ch + 1) * SL], xv(ch)), inc=sV,
                    waits=[(dS, x_ld[s]), (sT, xbf_free_t)])
                p.op("scalar", lambda e, ch=ch, xv=xv: e.activation(
                    xsq[:, ch * SL:(ch + 1) * SL], xv(ch), AF.Square), inc=sS,
                    waits=[(dS, x_ld[s]), (sT, xbf_free_t), (sT, ain_free_t)])
            xbf_done = p.cnt[sV]; xsq_done = p.cnt[sS]

            for ch in range(8):
                p.op("tensor", lambda e, ch=ch: e.matmul(
                    pstatA[0:1, :], E.ones128b[:, 0:1], xbf[:, ch * SL:(ch + 1) * SL],
                    start=(ch == 0), stop=(ch == 7)),
                    inc=(sT if ch == 7 else None),
                    waits=[(sV, xbf_done), (sV, stats_free)] if ch == 0 else [])
            for ch in range(8):
                p.op("tensor", lambda e, ch=ch: e.matmul(
                    pstatB[0:1, :], E.ones128b[:, 0:1], xsq[:, ch * SL:(ch + 1) * SL],
                    start=(ch == 0), stop=(ch == 7)),
                    inc=(sT if ch == 7 else None),
                    waits=[(sS, xsq_done)] if ch == 0 else [])
            stats_mm = p.cnt[sT]
            xbf_free_t = stats_mm

            p.op("vector", lambda e: e.tensor_scalar(mu_r, pstatA[0:1, :], 1.0 / 1024, None, ALU.mult),
                 inc=sV, waits=[(sT, stats_mm), (sT, rows_free_t)])
            p.op("vector", lambda e: e.tensor_scalar(ms_r, pstatB[0:1, :], 1.0 / 1024, None, ALU.mult),
                 inc=sV)
            stats_evaced = p.cnt[sV]
            p.op("scalar", lambda e: e.activation(sc1_r, mu_r, AF.Square), inc=sS,
                 waits=[(sV, stats_evaced), (sT, rows_free_t)])
            p.op("vector", lambda e: e.tensor_sub(var_r, ms_r, sc1_r), inc=sV,
                 waits=[(sS, p.cnt[sS])])
            p.op("scalar", lambda e: e.activation(sc1_r, var_r, AF.Ln, bias=E.epsb[0:1, 0:1]), inc=sS,
                 waits=[(sV, p.cnt[sV]), (sV, E.epsb_ready)])
            p.op("scalar", lambda e: e.activation(r1_r, sc1_r, AF.Exp, scale=-0.5), inc=sS)
            p.op("scalar", lambda e: e.activation(sc1_r, r1_r, AF.Square), inc=sS)
            p.op("vector", lambda e: e.tensor_mul(q_r, var_r, sc1_r), inc=sV,
                 waits=[(sS, p.cnt[sS])])
            p.op("scalar", lambda e: e.activation(sc1_r, q_r, AF.Ln, bias=E.epsb[0:1, 1:2]), inc=sS,
                 waits=[(sV, p.cnt[sV])])
            p.op("scalar", lambda e: e.activation(sc1_r, sc1_r, AF.Exp, scale=-0.5), inc=sS)
            p.op("vector", lambda e: e.tensor_mul(r2_r, r1_r, sc1_r), inc=sV,
                 waits=[(sS, p.cnt[sS])])
            rows_done_v = p.cnt[sV]
            rows_done_s = p.cnt[sS]

            prev_evac = None
            for i, (row, mi) in enumerate([(mu_r, 0), (r1_r, 1), (r2_r, 2)]):
                w = [(sV, rows_done_v), (sS, rows_done_s)]
                if i == 0:
                    w += [pzx_free[3], (sV, mapsb_free)]
                else:
                    w.append((sV, prev_evac))
                n = p.op("tensor", lambda e, row=row: e.matmul(
                    pzx[:, 1536:2048], E.ones1f[0:1, :], row, start=True, stop=True),
                    inc=sT, waits=w)
                prev_evac = p.op("vector", lambda e, mi=mi: e.tensor_copy(
                    mapsb[:, mi * SL:(mi + 1) * SL], pzx[:, 1536:2048]), inc=sV,
                    waits=[(sT, n)])
            maps_done = p.cnt[sV]
            rows_free_t = p.cnt[sT]
            pzx_free[3] = (sV, maps_done)

            for ch in range(8):
                hb = ch % 2
                wv = [(dG, xn_stored[hb])]
                if ch == 0:
                    wv += [(sV, maps_done), (sT, ain_free_t)]
                p.op("vector", lambda e, ch=ch, xv=xv: e.tensor_sub(tch[:, :], xv(ch), mapsb[:, 0:SL]),
                     inc=sV, waits=wv)
                n = p.op("vector", lambda e, ch=ch, hb=hb: e.tensor_mul(
                    xnb[:, hb * SL:(hb + 1) * SL], tch[:, :], mapsb[:, SL:2*SL]), inc=sV)
                p.op("vector", lambda e, ch=ch: e.tensor_mul(
                    ain_sb[:, ch * SL:(ch + 1) * SL], tch[:, :], mapsb[:, 2*SL:3*SL]), inc=sV)
                xn_stored[hb] = p.op("gpsimd", lambda e, s=s, ch=ch, hb=hb: e.dma_start(
                    out=E.sp_xn[ch * 128:(ch + 1) * 128, s * SL:(s + 1) * SL],
                    in_=xnb[:, hb * SL:(hb + 1) * SL]),
                    inc=dG, inck=16, waits=[(sV, n)])
            apply_done = p.cnt[sV]
            xbuf_free[b] = apply_done
            mapsb_free = apply_done

            for mt in range(NMT):
                r = mt % 4
                M = 128 if mt < NMT - 1 else 32
                w = []
                if mt == 0:
                    w += [(sV, apply_done), (dG, w1_loaded)]
                if pzx_free[r] is not None:
                    w.append(pzx_free[r])
                for kc in range(8):
                    p.op("tensor", lambda e, kc=kc, mt=mt, r=r, M=M: e.matmul(
                        pzx[0:M, r * 512:r * 512 + 512],
                        w_in_sb[:, kc * DINPROJ + mt * 128: kc * DINPROJ + mt * 128 + M],
                        ain_sb[:, kc * SL:(kc + 1) * SL],
                        start=(kc == 0), stop=(kc == 7)),
                        inc=(sT if kc == 7 else None),
                        waits=w if kc == 0 else [])
                g = p.cnt[sT]
                if mt < NZT:
                    hb = mt % 2
                    wz = [(sT, g), (dG, silz_stored[hb])]
                    n = p.op("scalar", lambda e, hb=hb, r=r: e.activation(
                        szb[:, hb * SL:(hb + 1) * SL], pzx[:, r * 512:r * 512 + 512], AF.Silu),
                        inc=sS, waits=wz)
                    pzx_free[r] = (sS, n)
                    silz_stored[hb] = p.op("gpsimd", lambda e, s=s, mt=mt, hb=hb: e.dma_start(
                        out=E.sp_silz[mt * 128:(mt + 1) * 128, s * SL:(s + 1) * SL],
                        in_=szb[:, hb * SL:(hb + 1) * SL]),
                        inc=dG, inck=16, waits=[(sS, n)])
                elif mt < NZT + NBCT:
                    c = mt - NZT
                    wz = [(sT, g)]
                    if c == 0:
                        wz += [(sT, xc_prev_conv)]
                    n = p.op("vector", lambda e, c=c, r=r: e.tensor_copy(
                        xbc_sb[:, c * 516 + 3:c * 516 + 515], pzx[:, r * 512:r * 512 + 512]),
                        inc=sV, waits=wz)
                    pzx_free[r] = (sV, n)
                else:
                    n = p.op("vector", lambda e, r=r: e.tensor_copy(
                        dtraw_sb[0:32, :], pzx[0:32, r * 512:r * 512 + 512]),
                        inc=sV, waits=[(sT, g), (dG, dtraw_stored)])
                    pzx_free[r] = (sV, n)
            zx_done_v = p.cnt[sV]; zx_done_s = p.cnt[sS]
            ain_free_t = p.cnt[sT]
            dtraw_stored = p.op("gpsimd", lambda e, s=s: e.dma_start(
                out=E.sp_dtraw[:, s * SL:(s + 1) * SL], in_=dtraw_sb[0:32, :]),
                inc=dG, inck=16, waits=[(sV, zx_done_v)])

            for c in range(NBCT):
                r = c % 2
                w = [(sV, zx_done_v), (sV, diag_done)]
                if pconv_free[r] is not None:
                    w.append(pconv_free[r])
                for k in range(4):
                    p.op("tensor", lambda e, c=c, k=k, r=r: e.matmul(
                        pconv[:, r * 512:r * 512 + 512],
                        diag[:, (c * 4 + k) * 128:(c * 4 + k + 1) * 128],
                        xbc_sb[:, c * 516 + k:c * 516 + k + 512],
                        start=(k == 0), stop=(k == 3)),
                        inc=(sT if k == 3 else None),
                        waits=w if k == 0 else [])
                g = p.cnt[sT]
                hb = c % 2
                wz = [(sT, g), (dG, xc_stored[hb])]
                n = p.op("scalar", lambda e, hb=hb, c=c, r=r: e.activation(
                    xcb[:, hb * SL:(hb + 1) * SL], pconv[:, r * 512:r * 512 + 512],
                    AF.Silu, bias=convb[:, c:c + 1]),
                    inc=sS, waits=wz)
                pconv_free[r] = (sS, n)
                xc_stored[hb] = p.op("gpsimd", lambda e, s=s, c=c, hb=hb: e.dma_start(
                    out=E.sp_xc[c * 128:(c + 1) * 128, s * SL:(s + 1) * SL],
                    in_=xcb[:, hb * SL:(hb + 1) * SL]),
                    inc=dG, inck=16, waits=[(sS, n)])
                p.op("vector", lambda e, c=c: e.tensor_copy(
                    xbc_sb[:, c * 516:c * 516 + 3], xbc_sb[:, c * 516 + 512:c * 516 + 515]),
                    inc=sV, waits=[(sT, g)])
            xc_prev_conv = p.cnt[sT]
            stats_free = stats_evaced
            E.p1_slice_dG[s] = p.cnt[dG]

        E.p1_spills = p.cnt[dG]
        if nphases == 1:
            p.op("sync", lambda e: e.nop(), waits=[(dG, E.p1_spills)])
    px.close()


def _phase2(E, nphases):
    nc, p = E.nc, E.p
    p.fence()
    dS, dG, sT, sV, sS, sG = E.dS, E.dG, E.sT, E.sV, E.sS, E.sG
    px = ExitStack()
    sbl = lambda n, s, d: px.enter_context(nc.sbuf_tensor(n, s, d))
    w_out_sb = sbl("w_out_sb", [128, 16 * 1024], BF16)
    normw_sb = sbl("normw_sb", [128, 16], F32)
    xctok = sbl("xctok", [128, 2 * 2176], BF16)
    bcfeat = sbl("bcfeat", [128, 512], BF16)
    siluzt = sbl("siluzt", [128, 2 * 2048], BF16)
    xn_c = sbl("xn_c", [128, 8 * 256], F32)
    dtraw_c = sbl("dtraw_c", [32, 256], F32)
    dsm = sbl("dsm", [32, 5 * 256], F32)
    dsb = sbl("dsb", [32, 4 * 256], BF16)
    cdcol = sbl("cdcol", [32, 1], F32)
    mEXA = sbl("mEXA", [128, 2 * 2048], BF16)
    mU = sbl("mU", [128, 2048], BF16)
    mUb = sbl("mUb", [128, 2048], BF16)
    mSKIP = sbl("mSKIP", [128, 2048], BF16)
    mDECAY = sbl("mDECAY", [128, 2048], F32)
    u_tok = sbl("u_tok", [128, 2 * 2048], BF16)
    udec = sbl("udec", [128, 2 * 2048], BF16)
    CBT_sb = sbl("CBT_sb", [128, 512], BF16)
    mapALL = sbl("mapALL", [128, 8192], BF16)
    CBTrep = sbl("CBTrep", [128, 2 * 1024], BF16)
    expa_row = sbl("expa_row", [1, 2048], BF16)
    expa_rowb = sbl("expa_rowb", [1, 2048], BF16)
    LTb = sbl("LTb", [128, 2 * 1024], BF16)
    Mbuf = sbl("Mbuf", [128, 2048], BF16)
    Abuf = sbl("Abuf", [128, 2048], F32)
    gtok = sbl("gtok", [128, 2 * 2048], BF16)
    gfeat = sbl("gfeat", [128, 16 * 256], BF16)
    hbuf = sbl("hbuf", [128, 2 * 256], F32)
    prev = sbl("prev", [128, 2048], F32)
    prevbf = sbl("prevbf", [128, 2048], BF16)
    rcol = sbl("rcol", [128, 4], F32)
    eyeArep = sbl("eyeArep", [128, 1024], BF16)
    eyeBrep = sbl("eyeBrep", [128, 1024], BF16)
    REPb = sbl("REPb2", [32, 2048], BF16)
    REPf = sbl("REPf2", [32, 2048], F32)
    ones32f = sbl("ones32f2", [32, 256], F32)
    for t, cc in [(eyeArep, "eyeArep"), (eyeBrep, "eyeBrep"), (REPb, "REPb"), (REPf, "REPf"),
                  (ones32f, "ones32f")]:
        p.op("sync", lambda e, t=t, cc=cc: e.dma_start(out=t[:, :], in_=E.cst[cc][:, :]),
             inc=dS, inck=16)
    E.c2_loaded = p.cnt[dS]
    pA = px.enter_context(nc.psum_tensor("pA", [128, 2048], F32))
    pB = px.enter_context(nc.psum_tensor("pB", [128, 2048], F32))

    s_abs = dsm[0:32, 0:256]; s_dt = dsm[0:32, 256:512]
    s_a = dsm[0:32, 512:768]; s_acum = dsm[0:32, 768:1024]
    s_dec = dsm[0:32, 1024:1280]
    b_dt = dsb[0:32, 0:256]; b_dtdec = dsb[0:32, 256:512]
    b_expa = dsb[0:32, 512:768]; b_expacum = dsb[0:32, 768:1024]

    p.op("gpsimd", lambda e: e.dma_start(out=normw_sb[:, :], in_=E.normw[:, :]), inc=dG, inck=16)
    for kc in range(16):
        p.op("gpsimd", lambda e, kc=kc: e.dma_start(
            out=w_out_sb[:, kc * 1024:(kc + 1) * 1024],
            in_=E.w_out[kc * 128:(kc + 1) * 128, :]), inc=dG, inck=16)
    nwl = p.cnt[dG]
    for kc in range(16):
        p.op("vector", lambda e, kc=kc: e.tensor_scalar(
            w_out_sb[:, kc * 1024:(kc + 1) * 1024], w_out_sb[:, kc * 1024:(kc + 1) * 1024],
            normw_sb[:, kc:kc + 1], None, ALU.mult),
            inc=sV, waits=[(dG, nwl)] if kc == 0 else [])
    wout_ready = p.cnt[sV]

    for i in range(4):
        w = [(dS, E.c_loaded), (dS, E.c2_loaded)] if i == 0 else []
        n = p.op("tensor", lambda e, i=i: e.matmul(
            pB[0:1, i * 512:(i + 1) * 512], E.smalls[0:32, 2:3], REPf[:, i * 512:(i + 1) * 512],
            start=True, stop=True), inc=sT, waits=w)
    nv = p.op("vector", lambda e: e.tensor_copy(Abuf[0:1, :], pB[0:1, :]), inc=sV, waits=[(sT, n)])
    for i in range(4):
        w = [(sV, nv)] if i == 0 else []
        n = p.op("tensor", lambda e, i=i: e.matmul(
            pA[:, i * 512:(i + 1) * 512], E.ones1f[0:1, :], Abuf[0:1, i * 512:(i + 1) * 512],
            start=True, stop=True), inc=sT, waits=w)
    nv2 = p.op("vector", lambda e: e.tensor_copy(mSKIP[:, :], pA[:, :]), inc=sV, waits=[(sT, n)])
    skip_ready = nv2
    pA_free = (sV, nv2)
    pB_free = (sV, nv)

    def rep_map(lhs_slice, rep, dst_ap, wait_mm, wait_evac, ps):
        for i in range(4):
            w = list(wait_mm) if i == 0 else []
            n = p.op("tensor", lambda e, i=i, lhs_slice=lhs_slice, rep=rep, ps=ps: e.matmul(
                ps[:, i * 512:(i + 1) * 512], lhs_slice, rep[:, i * 512:(i + 1) * 512],
                start=True, stop=True), inc=sT, waits=w)
        return p.op("scalar", lambda e, dst_ap=dst_ap, ps=ps: e.copy(dst_ap, ps[:, :]),
                    inc=sS, waits=[(sT, n)] + list(wait_evac))

    prev_ready_v = 0
    hb_free = [0, 0]
    gfeat_free_t = 0
    gtok_free_d = 0
    xn_free_v = 0
    xctok_free = [(0, 0), (0, 0)]
    siluz_free = [0, 0]
    bc_free_t = 0
    dtraw_free_s = 0
    mexa_free = [0, 0]
    utok_free_t = 0
    udec_free_t = 0
    mdecay_free_g = 0
    abuf_free = 0
    bbuf_free = 0
    prevbf_free_t = 0
    dsb_store = 0
    expa_row_free_t = [0, 0, 0, 0]
    cbtrep_free_g = 0
    upool_free = [0, 0]
    mapall_free = 0

    for c in range(NCH):
        sl = c // 2
        p1rdy = E.p1_slice_dG[sl]
        ld_xct = []
        for tt in range(2):
            ld_xct.append(p.op("sync", lambda e, c=c, tt=tt: e.dma_start_transpose(
                xctok[:, tt * 2176:(tt + 1) * 2176],
                E.sp_xc[0:2176, c * 256 + tt * 128: c * 256 + (tt + 1) * 128]),
                inc=dS, inck=16,
                waits=[(dG, p1rdy), (sT, xctok_free[tt][0]), (sG, xctok_free[tt][1])]))
        ld_bc = p.op("sync", lambda e, c=c: e.dma_start(
            out=bcfeat[:, 0:512].rearrange("p (g t) -> p g t", t=256),
            in_=E.sp_xc[2048:2304, c * 256:(c + 1) * 256].rearrange("(g p) t -> p g t", p=128)),
            inc=dS, inck=16, waits=[(dG, p1rdy), (sT, bc_free_t)])
        ld_sz = []
        for tt in range(2):
            ld_sz.append(p.op("sync", lambda e, c=c, tt=tt: e.dma_start_transpose(
                siluzt[:, tt * 2048:(tt + 1) * 2048],
                E.sp_silz[0:2048, c * 256 + tt * 128: c * 256 + (tt + 1) * 128]),
                inc=dS, inck=16, waits=[(dG, p1rdy), (sV, siluz_free[tt])]))
        ld_xn = p.op("sync", lambda e, c=c: e.dma_start(
            out=xn_c.rearrange("p (g t) -> p g t", t=256),
            in_=E.sp_xn[:, c * 256:(c + 1) * 256].rearrange("(g p) t -> p g t", p=128)),
            inc=dS, inck=16, waits=[(dG, p1rdy), (sV, xn_free_v)])
        ld_dt = p.op("sync", lambda e, c=c: e.dma_start(
            out=dtraw_c[:, :], in_=E.sp_dtraw[0:32, c * 256:(c + 1) * 256]),
            inc=dS, inck=16, waits=[(dG, p1rdy), (sS, dtraw_free_s)])

        p.op("scalar", lambda e: e.activation(s_abs, dtraw_c[:, :], AF.Abs, bias=E.smalls[0:32, 0:1]),
             inc=sS, waits=[(dS, ld_dt), (dG, dsb_store)])
        p.op("scalar", lambda e: e.activation(s_abs, s_abs, AF.Exp, scale=-1.0), inc=sS)
        p.op("scalar", lambda e: e.activation(s_abs, s_abs, AF.Ln, bias=E.epsb[0:32, 3:4]), inc=sS)
        p.op("scalar", lambda e: e.activation(s_dt, dtraw_c[:, :], AF.Relu, bias=E.smalls[0:32, 0:1]),
             inc=sS)
        dtraw_free_s = p.cnt[sS]
        ns = p.cnt[sS]
        p.op("vector", lambda e: e.tensor_add(s_dt, s_dt, s_abs), inc=sV,
             waits=[(sS, ns), (dG, dsb_store)])
        p.op("vector", lambda e: e.tensor_copy(b_dt, s_dt), inc=sV)
        p.op("vector", lambda e: e.tensor_scalar(s_a, s_dt, E.smalls[0:32, 1:2], None, ALU.mult),
             inc=sV)
        p.op("vector", lambda e: e.reduce_sum(E.smalls[0:32, 5:6], s_a, axis=AXX), inc=sV)
        p.op("vector", lambda e: e.tensor_tensor_scan(s_acum, ones32f[:, :], s_a, 0.0,
                                                      ALU.mult, ALU.add), inc=sV)
        nv = p.cnt[sV]
        p.op("scalar", lambda e: e.activation(b_expa, s_a, AF.Exp), inc=sS, waits=[(sV, nv)])
        exz = p.cnt[sS]
        p.op("vector", lambda e: e.memset(dsb[0:32, 512:513], 0.0), inc=sV, waits=[(sS, exz)])
        p.op("scalar", lambda e: e.activation(b_expacum, s_acum, AF.Exp), inc=sS)
        p.op("scalar", lambda e: e.activation(cdcol[0:32, :], E.smalls[0:32, 5:6], AF.Exp), inc=sS)
        p.op("vector", lambda e: e.tensor_scalar(s_dec, s_acum, E.smalls[0:32, 5:6], -1.0,
                                                 ALU.subtract, ALU.mult), inc=sV)
        nv = p.cnt[sV]
        p.op("scalar", lambda e: e.activation(s_dec, s_dec, AF.Exp), inc=sS,
             waits=[(sV, nv)])
        ns = p.cnt[sS]
        p.op("vector", lambda e: e.tensor_tensor(b_dtdec, s_dt, s_dec, ALU.mult), inc=sV,
             waits=[(sS, ns)])
        dt_small_v = p.cnt[sV]; dt_small_s = p.cnt[sS]

        dsb_store = p.op("gpsimd", lambda e: e.dma_start(out=E.sp_sm[:, :], in_=dsb[0:32, :]),
                         inc=dG, inck=16, waits=[(sV, dt_small_v), (sS, dt_small_s)])

        prevbf_done = 0
        if c > 0:
            prevbf_done = p.op("vector", lambda e: e.tensor_copy(prevbf[:, :], prev[:, :]),
                               inc=sV, waits=[(sV, prev_ready_v)])

        mexa_ev = []
        frees = {0: pA_free, 1: pB_free}
        psx = {0: pA, 1: pB}
        k = 0
        for tt in range(2):
            nn = rep_map(dsb[0:32, 768 + tt * 128:768 + (tt + 1) * 128], REPb,
                         mEXA[:, tt * 2048:(tt + 1) * 2048],
                         [(sS, dt_small_s), frees[k]], [(sV, mexa_free[tt])], psx[k])
            frees[k] = (sS, nn)
            k ^= 1
            mexa_ev.append((sS, nn))
        upool = 0
        mU2 = mEXA
        for j, (off, dst, freet) in enumerate(((0, u_tok, utok_free_t), (256, udec, udec_free_t))):
            for tt in range(2):
                dstm = mU[:, 0:2048] if (j * 2 + tt) % 2 == 0 else mUb[:, 0:2048]
                nn = rep_map(dsb[0:32, off + tt * 128:off + (tt + 1) * 128], REPb,
                             dstm, [(sV, dt_small_v), frees[k]], [(sG, upool_free[(j * 2 + tt) % 2])],
                             psx[k])
                frees[k] = (sS, nn)
                k ^= 1
                upool = p.op("gpsimd", lambda e, dst=dst, tt=tt, dstm=dstm: e.tensor_tensor(
                    dst[:, tt * 2048:(tt + 1) * 2048], xctok[:, tt * 2176:tt * 2176 + 2048],
                    dstm, ALU.mult), inc=sG,
                    waits=[(sS, nn), (dS, ld_xct[tt]), (sT, freet)])
                upool_free[(j * 2 + tt) % 2] = upool
        upool_done = p.cnt[sG]
        pA_free, pB_free = frees[0], frees[1]
        for i in range(4):
            w = [(sS, dt_small_s), pB_free] if i == 0 else []
            n = p.op("tensor", lambda e, i=i: e.matmul(
                pB[0:1, i * 512:(i + 1) * 512], cdcol[0:32, :], REPf[:, i * 512:(i + 1) * 512],
                start=True, stop=True), inc=sT, waits=w)
        dbgw2 = [(dG, getattr(E, "dbg_guard", 0))] if (E.dbg and c == 1) else []
        drow_done = p.op("vector", lambda e: e.tensor_copy(Abuf[0:1, :], pB[0:1, :]), inc=sV,
                         waits=[(sT, n), (sG, mdecay_free_g), (sV, abuf_free)] + dbgw2)
        for i in range(4):
            w = [(sV, drow_done), pA_free] if i == 0 else []
            n = p.op("tensor", lambda e, i=i: e.matmul(
                pA[:, i * 512:(i + 1) * 512], E.ones1f[0:1, :], Abuf[0:1, i * 512:(i + 1) * 512],
                start=True, stop=True), inc=sT, waits=w)
        mdecay_done = p.op("scalar", lambda e: e.copy(mDECAY[:, :], pA[:, :]), inc=sS,
                           waits=[(sT, n)])
        frees[0] = (sS, mdecay_done)
        frees[1] = (sV, drow_done)

        for st in range(2):
            w = [(dS, ld_bc), frees[0]] if st == 0 else []
            n = p.op("tensor", lambda e, st=st: e.matmul(
                pA[0:128, st * 256:(st + 1) * 256],
                bcfeat[:, st * 128:(st + 1) * 128],
                bcfeat[:, 256:512], start=True, stop=True), inc=sT, waits=w)
        cbt_done = p.op("scalar", lambda e: e.copy(CBT_sb[:, :], pA[:, 0:512]), inc=sS,
                        waits=[(sT, n)])
        frees[0] = (sS, cbt_done)
        for st in range(2):
            for j in range(4):
                nn = p.op("vector", lambda e, st=st, j=j: e.tensor_copy(
                    CBTrep[:, st * 1024 + j * 256: st * 1024 + (j + 1) * 256],
                    CBT_sb[:, st * 256:(st + 1) * 256]), inc=sV,
                    waits=[(sS, cbt_done), (sG, cbtrep_free_g)] if (st == 0 and j == 0) else [])
        cbtrep_done = p.cnt[sV]

        map_evs = []
        for grp in range(4):
            k2 = grp % 2
            er = expa_row if k2 == 0 else expa_rowb
            ld_row = p.op("sync", lambda e, grp=grp, er=er: e.dma_start(
                out=er.rearrange("a (h t) -> a h t", t=256),
                in_=E.sp_sm.rearrange("h (j t) -> j h t", t=256)[2:3, grp * 8:(grp + 1) * 8, :]),
                inc=dS, inck=16,
                waits=[(dG, dsb_store), (sT, expa_row_free_t[(grp + 2) % 4])])
            for i in range(4):
                w = [(dS, ld_row)]
                if grp < 2 and i == 0:
                    w += [frees[k2], (sV, mapall_free)]
                elif i == 0:
                    w.append((sS, map_evs[grp - 2]))
                n = p.op("tensor", lambda e, i=i, k2=k2, er=er: e.matmul(
                    psx[k2][:, i * 512:(i + 1) * 512], E.ones1b[0:1, :],
                    er[0:1, i * 512:(i + 1) * 512], start=True, stop=True), inc=sT,
                    waits=w)
            expa_row_free_t[grp] = n
            nn = p.op("scalar", lambda e, grp=grp, k2=k2: e.copy(
                mapALL[:, grp * 2048:(grp + 1) * 2048], psx[k2][:, :]), inc=sS,
                waits=[(sT, n)] + ([(sV, mapall_free)] if grp >= 2 else []))
            map_evs.append(nn)
        frees[0] = (sS, map_evs[2]); frees[1] = (sS, map_evs[3])
        pA_free, pB_free = frees[0], frees[1]

        mbuf_free_t = 0
        e1_last = 0
        mprime_g = {}
        for bt in range(8):
            w = [(sS, map_evs[bt // 2]), (sG, mprime_g.get(bt - 1, 0)), (dS, E.c2_loaded)]
            p.op("vector", lambda e, bt=bt: e.tensor_tensor_scan(
                LTb[:, 0:1024], mapALL[:, bt * 1024:(bt + 1) * 1024], eyeArep[:, :], 0.0,
                ALU.mult, ALU.add), inc=sV, waits=w)
            nv = p.op("vector", lambda e, bt=bt: e.tensor_tensor_scan(
                LTb[:, 1024:2048], mapALL[:, bt * 1024:(bt + 1) * 1024], eyeBrep[:, :], 0.0,
                ALU.mult, ALU.add), inc=sV)
            last_scan_v = nv
            ng = p.op("gpsimd", lambda e: e.tensor_tensor(
                Mbuf[:, 0:1024], CBTrep[:, 0:1024], LTb[:, 0:1024], ALU.mult), inc=sG,
                waits=[(sV, nv), (sV, cbtrep_done), (sT, mbuf_free_t)])
            ng = p.op("gpsimd", lambda e: e.tensor_tensor(
                Mbuf[:, 1024:2048], CBTrep[:, 1024:2048], LTb[:, 1024:2048], ALU.mult), inc=sG)
            mprime_g[bt] = ng
            if E.dbg and c == 0 and bt == 0:
                tapm = p.op("gpsimd", lambda e: e.dma_start(out=E.t_msb[:, :], in_=Mbuf[:, 0:512]),
                            inc=dG, inck=16, waits=[(sG, ng)])
                E.msb_tap = tapm
            for j in range(4):
                h = bt * 4 + j
                w = [(sG, ng), (sG, upool_done)]
                if E.dbg and c == 0 and bt == 0 and j == 0:
                    w.append((dG, E.msb_tap))
                if h == 0:
                    w += [pA_free, pB_free]
                p.op("tensor", lambda e, h=h, j=j: e.matmul(
                    pA[:, h * 64:(h + 1) * 64], Mbuf[:, j * 256:j * 256 + 128],
                    u_tok[:, h * 64:(h + 1) * 64], start=True, stop=True), inc=sT, waits=w)
                p.op("tensor", lambda e, h=h, j=j: e.matmul(
                    pB[:, h * 64:(h + 1) * 64],
                    Mbuf[:, j * 256 + 128:j * 256 + 256],
                    u_tok[:, h * 64:(h + 1) * 64], start=True, stop=False), inc=sT)
                e1_last = p.op("tensor", lambda e, h=h, j=j: e.matmul(
                    pB[:, h * 64:(h + 1) * 64],
                    Mbuf[:, 1024 + j * 256 + 128:1024 + j * 256 + 256],
                    u_tok[:, 2048 + h * 64: 2048 + (h + 1) * 64], start=False, stop=True),
                    inc=sT)
            mbuf_free_t = e1_last
        cbtrep_free_g = mprime_g[7]
        mapall_free = last_scan_v
        utok_free_t = p.cnt[sT]

        g_done = []
        skip_g = []
        e3_read = [0, 0]
        for tt in range(2):
            psr = pA if tt == 0 else pB
            w = [(sT, e1_last), (sV, abuf_free)]
            if tt == 1:
                w.append((sV, g_done[0]))
            ycp = p.op("vector", lambda e, psr=psr: e.tensor_copy(Abuf[:, :], psr[:, :]),
                       inc=sV, waits=w)
            if E.dbg and c == 0 and tt == 0:
                tap = p.op("gpsimd", lambda e: e.dma_start(out=E.t_G[:, :], in_=Abuf[:, :]),
                           inc=dG, inck=16, waits=[(sV, ycp)])
                p.op("vector", lambda e: e.engine_nop(), inc=sV, waits=[(dG, tap)])
                ycp = p.cnt[sV]
            if c > 0:
                w0 = [(sV, prevbf_done), (dS, ld_bc), (sV, ycp)]
                for h in range(NH):
                    n3 = p.op("tensor", lambda e, tt=tt, h=h, psr=psr: e.matmul(
                        psr[:, h * 64:(h + 1) * 64],
                        bcfeat[:, 256 + tt * 128: 384 + tt * 128],
                        prevbf[:, h * 64:(h + 1) * 64], start=True, stop=True), inc=sT,
                        waits=w0 if h == 0 else [])
                    w0 = []
                nn = p.op("vector", lambda e, tt=tt, psr=psr: e.scalar_tensor_tensor(
                    LTb[:, :], psr[:, :], 1.0, mEXA[:, tt * 2048:(tt + 1) * 2048],
                    ALU.mult, ALU.mult), inc=sV,
                    waits=[(sT, n3), mexa_ev[tt], (sG, mprime_g[7])])
                e3_read[tt] = nn
                p.op("vector", lambda e: e.tensor_add(Abuf[:, :], Abuf[:, :], LTb[:, :]),
                     inc=sV)
            else:
                e3_read[tt] = ycp
            if tt == 0:
                prevbf_free_t0 = p.cnt[sT]
            ng = p.op("gpsimd", lambda e, tt=tt: e.tensor_tensor(
                gtok[:, tt * 2048:(tt + 1) * 2048], xctok[:, tt * 2176:tt * 2176 + 2048],
                mSKIP[:, :], ALU.mult), inc=sG,
                waits=[(sV, skip_ready), (dS, ld_xct[tt]), (dS, gtok_free_d)])
            skip_g.append(ng)
            p.op("vector", lambda e, tt=tt: e.tensor_add(
                Abuf[:, :], Abuf[:, :], gtok[:, tt * 2048:(tt + 1) * 2048]), inc=sV,
                waits=[(sG, ng)])
            nn = p.op("vector", lambda e, tt=tt: e.tensor_tensor(
                Abuf[:, :], Abuf[:, :], siluzt[:, tt * 2048:(tt + 1) * 2048], ALU.mult), inc=sV,
                waits=[(dS, ld_sz[tt])])
            siluz_free[tt] = nn
            ns = p.op("scalar", lambda e, tt=tt: e.activation(
                gtok[:, tt * 2048:(tt + 1) * 2048], Abuf[:, :], AF.Square,
                accum_out=rcol[:, tt:tt + 1]), inc=sS, waits=[(sV, nn), (sG, ng)])
            ns = p.op("scalar", lambda e, tt=tt: e.activation(
                rcol[:, 2 + tt:3 + tt], rcol[:, tt:tt + 1], AF.Ln, scale=1.0 / 2048,
                bias=E.epsb[:, 2:3]), inc=sS)
            ns = p.op("scalar", lambda e, tt=tt: e.activation(
                rcol[:, 2 + tt:3 + tt], rcol[:, 2 + tt:3 + tt], AF.Exp, scale=-0.5), inc=sS)
            ng2 = p.op("vector", lambda e, tt=tt: e.tensor_scalar(
                gtok[:, tt * 2048:(tt + 1) * 2048], Abuf[:, :], rcol[:, 2 + tt:3 + tt],
                None, ALU.mult), inc=sV, waits=[(sS, ns)])
            g_done.append(ng2)
        abuf_free = g_done[1]
        prevbf_free_t = p.cnt[sT]
        mexa_free = [e3_read[0], e3_read[1]]
        pA_free = (sV, e3_read[0])
        pB_free = (sV, e3_read[1])

        if E.dbg and c == 0:
            p.op("gpsimd", lambda e: e.dma_start(out=E.t_dsm[:, :], in_=dsm[0:32, :]),
                 inc=dG, inck=16, waits=[(sV, dt_small_v), (sS, dt_small_s)])
            p.op("gpsimd", lambda e: e.dma_start(out=E.t_u[:, :], in_=u_tok[:, :]),
                 inc=dG, inck=16, waits=[(sG, upool_done)])
            p.op("gpsimd", lambda e: e.dma_start(out=E.t_ud[:, :], in_=udec[:, :]),
                 inc=dG, inck=16, waits=[(sG, upool_done)])
            p.op("gpsimd", lambda e: e.dma_start(out=E.t_cbt[:, :], in_=CBT_sb[:, :]),
                 inc=dG, inck=16, waits=[(sS, cbt_done)])
            p.op("gpsimd", lambda e: e.dma_start(out=E.t_mexa[:, :], in_=mapALL[:, 0:4096]),
                 inc=dG, inck=16, waits=[(sS, map_evs[1])])
            p.op("gpsimd", lambda e: e.dma_start(out=E.t_g[:, :], in_=gtok[:, :]),
                 inc=dG, inck=16, waits=[(sV, g_done[1])])
            E.dbg_guard = p.cnt[dG]

        tr = 0
        for tt in range(2):
            for hk in range(16):
                tr = p.op("sync", lambda e, tt=tt, hk=hk: e.dma_start_transpose(
                    gfeat[:, hk * 256 + tt * 128: hk * 256 + (tt + 1) * 128],
                    gtok[:, tt * 2048 + hk * 128: tt * 2048 + (hk + 1) * 128]),
                    inc=dS, inck=16,
                    waits=[(sV, g_done[tt]), (sT, gfeat_free_t)])
        gtok_free_d = tr

        st_regs = []
        for st in range(2):
            psreg = pA if st == 0 else pB
            w = [(sG, upool_done), (dS, ld_xct[st]), (pA_free if st == 0 else pB_free)]
            for h in range(NH):
                n2 = p.op("tensor", lambda e, st=st, h=h, psreg=psreg: e.matmul(
                    psreg[:, h * 64:(h + 1) * 64], xctok[:, st * 2176 + 2048: st * 2176 + 2176],
                    udec[:, st * 2048 + h * 64: st * 2048 + (h + 1) * 64],
                    start=True, stop=True), inc=sT, waits=w if h == 0 else [])
                w = []
            st_regs.append(n2)
        udec_free_t = p.cnt[sT]
        bc_free_t = p.cnt[sT]
        e2t = p.cnt[sT]
        xctok_free = [(e2t, skip_g[0]), (e2t, skip_g[1])]

        if c == 0:
            p.op("vector", lambda e: e.tensor_copy(prev[:, :], pA[:, :]), inc=sV,
                 waits=[(sT, st_regs[0])])
            prev_ready_v = p.op("vector", lambda e: e.tensor_add(prev[:, :], pB[:, :], prev[:, :]),
                                inc=sV, waits=[(sT, st_regs[1])])
        else:
            ng = p.op("gpsimd", lambda e: e.tensor_tensor(prev[:, :], prev[:, :], mDECAY[:, :],
                                                          ALU.mult), inc=sG,
                      waits=[(sS, mdecay_done), (sT, prevbf_free_t)])
            mdecay_free_g = ng
            p.op("vector", lambda e: e.tensor_add(prev[:, :], pA[:, :], prev[:, :]), inc=sV,
                 waits=[(sT, st_regs[0]), (sG, ng)])
            prev_ready_v = p.op("vector", lambda e: e.tensor_add(prev[:, :], pB[:, :], prev[:, :]),
                                inc=sV, waits=[(sT, st_regs[1])])
        pA_free = (sV, prev_ready_v)
        pB_free = (sV, prev_ready_v)

        wout_evacs = []
        for mt in range(8):
            r = mt % 4
            if mt == 0:
                w = [pA_free, (dS, tr), (dG, nwl), (sV, wout_ready)]
            elif mt >= 4:
                w = [wout_evacs[mt - 4]]
            else:
                w = []
            for kc in range(16):
                p.op("tensor", lambda e, mt=mt, kc=kc, r=r: e.matmul(
                    pA[:, r * 512: r * 512 + 256],
                    w_out_sb[:, kc * 1024 + mt * 128: kc * 1024 + (mt + 1) * 128],
                    gfeat[:, kc * 256:(kc + 1) * 256],
                    start=(kc == 0), stop=(kc == 15)),
                    inc=(sT if kc == 15 else None),
                    waits=w if kc == 0 else [])
            g = p.cnt[sT]
            hb = mt % 2
            nn = p.op("vector", lambda e, mt=mt, r=r, hb=hb: e.tensor_add(
                hbuf[:, hb * 256:(hb + 1) * 256], pA[:, r * 512:r * 512 + 256],
                xn_c[:, mt * 256:(mt + 1) * 256]), inc=sV,
                waits=[(sT, g), (dS, ld_xn), (dG, hb_free[hb])])
            wout_evacs.append((sV, nn))
            hb_free[hb] = p.op("gpsimd", lambda e, c=c, mt=mt, hb=hb: e.dma_start(
                out=E.sp_h[mt * 128:(mt + 1) * 128, c * 256:(c + 1) * 256],
                in_=hbuf[:, hb * 256:(hb + 1) * 256]), inc=dG, inck=16,
                waits=[(sV, nn)])
        gfeat_free_t = p.cnt[sT]
        xn_free_v = wout_evacs[-1][1]
        pA_free = wout_evacs[-1]
        E.p2_chunk_dG = getattr(E, "p2_chunk_dG", [0] * NCH)
        E.p2_chunk_dG[c] = p.cnt[dG]

    E.p2_done_dG = p.cnt[dG]
    if nphases == 2:
        p.op("sync", lambda e: e.nop(), waits=[(dG, E.p2_done_dG)])
    px.close()


def _phase3(E, nphases):
    nc, p = E.nc, E.p
    p.fence()
    dS, dG, sT, sV, sS, sG = E.dS, E.dG, E.sT, E.sV, E.sS, E.sG
    SL3 = 256
    px = ExitStack()
    sbl = lambda n, s, d: px.enter_context(nc.sbuf_tensor(n, s, d))
    wfc_sb = sbl("wfc_sb", [128, 8 * 4096], BF16)
    wproj_sb = sbl("wproj_sb", [128, 32 * 1024], BF16)
    h_sl = sbl("h_sl", [128, 8 * SL3], F32)
    hbf = sbl("hbf", [128, 8 * SL3], BF16)
    r_sb = sbl("r_sb", [128, 32 * SL3], BF16)
    hsq = r_sb
    ao_sb = sbl("ao_sb", [128, 8 * SL3], F32)
    x_sl = sbl("x_sl", [128, 8 * SL3], F32)
    x1_sb = sbl("x1_sb", [128, 8 * SL3], F32)
    rows3 = sbl("rows3", [1, 6 * SL3], F32)
    maps3 = sbl("maps3", [128, 3 * SL3], F32)
    pP3 = px.enter_context(nc.psum_tensor("pP3", [128, 2048], F32))
    pQ3 = px.enter_context(nc.psum_tensor("pQ3", [128, 1024], F32))
    pstA = px.enter_context(nc.psum_tensor("pstA3", [1, 256], F32))
    pstB = px.enter_context(nc.psum_tensor("pstB3", [1, 256], F32))

    mu_r = rows3[0:1, 0:SL3]
    var_r = rows3[0:1, SL3:2 * SL3]
    r1_r = rows3[0:1, 2 * SL3:3 * SL3]
    mur_r = rows3[0:1, 3 * SL3:4 * SL3]
    sc_r = rows3[0:1, 4 * SL3:5 * SL3]
    rstd2_r = rows3[0:1, 5 * SL3:6 * SL3]

    for kc in range(8):
        p.op("gpsimd", lambda e, kc=kc: e.dma_start(
            out=wfc_sb[:, kc * 4096:(kc + 1) * 4096],
            in_=E.w_fc[kc * 128:(kc + 1) * 128, :]), inc=dG, inck=16)
    for kc in range(32):
        p.op("gpsimd", lambda e, kc=kc: e.dma_start(
            out=wproj_sb[:, kc * 1024:(kc + 1) * 1024],
            in_=E.w_proj[kc * 128:(kc + 1) * 128, :]), inc=dG, inck=16)
    w_loaded = p.cnt[dG]

    hbuf_free = [0, 0]
    hbf_free_t = 0
    stats_free = 0
    rows_free_t = 0
    maps_free = 0
    r_free_t = 0
    pP_free = [None] * 4
    pQ_free = None
    ao_free = 0
    x1_stored = 0
    aocast_free_t = 0
    hsl_free_s = 0
    ao_sq_s = 0

    for s in range(16):
        ld_h = p.op("sync", lambda e, s=s: e.dma_start(
            out=h_sl.rearrange("p (g t) -> p g t", t=SL3),
            in_=E.sp_h[:, s * SL3:(s + 1) * SL3].rearrange("(g p) t -> p g t", p=128)),
            inc=dS, inck=16, waits=[(dG, E.p2_done_dG), (sV, ao_free), (sS, hsl_free_s)])
        ld_x = p.op("sync", lambda e, s=s: e.dma_start(
            out=x_sl.rearrange("p (g t) -> p g t", t=SL3),
            in_=E.xt[:, s * SL3:(s + 1) * SL3].rearrange("(g p) t -> p g t", p=128)),
            inc=dS, inck=16, waits=[(sV, x1_stored if False else 0), (sV, maps_free)])

        for ch in range(8):
            p.op("vector", lambda e, ch=ch: e.tensor_copy(
                hbf[:, ch * SL3:(ch + 1) * SL3], h_sl[:, ch * SL3:(ch + 1) * SL3]), inc=sV,
                waits=[(dS, ld_h), (sT, hbf_free_t)])
            p.op("scalar", lambda e, ch=ch: e.activation(
                hsq[:, ch * SL3:(ch + 1) * SL3], h_sl[:, ch * SL3:(ch + 1) * SL3], AF.Square),
                inc=sS, waits=[(dS, ld_h), (sT, r_free_t)])
        cast_v = p.cnt[sV]; cast_s = p.cnt[sS]
        for ch in range(8):
            p.op("tensor", lambda e, ch=ch: e.matmul(
                pstA[0:1, :], E.ones128b[:, 0:1], hsq[:, ch * SL3:(ch + 1) * SL3],
                start=(ch == 0), stop=(ch == 7)), inc=(sT if ch == 7 else None),
                waits=[(sS, cast_s), (sV, stats_free)] if ch == 0 else [])
        g1 = p.cnt[sT]
        nv = p.op("vector", lambda e: e.tensor_copy(sc_r, pstA[0:1, :]), inc=sV,
                  waits=[(sT, g1), (sT, rows_free_t)])
        ns = p.op("scalar", lambda e: e.activation(rstd2_r, sc_r, AF.Ln, scale=1.0 / 1024,
                                                   bias=E.epsb[0:1, 1:2]), inc=sS,
                  waits=[(sV, nv)])
        ns = p.op("scalar", lambda e: e.activation(rstd2_r, rstd2_r, AF.Exp, scale=-1.0), inc=sS)
        stats_evA = nv
        w = [(sS, ns)]
        if pQ_free is not None:
            w.append(pQ_free)
        n = p.op("tensor", lambda e: e.matmul(pQ3[:, 0:256], E.ones1f[0:1, :], rstd2_r,
                                              start=True, stop=True), inc=sT, waits=w)
        m_rstd2 = p.op("vector", lambda e: e.tensor_copy(maps3[:, 2 * SL3:3 * SL3], pQ3[:, 0:256]),
                       inc=sV, waits=[(sT, n)])
        pQ_free = (sV, m_rstd2)

        for mt in range(32):
            r = mt % 4
            w = [(sV, cast_v)] if mt == 0 else []
            if pP_free[r] is not None:
                w.append(pP_free[r])
            for kc in range(8):
                p.op("tensor", lambda e, mt=mt, kc=kc, r=r: e.matmul(
                    pP3[:, r * 512:r * 512 + 256],
                    wfc_sb[:, kc * 4096 + mt * 128: kc * 4096 + (mt + 1) * 128],
                    hbf[:, kc * SL3:(kc + 1) * SL3],
                    start=(kc == 0), stop=(kc == 7)), inc=(sT if kc == 7 else None),
                    waits=(w + [(dG, w_loaded)]) if kc == 0 else [])
            g = p.cnt[sT]
            n = p.op("scalar", lambda e, mt=mt, r=r: e.activation(
                r_sb[:, mt * SL3:(mt + 1) * SL3], pP3[:, r * 512:r * 512 + 256], AF.Relu),
                inc=sS, waits=[(sT, g)])
            pP_free[r] = (sS, n)
        hbf_free_t = p.cnt[sT]
        relu_s = p.cnt[sS]
        for q8 in range(8):
            p.op("vector", lambda e, q8=q8: e.tensor_tensor(
                r_sb[:, q8 * 1024:(q8 + 1) * 1024], r_sb[:, q8 * 1024:(q8 + 1) * 1024],
                r_sb[:, q8 * 1024:(q8 + 1) * 1024], ALU.mult), inc=sV,
                waits=[(sS, relu_s)] if q8 == 0 else [])
        sq_v = p.cnt[sV]

        ao_evs = []
        for mt in range(8):
            r = mt % 4
            w = [(sV, sq_v)] if mt == 0 else []
            if pP_free[r] is not None:
                w.append(pP_free[r])
            for kc in range(32):
                p.op("tensor", lambda e, mt=mt, kc=kc, r=r: e.matmul(
                    pP3[:, r * 512:r * 512 + 256],
                    wproj_sb[:, kc * 1024 + mt * 128: kc * 1024 + (mt + 1) * 128],
                    r_sb[:, kc * SL3:(kc + 1) * SL3],
                    start=(kc == 0), stop=(kc == 31)), inc=(sT if kc == 31 else None),
                    waits=w if kc == 0 else [])
            g = p.cnt[sT]
            nn = p.op("vector", lambda e, mt=mt, r=r: e.tensor_tensor(
                ao_sb[:, mt * SL3:(mt + 1) * SL3], pP3[:, r * 512:r * 512 + 256],
                maps3[:, 2 * SL3:3 * SL3], ALU.mult), inc=sV,
                waits=[(sT, g), (sV, m_rstd2), (sS, ao_sq_s)] if mt == 0 else [(sT, g)])
            nn = p.op("vector", lambda e, mt=mt: e.tensor_add(
                ao_sb[:, mt * SL3:(mt + 1) * SL3], ao_sb[:, mt * SL3:(mt + 1) * SL3],
                h_sl[:, mt * SL3:(mt + 1) * SL3]), inc=sV)
            ao_evs.append((sV, nn))
            pP_free[r] = (sV, nn)
        r_free_t = p.cnt[sT]
        ao_v = p.cnt[sV]
        ao_free = ao_v

        for ch in range(8):
            p.op("vector", lambda e, ch=ch: e.tensor_copy(
                r_sb[:, ch * SL3:(ch + 1) * SL3], ao_sb[:, ch * SL3:(ch + 1) * SL3]), inc=sV,
                waits=[(sV, ao_v), (sT, r_free_t)] if ch == 0 else [])
            p.op("scalar", lambda e, ch=ch: e.activation(
                r_sb[:, 2048 + ch * SL3:2048 + (ch + 1) * SL3],
                ao_sb[:, ch * SL3:(ch + 1) * SL3], AF.Square), inc=sS,
                waits=[(sV, ao_v), (sT, r_free_t)] if ch == 0 else [])
        c2v = p.cnt[sV]; c2s = p.cnt[sS]
        for ch in range(8):
            p.op("tensor", lambda e, ch=ch: e.matmul(
                pstA[0:1, :], E.ones128b[:, 0:1], r_sb[:, ch * SL3:(ch + 1) * SL3],
                start=(ch == 0), stop=(ch == 7)), inc=(sT if ch == 7 else None),
                waits=[(sV, c2v), (sV, stats_evA)] if ch == 0 else [])
        for ch in range(8):
            p.op("tensor", lambda e, ch=ch: e.matmul(
                pstB[0:1, :], E.ones128b[:, 0:1], r_sb[:, 2048 + ch * SL3:2048 + (ch + 1) * SL3],
                start=(ch == 0), stop=(ch == 7)), inc=(sT if ch == 7 else None),
                waits=[(sS, c2s)] if ch == 0 else [])
        g2 = p.cnt[sT]
        aocast_free_t = g2
        r_free_t = g2
        ao_sq_s = c2s
        hsl_free_s = c2s
        p.op("vector", lambda e: e.tensor_scalar(mu_r, pstA[0:1, :], 1.0 / 1024, None, ALU.mult),
             inc=sV, waits=[(sT, g2)])
        p.op("vector", lambda e: e.tensor_scalar(var_r, pstB[0:1, :], 1.0 / 1024, None, ALU.mult),
             inc=sV)
        stats_free = p.cnt[sV]
        ns = p.op("scalar", lambda e: e.activation(sc_r, mu_r, AF.Square), inc=sS,
                  waits=[(sV, stats_free)])
        nv = p.op("vector", lambda e: e.tensor_sub(var_r, var_r, sc_r), inc=sV,
                  waits=[(sS, ns)])
        ns = p.op("scalar", lambda e: e.activation(sc_r, var_r, AF.Ln, bias=E.epsb[0:1, 0:1]),
                  inc=sS, waits=[(sV, nv)])
        ns = p.op("scalar", lambda e: e.activation(r1_r, sc_r, AF.Exp, scale=-0.5), inc=sS)
        nv = p.op("vector", lambda e: e.tensor_mul(mur_r, mu_r, r1_r), inc=sV, waits=[(sS, ns)])
        rows_v = p.cnt[sV]
        prev_ev = None
        for i, row in enumerate([r1_r, mur_r]):
            w = [(sV, rows_v), pQ_free] if i == 0 else [(sV, prev_ev)]
            n = p.op("tensor", lambda e, row=row: e.matmul(
                pQ3[:, 0:256], E.ones1f[0:1, :], row, start=True, stop=True), inc=sT, waits=w)
            prev_ev = p.op("vector", lambda e, i=i: e.tensor_copy(
                maps3[:, i * SL3:(i + 1) * SL3], pQ3[:, 0:256]), inc=sV, waits=[(sT, n)])
        maps_v = p.cnt[sV]
        rows_free_t = p.cnt[sT]
        pQ_free = (sV, maps_v)
        maps_free_new = []
        for ch in range(8):
            p.op("vector", lambda e, ch=ch: e.tensor_tensor(
                x1_sb[:, ch * SL3:(ch + 1) * SL3], ao_sb[:, ch * SL3:(ch + 1) * SL3],
                maps3[:, 0:SL3], ALU.mult), inc=sV,
                waits=[(sV, maps_v), (dG, x1_stored)] if ch == 0 else [])
            p.op("vector", lambda e, ch=ch: e.tensor_sub(
                x1_sb[:, ch * SL3:(ch + 1) * SL3], x1_sb[:, ch * SL3:(ch + 1) * SL3],
                maps3[:, SL3:2 * SL3]), inc=sV)
            p.op("vector", lambda e, ch=ch: e.tensor_add(
                x1_sb[:, ch * SL3:(ch + 1) * SL3], x1_sb[:, ch * SL3:(ch + 1) * SL3],
                x_sl[:, ch * SL3:(ch + 1) * SL3]), inc=sV,
                waits=[(dS, ld_x)] if ch == 0 else [])
        x1_v = p.cnt[sV]
        maps_free = x1_v
        x1_stored = p.op("gpsimd", lambda e, s=s: e.dma_start(
            out=E.sp_x1[:, s * SL3:(s + 1) * SL3].rearrange("(g p) t -> p g t", p=128),
            in_=x1_sb.rearrange("p (g t) -> p g t", t=SL3)), inc=dG, inck=16,
            waits=[(sV, x1_v)])

    E.p3_done_dG = p.cnt[dG]
    if nphases == 3:
        p.op("sync", lambda e: e.nop(), waits=[(dG, E.p3_done_dG)])
    px.close()


def _phase4(E, nphases):
    nc, p = E.nc, E.p
    p.fence()
    dS, dG, sT, sV, sS, sG = E.dS, E.dG, E.sT, E.sV, E.sS, E.sG
    px = ExitStack()
    sbl = lambda n, s, d: px.enter_context(nc.sbuf_tensor(n, s, d))
    xn2 = sbl("xn2", [128, 8 * 4096], BF16)
    xn2lo = sbl("xn2lo", [128, 8 * 4096], BF16)

    pxa = ExitStack()
    sba = lambda n, s, d: pxa.enter_context(nc.sbuf_tensor(n, s, d))
    x1_sl = sba("x1_sl", [128, 8 * SL], F32)
    x1bf = sba("x1bf", [128, 8 * SL], BF16)
    x1sq = sba("x1sq", [128, 8 * SL], BF16)
    rows4 = sba("rows4", [1, 5 * SL], F32)
    maps4 = sba("maps4", [128, 2 * SL], F32)
    tch4 = sba("tch4", [128, SL], F32)
    xn2f = sba("xn2f", [128, SL], F32)
    p4a = pxa.enter_context(nc.psum_tensor("p4a", [128, 512], F32))
    pstA = pxa.enter_context(nc.psum_tensor("pstA4", [1, 512], F32))
    pstB = pxa.enter_context(nc.psum_tensor("pstB4", [1, 512], F32))
    mu_r = rows4[0:1, 0:SL]; var_r = rows4[0:1, SL:2 * SL]
    r1_r = rows4[0:1, 2 * SL:3 * SL]; mur_r = rows4[0:1, 3 * SL:4 * SL]
    sc_r = rows4[0:1, 4 * SL:5 * SL]

    cast_free_t = 0
    stats_free = 0
    rows_free_t = 0
    maps_free = 0
    pm_free = None
    for s in range(NSL):
        ld = p.op("sync", lambda e, s=s: e.dma_start(
            out=x1_sl.rearrange("p (g t) -> p g t", t=SL),
            in_=E.sp_x1[:, s * SL:(s + 1) * SL].rearrange("(g p) t -> p g t", p=128)),
            inc=dS, inck=16, waits=[(dG, E.p3_done_dG), (sV, maps_free)])
        for ch in range(8):
            p.op("vector", lambda e, ch=ch: e.tensor_copy(
                x1bf[:, ch * SL:(ch + 1) * SL], x1_sl[:, ch * SL:(ch + 1) * SL]), inc=sV,
                waits=[(dS, ld), (sT, cast_free_t)])
            p.op("scalar", lambda e, ch=ch: e.activation(
                x1sq[:, ch * SL:(ch + 1) * SL], x1_sl[:, ch * SL:(ch + 1) * SL], AF.Square),
                inc=sS, waits=[(dS, ld), (sT, cast_free_t)])
        cast_v = p.cnt[sV]; cast_s = p.cnt[sS]
        for ch in range(8):
            p.op("tensor", lambda e, ch=ch: e.matmul(
                pstA[0:1, :], E.ones128b[:, 0:1], x1bf[:, ch * SL:(ch + 1) * SL],
                start=(ch == 0), stop=(ch == 7)), inc=(sT if ch == 7 else None),
                waits=[(sV, cast_v), (sV, stats_free)] if ch == 0 else [])
        for ch in range(8):
            p.op("tensor", lambda e, ch=ch: e.matmul(
                pstB[0:1, :], E.ones128b[:, 0:1], x1sq[:, ch * SL:(ch + 1) * SL],
                start=(ch == 0), stop=(ch == 7)), inc=(sT if ch == 7 else None),
                waits=[(sS, cast_s)] if ch == 0 else [])
        g1 = p.cnt[sT]
        cast_free_t = g1
        p.op("vector", lambda e: e.tensor_scalar(mu_r, pstA[0:1, :], 1.0 / 1024, None, ALU.mult),
             inc=sV, waits=[(sT, g1), (sT, rows_free_t)])
        p.op("vector", lambda e: e.tensor_scalar(var_r, pstB[0:1, :], 1.0 / 1024, None, ALU.mult),
             inc=sV)
        stats_free = p.cnt[sV]
        ns = p.op("scalar", lambda e: e.activation(sc_r, mu_r, AF.Square), inc=sS,
                  waits=[(sV, stats_free), (sT, rows_free_t)])
        nv = p.op("vector", lambda e: e.tensor_sub(var_r, var_r, sc_r), inc=sV, waits=[(sS, ns)])
        ns = p.op("scalar", lambda e: e.activation(sc_r, var_r, AF.Ln, bias=E.epsb[0:1, 0:1]),
                  inc=sS, waits=[(sV, nv)])
        ns = p.op("scalar", lambda e: e.activation(r1_r, sc_r, AF.Exp, scale=-0.5), inc=sS)
        rows_v = p.cnt[sV]; rows_s = p.cnt[sS]
        prev_ev = None
        for i, row in enumerate([mu_r, r1_r]):
            w = [(sS, rows_s), (sV, rows_v)]
            if i == 0:
                w += [pm_free, (sV, maps_free)]
            else:
                w.append((sV, prev_ev))
            n = p.op("tensor", lambda e, row=row: e.matmul(
                p4a[:, 0:512], E.ones1f[0:1, :], row, start=True, stop=True), inc=sT, waits=w)
            prev_ev = p.op("vector", lambda e, i=i: e.tensor_copy(
                maps4[:, i * SL:(i + 1) * SL], p4a[:, 0:512]), inc=sV, waits=[(sT, n)])
        maps_v = p.cnt[sV]
        rows_free_t = p.cnt[sT]
        pm_free = (sV, maps_v)
        for ch in range(8):
            p.op("vector", lambda e, ch=ch: e.tensor_sub(
                tch4[:, :], x1_sl[:, ch * SL:(ch + 1) * SL], maps4[:, 0:SL]), inc=sV,
                waits=[(sV, maps_v)] if ch == 0 else [])
            p.op("vector", lambda e, ch=ch: e.tensor_mul(
                xn2f[:, :], tch4[:, :], maps4[:, SL:2 * SL]), inc=sV)
            p.op("vector", lambda e, ch=ch, s=s: e.tensor_copy(
                xn2[:, ch * 4096 + s * SL: ch * 4096 + (s + 1) * SL], xn2f[:, :]), inc=sV)
            p.op("vector", lambda e, ch=ch, s=s: e.tensor_sub(
                xn2lo[:, ch * 4096 + s * SL: ch * 4096 + (s + 1) * SL], xn2f[:, :],
                xn2[:, ch * 4096 + s * SL: ch * 4096 + (s + 1) * SL]), inc=sV)
        maps_free = p.cnt[sV]
    xn2_done_v = p.cnt[sV]
    pxa.close()
    p.fence()

    w1_sb = sbl("w1_sb", [128, 8 * 1024], BF16)
    w2_sb = sbl("w2_sb", [128, 8 * 1024], BF16)
    nwT_sb = sbl("nwT_sb", [128, 8 * 15], BF16)
    nwTlo_sb = sbl("nwTlo_sb", [128, 8 * 15], BF16)
    nwTst = sbl("nwTst", [128, 8 * 15], F32)
    nb_sb = sbl("nb_sb", [15, 1], F32)
    cbit = sbl("cbit", [16, 512], BF16)
    b1c_sb = sbl("b1c_sb", [128, 8], F32)
    b2_sb = sbl("b2_sb", [16, 1024], BF16)
    hid = sbl("hid", [128, 8 * 512], BF16)
    out_sb = sbl("out_sb", [128, 8 * 512], BF16)
    selT = sbl("selT", [16, 512], BF16)
    LEAF = sbl("LEAF", [128, 512], BF16)
    pF = px.enter_context(nc.psum_tensor("pF", [128, 2048], F32))
    pG = px.enter_context(nc.psum_tensor("pG", [128, 1536], F32))

    wload_free = [0, 0, 0]
    nwv_prev = 0
    for bi, br in enumerate("qkv"):
        ff = E.fff[br]
        wl = []
        wld = [(sT, wload_free[0])]
        for kc in range(8):
            wl.append(p.op("gpsimd", lambda e, kc=kc, ff=ff: e.dma_start(
                out=w1_sb[:, kc * 1024:(kc + 1) * 1024],
                in_=ff["w1"][kc * 128:(kc + 1) * 128, :]), inc=dG, inck=16, waits=wld))
            wld = []
        for kc in range(8):
            wl.append(p.op("gpsimd", lambda e, kc=kc, ff=ff: e.dma_start(
                out=w2_sb[:, kc * 1024:(kc + 1) * 1024],
                in_=ff["w2"][kc * 128:(kc + 1) * 128, :]), inc=dG, inck=16))
        for kc in range(8):
            wl.append(p.op("gpsimd", lambda e, kc=kc, ff=ff: e.dma_start(
                out=nwTst[:, kc * 15:(kc + 1) * 15],
                in_=ff["nwT"][kc * 128:(kc + 1) * 128, :]), inc=dG, inck=16,
                waits=[(sV, nwv_prev)] if kc == 0 else []))
        wl.append(p.op("gpsimd", lambda e, ff=ff: e.dma_start(out=nb_sb[:, :], in_=ff["nb"][:, :]),
                       inc=dG, inck=16))
        wl.append(p.op("gpsimd", lambda e, ff=ff: e.dma_start(out=b1c_sb[:, :], in_=ff["b1c"][:, :]),
                       inc=dG, inck=16))
        wl.append(p.op("gpsimd", lambda e, ff=ff: e.dma_start(out=b2_sb[:, :], in_=ff["b2"][:, :]),
                       inc=dG, inck=16))
        w_ld = p.cnt[dG]
        p.op("vector", lambda e: e.tensor_copy(nwT_sb[:, :], nwTst[:, :]), inc=sV,
             waits=[(dG, w_ld), (sT, wload_free[0])])
        nwv = p.op("vector", lambda e: e.tensor_sub(nwTlo_sb[:, :], nwTst[:, :], nwT_sb[:, :]),
                   inc=sV)
        nwv_prev = nwv

        pF_free = [None] * 4
        pG_free = [None] * 3
        sel_free_t = 0
        leaf_free_v = 0
        hid_free_t = 0
        out_stored = 0
        for s in range(NSL):
            w = [(sV, xn2_done_v), (dG, w_ld), (sV, nwv)]
            if pF_free[0] is not None:
                w.append(pF_free[0])
            first = True
            for wmat, xoff in ((nwT_sb, 0), (nwT_sb, 1), (nwTlo_sb, 0)):
                xsrc = xn2 if xoff == 0 else xn2lo
                for kc in range(8):
                    last = (not first) and (wmat is nwTlo_sb) and kc == 7
                    p.op("tensor", lambda e, s=s, kc=kc, wmat=wmat, xsrc=xsrc, first=first, last=last: e.matmul(
                        pF[0:15, 0:512],
                        wmat[:, kc * 15:(kc + 1) * 15],
                        xsrc[:, kc * 4096 + s * 512: kc * 4096 + (s + 1) * 512],
                        start=first, stop=last), inc=(sT if last else None),
                        waits=w if first else [])
                    first = False
            g = p.cnt[sT]
            tapw = [(dG, getattr(E, "p4_tap", 0))] if (E.dbg and br == "q" and s == 1) else []
            ncb = p.op("vector", lambda e: e.tensor_scalar(
                cbit[0:15, :], pF[0:15, 0:512], nb_sb[0:15, 0:1], 0.0, ALU.add, ALU.is_ge),
                inc=sV, waits=[(sT, g)] + tapw)
            pF_free[0] = (sV, ncb)
            w = [(sV, ncb)]
            if pF_free[1] is not None:
                w.append(pF_free[1])
            p.op("tensor", lambda e: e.matmul(
                pF[0:16, 512:1024], E.Abit[0:15, :], cbit[0:15, :],
                start=True, stop=False), inc=None, waits=w)
            n = p.op("tensor", lambda e: e.matmul(
                pF[0:16, 512:1024], E.negrow[0:1, :], E.ones512b[0:1, :],
                start=False, stop=True), inc=sT)
            nst = p.op("vector", lambda e: e.tensor_scalar(
                selT[0:16, :], pF[0:16, 512:1024], 4.0, None, ALU.is_equal), inc=sV,
                waits=[(sT, n)])
            pF_free[1] = (sV, nst)
            sel_free_t = 0
            w = [(sV, nst), (sV, leaf_free_v)]
            if pG_free[0] is not None:
                w.append(pG_free[0])
            n = p.op("tensor", lambda e: e.matmul(
                pG[:, 0:512], E.IDX16[:, :], selT[0:16, :], start=True, stop=True), inc=sT,
                waits=w)
            nlf = p.op("vector", lambda e: e.tensor_copy(LEAF[:, :], pG[:, 0:512]), inc=sV,
                       waits=[(sT, n)])
            pG_free[0] = (sV, nlf)
            mask_v = []
            for hk in range(8):
                r = hk % 3
                w = []
                if hk == 0:
                    w = [(sV, xn2_done_v)]
                if pG_free[r] is not None:
                    w.append(pG_free[r])
                for kc in range(8):
                    p.op("tensor", lambda e, s=s, hk=hk, kc=kc, r=r: e.matmul(
                        pG[:, r * 512:(r + 1) * 512],
                        w1_sb[:, kc * 1024 + hk * 128: kc * 1024 + (hk + 1) * 128],
                        xn2[:, kc * 4096 + s * 512: kc * 4096 + (s + 1) * 512],
                        start=(kc == 0), stop=(kc == 7)), inc=(sT if kc == 7 else None),
                        waits=w if kc == 0 else [])
                g = p.cnt[sT]
                ns = p.op("scalar", lambda e, hk=hk, r=r: e.activation(
                    hid[:, hk * 512:(hk + 1) * 512], pG[:, r * 512:(r + 1) * 512], AF.Relu,
                    bias=b1c_sb[:, hk:hk + 1]), inc=sS,
                    waits=[(sT, g), (sT, hid_free_t)] if hk == 0 else [(sT, g)])
                nm = p.op("vector", lambda e, hk=hk: e.scalar_tensor_tensor(
                    hid[:, hk * 512:(hk + 1) * 512], LEAF[:, :], E.leafof[:, hk:hk + 1],
                    hid[:, hk * 512:(hk + 1) * 512], ALU.is_equal, ALU.mult), inc=sV,
                    waits=[(sS, ns), (sV, nlf)])
                mask_v.append(nm)
                pG_free[r] = (sV, nm)
            hid_v = p.cnt[sV]
            leaf_free_v = hid_v
            if br in "qk":
                spill = E.sp_q if br == "q" else E.sp_k
                for mt in range(8):
                    r = mt % 4
                    w = [(sV, hid_v)] if mt == 0 else []
                    if pF_free[r] is not None:
                        w.append(pF_free[r])
                    for kc in range(8):
                        p.op("tensor", lambda e, mt=mt, kc=kc, r=r: e.matmul(
                            pF[:, r * 512:(r + 1) * 512],
                            w2_sb[:, kc * 1024 + mt * 128: kc * 1024 + (mt + 1) * 128],
                            hid[:, kc * 512:(kc + 1) * 512],
                            start=(kc == 0), stop=False), inc=None,
                            waits=w if kc == 0 else [])
                    p.op("tensor", lambda e, mt=mt, r=r: e.matmul(
                        pF[:, r * 512:(r + 1) * 512], b2_sb[:, mt * 128:(mt + 1) * 128],
                        selT[0:16, :], start=False, stop=True), inc=sT)
                    g = p.cnt[sT]
                    nn = p.op("vector", lambda e, mt=mt, r=r: e.tensor_copy(
                        out_sb[:, mt * 512:(mt + 1) * 512], pF[:, r * 512:(r + 1) * 512]),
                        inc=sV, waits=[(sT, g), (dG, out_stored)] if mt == 0 else [(sT, g)])
                    pF_free[r] = (sV, nn)
                ev = p.cnt[sV]
                hid_free_t = p.cnt[sT]
                out_stored = p.op("gpsimd", lambda e, s=s, spill=spill: e.dma_start(
                    out=spill[:, s * 512:(s + 1) * 512].rearrange("(g p) t -> p g t", p=128),
                    in_=out_sb.rearrange("p (g t) -> p g t", t=512)), inc=dG, inck=16,
                    waits=[(sV, ev)])
            else:
                for tt in range(4):
                    for dv in range(2):
                        r = (tt * 2 + dv) % 4
                        w = [(sV, hid_v)] if (tt == 0 and dv == 0) else []
                        if pF_free[r] is not None:
                            w.append(pF_free[r])
                        for kc in range(8):
                            p.op("tensor", lambda e, tt=tt, dv=dv, kc=kc, r=r: e.matmul(
                                pF[:, r * 512:(r + 1) * 512],
                                hid[:, kc * 512 + tt * 128: kc * 512 + (tt + 1) * 128],
                                w2_sb[:, kc * 1024 + dv * 512: kc * 1024 + (dv + 1) * 512],
                                start=(kc == 0), stop=False), inc=None,
                                waits=w if kc == 0 else [])
                        p.op("tensor", lambda e, tt=tt, dv=dv, r=r: e.matmul(
                            pF[:, r * 512:(r + 1) * 512], selT[0:16, tt * 128:(tt + 1) * 128],
                            b2_sb[:, dv * 512:(dv + 1) * 512], start=False, stop=True), inc=sT)
                        g = p.cnt[sT]
                        nn = p.op("vector", lambda e, tt=tt, dv=dv, r=r: e.tensor_copy(
                            out_sb[:, tt * 1024 + dv * 512: tt * 1024 + (dv + 1) * 512],
                            pF[:, r * 512:(r + 1) * 512]), inc=sV,
                            waits=[(sT, g), (dG, out_stored)] if (tt == 0 and dv == 0) else [(sT, g)])
                        pF_free[r] = (sV, nn)
                ev = p.cnt[sV]
                hid_free_t = p.cnt[sT]
                out_stored = p.op("gpsimd", lambda e, s=s: e.dma_start(
                    out=E.sp_v[s * 512:(s + 1) * 512, :].rearrange("(t p) d -> p t d", p=128),
                    in_=out_sb.rearrange("p (t d) -> p t d", d=1024)), inc=dG, inck=16,
                    waits=[(sV, ev)])
            if E.dbg and br == "q" and s == 0:
                p.op("gpsimd", lambda e: e.dma_start(
                    out=E.t_xn2.rearrange("p (c t) -> p c t", t=512),
                    in_=xn2.rearrange("p (c t) -> p c t", t=4096)[:, :, 0:512]),
                    inc=dG, inck=16, waits=[(sV, xn2_done_v)])
                p.op("gpsimd", lambda e: e.dma_start(out=E.t_selT[:, :], in_=selT[0:16, :]),
                     inc=dG, inck=16, waits=[(sV, nst)])
                p.op("gpsimd", lambda e: e.dma_start(out=E.t_leaf[:, :], in_=LEAF[:, :]),
                     inc=dG, inck=16, waits=[(sV, nlf)])
                p.op("gpsimd", lambda e: e.dma_start(out=E.t_hid[:, :], in_=hid[:, :]),
                     inc=dG, inck=16, waits=[(sV, hid_v)])
                p.op("gpsimd", lambda e: e.dma_start(out=E.t_fout[:, :], in_=out_sb[:, :]),
                     inc=dG, inck=16, waits=[(sV, ev)])
                E.p4_tap = p.cnt[dG]
        wload_free[0] = p.cnt[sT]

    E.p4_done_dG = p.cnt[dG]
    if nphases == 4:
        p.op("sync", lambda e: e.nop(), waits=[(dG, E.p4_done_dG)])
    px.close()


def _phase5(E):
    nc, p = E.nc, E.p
    p.fence()
    dS, dG, sT, sV, sS, sG = E.dS, E.dG, E.sT, E.sV, E.sS, E.sG
    dC = E.dC
    px = ExitStack()
    sbl = lambda n, s, d: px.enter_context(nc.sbuf_tensor(n, s, d))
    k_sb = sbl("k_sb", [128, 8 * 4096], BF16)
    v_sb = sbl("v_sb", [128, 32 * 1024], BF16)
    q_sb = sbl("q_sb", [128, 8 * 256], BF16)
    A_sb = sbl("A_sb", [128, 4096], F32)
    P_sb = sbl("P_sb", [128, 4096], BF16)
    PT_sb = sbl("PT_sb", [128, 32 * 256], BF16)
    u_sb = sbl("u_sb", [128, 8 * 512], F32)
    rows5 = sbl("rows5", [1, 4 * 512], F32)
    maps5 = sbl("maps5", [128, 2 * 512], F32)
    o_sb = sbl("o_sb", [128, 2 * 512], F32)
    dcol = sbl("dcol", [128, 4], F32)
    dbf = sbl("dbf", [128, 4], BF16)
    mcol = sbl("mcol", [128, 2], F32)
    pX = px.enter_context(nc.psum_tensor("pX", [128, 2048], F32))
    pY = px.enter_context(nc.psum_tensor("pY", [128, 2048], F32))

    sums_r = rows5[0:1, 0:512]; var_r = rows5[0:1, 512:1024]
    mur_r = rows5[0:1, 512:1024]
    drow_r = rows5[0:1, 1024:1536]
    rstd_r = rows5[0:1, 1024:1536]
    sc_r = rows5[0:1, 1536:2048]

    kv = []
    for kc in range(8):
        kv.append(p.op("sync", lambda e, kc=kc: e.dma_start(
            out=k_sb[:, kc * 4096:(kc + 1) * 4096],
            in_=E.sp_k[kc * 128:(kc + 1) * 128, :]), inc=dS, inck=16,
            waits=[(dG, E.p4_done_dG)]))
    for i in range(32):
        kv.append(p.op("sync", lambda e, i=i: e.dma_start(
            out=v_sb[:, i * 1024:(i + 1) * 1024],
            in_=E.sp_v[i * 128:(i + 1) * 128, :]), inc=dS, inck=16))
    kv_ld = p.cnt[dS]

    pX_free = [None] * 4
    pY_free = [None] * 8
    A_guard_dG = 0
    pst_free = None
    q_free_t = 0
    A_free_s = 0
    P_free_d = 0
    P_free_c = 0
    PT_free_t = [0, 0]
    u_free_v = 0
    P_guard_t = 0
    rows_free_t = 0
    maps_free_v = 0
    o_stored = [0, 0]
    mcol_free_s = 0
    dcol_free_v = 0

    for qb in range(NSL):
        exp_s = [0, 0, 0, 0]
        pend = []

        def emit_pv(sub, trS, trC, qt):
            nonlocal_vars = None
            for dv in range(8):
                reg = (dv % 4) * 512 + (dv // 4) * 256
                w = [(dS, trS), (dC, trC), (dS, kv_ld)]
                if pY_free[dv] is not None:
                    w.append(pY_free[dv])
                if pY_free[(dv + 4) % 8] is not None:
                    w.append(pY_free[(dv + 4) % 8])
                for kc in range(32):
                    p.op("tensor", lambda e, dv=dv, kc=kc, sub=sub, reg=reg: e.matmul(
                        pY[:, reg:reg + 128],
                        v_sb[:, kc * 1024 + dv * 128: kc * 1024 + (dv + 1) * 128],
                        PT_sb[:, kc * 256 + sub * 128: kc * 256 + sub * 128 + 128],
                        start=(kc == 0), stop=(kc == 31)), inc=(sT if kc == 31 else None),
                        waits=w if kc == 0 else [])
                g = p.cnt[sT]
                nn = p.op("vector", lambda e, dv=dv, qt=qt, reg=reg: e.tensor_copy(
                    u_sb[:, dv * 512 + qt * 128: dv * 512 + qt * 128 + 128],
                    pY[:, reg:reg + 128]), inc=sV,
                    waits=[(sT, g), (sV, u_free_v)] if dv == 0 else [(sT, g)])
                pY_free[dv] = (sV, nn)
            PT_free_t[sub] = p.cnt[sT]

        for half in range(2):
            ld_q = p.op("sync", lambda e, qb=qb, half=half: e.dma_start(
                out=q_sb.rearrange("p (g t) -> p g t", t=256),
                in_=E.sp_q[:, qb * 512 + half * 256:qb * 512 + (half + 1) * 256]
                    .rearrange("(g p) t -> p g t", p=128)),
                inc=dS, inck=16, waits=[(sT, q_free_t)])
            for sub in range(2):
                qt = half * 2 + sub
                for ks in range(8):
                    r = ks % 4
                    w = [(dS, ld_q), (dS, kv_ld)]
                    if pX_free[r] is not None:
                        w.append(pX_free[r])
                    for kc in range(8):
                        p.op("tensor", lambda e, qb=qb, sub=sub, ks=ks, kc=kc, r=r: e.matmul(
                            pX[:, r * 512:(r + 1) * 512],
                            q_sb[:, kc * 256 + sub * 128:kc * 256 + (sub + 1) * 128],
                            k_sb[:, kc * 4096 + ks * 512:kc * 4096 + (ks + 1) * 512],
                            start=(kc == 0), stop=(kc == 7)), inc=(sT if kc == 7 else None),
                            waits=w if kc == 0 else [])
                    g = p.cnt[sT]
                    nn = p.op("vector", lambda e, ks=ks, r=r: e.tensor_copy(
                        A_sb[:, ks * 512:(ks + 1) * 512], pX[:, r * 512:(r + 1) * 512]),
                        inc=sV, waits=[(sT, g), (sS, A_free_s), (dG, A_guard_dG)]
                        if ks == 0 else [(sT, g)])
                    pX_free[r] = (sV, nn)
                if sub == 1:
                    q_free_t = p.cnt[sT]
                a_v = p.cnt[sV]
                nn = p.op("vector", lambda e: e.reduce_max(
                    mcol[:, 0:1], A_sb[:, :], axis=AXX, negate=True), inc=sV,
                    waits=[(sS, mcol_free_s)])
                p.op("vector", lambda e: e.tensor_copy(mcol[:, 1:2], mcol[:, 0:1]), inc=sV)
                p.op("vector", lambda e: e.tensor_copy(mcol[:, 1:2], mcol[:, 0:1]), inc=sV)
                nn = p.op("vector", lambda e: e.tensor_scalar(
                    A_sb[:, :], A_sb[:, :], mcol[:, 0:1], SCALE_ATT, ALU.add, ALU.mult),
                    inc=sV)
                ns = p.op("scalar", lambda e, qt=qt: e.activation(
                    P_sb[:, :], A_sb[:, :], AF.Exp,
                    accum_out=dcol[:, qt:qt + 1]), inc=sS,
                    waits=[(sV, nn), (dS, P_free_d), (dC, P_free_c), (sV, dcol_free_v),
                           (sT, P_guard_t)])
                exp_s[qt] = ns
                A_free_s = ns
                mcol_free_s = ns
                if E.dbg and qb == 0 and qt == 0:
                    A_guard_dG = p.op("gpsimd", lambda e: e.dma_start(
                        out=E.t_A5[:, :], in_=A_sb[:, :]), inc=dG, inck=16,
                        waits=[(sV, a_v), (sS, ns)])
                trS = trC = 0
                for kc in range(32):
                    eng, sem = (("sync", dS) if kc % 2 == 0 else ("scalar", dC))
                    w = [(sS, ns)]
                    if kc < 2:
                        w.append((sT, PT_free_t[sub]))
                    n = p.op(eng, lambda e, kc=kc, sub=sub: e.dma_start_transpose(
                        PT_sb[:, kc * 256 + sub * 128: kc * 256 + sub * 128 + 128],
                        P_sb[:, kc * 128:(kc + 1) * 128]), inc=sem, inck=16, waits=w)
                    if kc % 2 == 0:
                        trS = n
                    else:
                        trC = n
                P_free_d = trS
                P_free_c = trC
                pend.append((sub, trS, trC, qt))
                if len(pend) > 1:
                    emit_pv(*pend.pop(0))
        while pend:
            emit_pv(*pend.pop(0))
        u_v = p.cnt[sV]

        for ch in range(8):
            p.op("scalar", lambda e, ch=ch: e.activation(
                P_sb[:, ch * 512:(ch + 1) * 512], u_sb[:, ch * 512:(ch + 1) * 512], AF.Square),
                inc=sS, waits=[(sV, u_v), (dS, P_free_d)] if ch == 0 else [])
        usq_s = p.cnt[sS]
        for ch in range(8):
            w = []
            if ch == 0:
                w = [(sS, usq_s)]
                if pst_free is not None:
                    w.append(pst_free)
            p.op("tensor", lambda e, ch=ch: e.matmul(
                pX[0:1, 0:512], E.ones128b[:, 0:1], P_sb[:, ch * 512:(ch + 1) * 512],
                start=(ch == 0), stop=(ch == 7)), inc=(sT if ch == 7 else None), waits=w)
        for ch in range(8):
            p.op("tensor", lambda e, ch=ch: e.matmul(
                pX[0:1, 512:1024], E.ones128f[:, 0:1], u_sb[:, ch * 512:(ch + 1) * 512],
                start=(ch == 0), stop=(ch == 7)), inc=(sT if ch == 7 else None),
                waits=[(sV, u_v)] if ch == 0 else [])
        g1 = p.cnt[sT]
        P_guard_t = g1
        nv = p.op("vector", lambda e: e.tensor_copy(dbf[:, :], dcol[:, :]), inc=sV,
                  waits=[(sS, exp_s[3])])
        dcol_free_v = nv
        for qt in range(4):
            p.op("tensor", lambda e, qt=qt: e.matmul(
                pX[0:1, 1024 + qt * 128: 1024 + (qt + 1) * 128], dbf[:, qt:qt + 1],
                E.I128b[:, :], start=True, stop=True), inc=sT,
                waits=[(sV, nv)] if qt == 0 else [])
        g2 = p.cnt[sT]
        p.op("vector", lambda e: e.tensor_scalar(var_r, pX[0:1, 0:512], 1.0 / 1024, None,
                                                 ALU.mult), inc=sV,
             waits=[(sT, g2), (sT, rows_free_t)])
        p.op("vector", lambda e: e.tensor_scalar(sums_r, pX[0:1, 512:1024], 1.0 / 1024, None,
                                                 ALU.mult), inc=sV)
        p.op("vector", lambda e: e.tensor_copy(drow_r, pX[0:1, 1024:1536]), inc=sV)
        st_ev = p.cnt[sV]
        pst_free = (sV, st_ev)
        ns = p.op("scalar", lambda e: e.activation(sc_r, sums_r, AF.Square), inc=sS,
                  waits=[(sV, st_ev)])
        nv = p.op("vector", lambda e: e.tensor_sub(var_r, var_r, sc_r), inc=sV, waits=[(sS, ns)])
        ns = p.op("scalar", lambda e: e.activation(sc_r, drow_r, AF.Square), inc=sS,
                  waits=[(sV, nv)])
        nv = p.op("vector", lambda e: e.scalar_tensor_tensor(
            sc_r, sc_r, EPS_LN, var_r, ALU.mult, ALU.add), inc=sV, waits=[(sS, ns)])
        ns = p.op("scalar", lambda e: e.activation(sc_r, sc_r, AF.Ln), inc=sS, waits=[(sV, nv)])
        ns = p.op("scalar", lambda e: e.activation(rstd_r, sc_r, AF.Exp, scale=-0.5), inc=sS)
        nv = p.op("vector", lambda e: e.tensor_mul(mur_r, sums_r, rstd_r), inc=sV,
                  waits=[(sS, ns)])
        rows_v = p.cnt[sV]
        prev_ev = None
        for i, row in enumerate([rstd_r, mur_r]):
            w = [(sV, rows_v)]
            if i == 0:
                w += [(sV, maps_free_v)]
                if pX_free[3] is not None:
                    w.append(pX_free[3])
            else:
                w.append((sV, prev_ev))
            n = p.op("tensor", lambda e, row=row: e.matmul(
                pX[:, 1536:2048], E.ones1f[0:1, :], row, start=True, stop=True), inc=sT, waits=w)
            prev_ev = p.op("vector", lambda e, i=i: e.tensor_copy(
                maps5[:, i * 512:(i + 1) * 512], pX[:, 1536:2048]), inc=sV, waits=[(sT, n)])
        maps_v = p.cnt[sV]
        rows_free_t = p.cnt[sT]
        pX_free[3] = (sV, maps_v)
        ld_x1 = p.op("sync", lambda e, qb=qb: e.dma_start(
            out=A_sb.rearrange("p (g t) -> p g t", t=512),
            in_=E.sp_x1[:, qb * 512:(qb + 1) * 512].rearrange("(g p) t -> p g t", p=128)),
            inc=dS, inck=16, waits=[(sS, exp_s[3])])
        for dv in range(8):
            hb = dv % 2
            w = [(sV, maps_v), (dG, o_stored[hb])]
            p.op("vector", lambda e, dv=dv, hb=hb: e.tensor_tensor(
                o_sb[:, hb * 512:(hb + 1) * 512], u_sb[:, dv * 512:(dv + 1) * 512],
                maps5[:, 0:512], ALU.mult), inc=sV, waits=w)
            p.op("vector", lambda e, dv=dv, hb=hb: e.tensor_sub(
                o_sb[:, hb * 512:(hb + 1) * 512], o_sb[:, hb * 512:(hb + 1) * 512],
                maps5[:, 512:1024]), inc=sV)
            nn = p.op("vector", lambda e, dv=dv, hb=hb: e.tensor_add(
                o_sb[:, hb * 512:(hb + 1) * 512], o_sb[:, hb * 512:(hb + 1) * 512],
                A_sb[:, dv * 512:(dv + 1) * 512]), inc=sV,
                waits=[(dS, ld_x1)] if dv == 0 else [])
            o_stored[hb] = p.op("gpsimd", lambda e, qb=qb, dv=dv, hb=hb: e.dma_start(
                out=E.out_t[dv * 128:(dv + 1) * 128, qb * 512:(qb + 1) * 512],
                in_=o_sb[:, hb * 512:(hb + 1) * 512]), inc=dG, inck=16,
                waits=[(sV, nn)])
        if E.dbg and qb == 0:
            p.op("gpsimd", lambda e: e.dma_start(out=E.t_u5[:, :], in_=u_sb[:, :]),
                 inc=dG, inck=16, waits=[(sV, u_v)])
            p.op("gpsimd", lambda e: e.dma_start(out=E.t_d5[:, :], in_=dcol[:, :]),
                 inc=dG, inck=16, waits=[(sS, exp_s[3])])
            p.op("gpsimd", lambda e: e.dma_start(out=E.t_r5[0:1, :], in_=rows5[0:1, :]),
                 inc=dG, inck=16, waits=[(sV, rows_v)])
            p.op("gpsimd", lambda e: e.dma_start(out=E.t_pt5[:, :], in_=PT_sb[:, :]),
                 inc=dG, inck=16, waits=[(dS, P_free_d), (E.dC, P_free_c)])
            E.p5_tap = p.cnt[dG]
        u_free_v = p.cnt[sV]
        maps_free_v = p.cnt[sV]
        A_free_s = exp_s[3]
        pX_free = [(sV, p.cnt[sV])] * 4
        if E.dbg and qb == 0:
            pX_free = [(dG, E.p5_tap)] * 4
            u_free_v = 0
            pY_free = [(dG, E.p5_tap)] * 8

    p.op("sync", lambda e: e.nop(), waits=[(dG, p.cnt[dG]), (dS, p.cnt[dS]), (E.dC, p.cnt[E.dC])])
    px.close()



def _prep_weights(inputs):
    f32 = lambda a: np.ascontiguousarray(np.asarray(a, np.float32))
    w = {}
    w["w_in"] = f32(inputs["L0_attn_w_in"])
    w["conv_wl"] = f32(np.asarray(inputs["L0_attn_conv_w"]).reshape(18, 128, 4).transpose(1, 0, 2).reshape(128, 72))
    w["conv_bl"] = f32(np.asarray(inputs["L0_attn_conv_b"]).reshape(18, 128).T)
    w["dt_bias"] = f32(np.asarray(inputs["L0_attn_dt_bias"]).reshape(32, 1))
    w["a_log"] = f32(np.asarray(inputs["L0_attn_a_log"]).reshape(32, 1))
    w["d_skip"] = f32(np.asarray(inputs["L0_attn_d_skip"]).reshape(32, 1))
    w["normw"] = f32(np.asarray(inputs["L0_attn_norm_w"]).reshape(16, 128).T)
    w["w_out"] = f32(inputs["L0_attn_w_out"])
    w["w_fc"] = f32(inputs["L0_attn_w_fc"])
    w["w_proj"] = f32(inputs["L0_attn_w_proj"])
    for pp in "qkv":
        w[f"{pp}_nwT"] = f32(np.asarray(inputs[f"L0_ffn_{pp}_node_w"]).T)
        w[f"{pp}_nb"] = f32(np.asarray(inputs[f"L0_ffn_{pp}_node_b"]).reshape(15, 1))
        w[f"{pp}_w1"] = f32(np.asarray(inputs[f"L0_ffn_{pp}_w1"]).transpose(1, 0, 2).reshape(1024, 1024))
        w[f"{pp}_b1c"] = f32(np.asarray(inputs[f"L0_ffn_{pp}_b1"]).reshape(1024).reshape(8, 128).T)
        w[f"{pp}_w2"] = f32(np.asarray(inputs[f"L0_ffn_{pp}_w2"]).reshape(1024, 1024))
        w[f"{pp}_b2"] = f32(inputs[f"L0_ffn_{pp}_b2"])
    return w


def kernel(**inputs):
    x = np.asarray(inputs["L0_attn_x"], np.float32)
    w = _prep_weights(inputs)
    nc, counts, nw = build(nphases=5, dbg=False)
    in_maps = []
    for c in range(8):
        m = dict(w)
        m["xt"] = np.ascontiguousarray(x[c].T)
        in_maps.append(m)
    res = run_bass_kernel_spmd(nc, in_maps, core_ids=list(range(8)))
    out = np.stack([np.asarray(r["o"], np.float32).T for r in res.results], axis=0)
    return np.ascontiguousarray(out)
